# version 19
# baseline (speedup 1.0000x reference)
import numpy as np
import ml_dtypes
from contextlib import ExitStack

import concourse.bass as bass
import concourse.mybir as mybir
from concourse.bass_utils import run_bass_kernel_spmd

F32 = mybir.dt.float32
BF16 = mybir.dt.bfloat16
I32 = mybir.dt.int32
U32 = mybir.dt.uint32
AF = mybir.ActivationFunctionType
ALU = mybir.AluOpType
AX = mybir.AxisListType
IOA = bass.IndirectOffsetOnAxis

N_CORES = 8
NTOK = 4096
SEQ = 2048
D = 1024
H = 16
CH = 256
NCH = SEQ // CH
NTILE = NTOK // 128
E = 64
EI = 512
SCAP = 512
NSLOT = 128
NSTREAM = NSLOT * SCAP
DUMP0 = NTOK * 9
SLROWS = DUMP0 + 128


class Counter:
    def __init__(self, nc, name):
        self.h = nc.alloc_semaphore(name=name)
        self.n = 0

    def inc(self, instr, k=1):
        instr.then_inc(self.h, k)
        self.n += k
        return instr


class B:
    def __init__(self, debug=(), no_scatter=False):
        self.no_scatter = no_scatter
        self.debug = set(debug)
        nc = bass.Bass("TRN2", target_bir_lowering=False, debug=False)
        self.nc = nc
        self.ctx = ExitStack()
        self._uid = 0
        d = self.dram = {}
        d["x"] = nc.dram_tensor("x", [NTOK, D], F32, kind="ExternalInput").ap()
        for w in ["wq", "wk", "wv", "wo", "swg", "swu", "swd"]:
            d[w] = nc.dram_tensor(w, [D, D], F32, kind="ExternalInput").ap()
        d["rw"] = nc.dram_tensor("rw", [E, D], F32, kind="ExternalInput").ap()
        d["ebias"] = nc.dram_tensor("ebias", [1, E], F32, kind="ExternalInput").ap()
        d["wg"] = nc.dram_tensor("wg", [E * 128, 8 * EI], F32, kind="ExternalInput").ap()
        d["wu"] = nc.dram_tensor("wu", [E * 128, 8 * EI], F32, kind="ExternalInput").ap()
        d["wd"] = nc.dram_tensor("wd", [E * 128, 4 * D], F32, kind="ExternalInput").ap()
        d["identb"] = nc.dram_tensor("identb", [128, 128], BF16, kind="ExternalInput").ap()
        d["identf"] = nc.dram_tensor("identf", [128, 128], F32, kind="ExternalInput").ap()
        d["triu"] = nc.dram_tensor("triu", [128, 384], BF16, kind="ExternalInput").ap()
        d["cdpat"] = nc.dram_tensor("cdpat", [128, 1536], F32, kind="ExternalInput").ap()
        d["out"] = nc.dram_tensor("out", [NTOK, D], F32, kind="ExternalOutput").ap()
        d["od"] = nc.dram_tensor("od", [NTOK, D], BF16, kind="Internal").ap()
        d["y1f"] = nc.dram_tensor("y1f", [NTOK, D], F32, kind="Internal").ap()
        d["y1b"] = nc.dram_tensor("y1b", [NTOK, D], BF16, kind="Internal").ap()
        d["cdest"] = nc.dram_tensor("cdest", [NSTREAM + 1, 3], F32, kind="Internal").ap()
        d["wgc"] = nc.dram_tensor("wgc", [E * 128, 8 * EI], BF16, kind="Internal").ap()
        d["wuc"] = nc.dram_tensor("wuc", [E * 128, 8 * EI], BF16, kind="Internal").ap()
        d["wdc"] = nc.dram_tensor("wdc", [E * 128, 4 * D], BF16, kind="Internal").ap()
        d["slots"] = nc.dram_tensor("slots", [SLROWS, D], BF16, kind="Internal").ap()
        self.dbg_out = {}

        e = self.ctx.enter_context
        self.sem = {k: Counter(nc, k) for k in
                    ["sx", "sg", "sw", "pe", "ve", "ac", "po", "bar", "gq", "cq", "io",
                     "gg"]}
        sb = self.sb = {}
        sb["identb"] = e(nc.sbuf_tensor("sb_identb", [128, 128], BF16))
        sb["identf"] = e(nc.sbuf_tensor("sb_identf", [128, 128], F32))
        sb["triu"] = e(nc.sbuf_tensor("sb_triu", [128, 384], BF16))
        sb["ones1p"] = e(nc.sbuf_tensor("ones1p", [1, 128], F32))
        sb["ones11"] = e(nc.sbuf_tensor("ones11", [1, 1], F32))
        sb["onesc"] = e(nc.sbuf_tensor("onesc", [64, 1], F32))
        sb["ones64"] = e(nc.sbuf_tensor("ones64", [128, 64], F32))
        self.etab = e(nc.sbuf_tensor("etab", [1, 128], I32))
        self.etabf = e(nc.sbuf_tensor("etabf", [1, 128], F32))
        self.alive = e(nc.sbuf_tensor("alive", [1, 128], I32))
        self.fastf = e(nc.sbuf_tensor("fastf", [1, 128], I32))
        self.zi32 = e(nc.sbuf_tensor("zi32", [1, 4], I32))
        self.y1fm_ctx = ExitStack()
        sb["y1fm"] = self.y1fm_ctx.enter_context(nc.sbuf_tensor("y1fm", [128, 8 * NTOK], BF16))
        self.block = e(nc.Block())

    def u(self, base):
        self._uid += 1
        return f"{base}_{self._uid}"

    def sbuf(self, ctx, base, shape, dtype):
        return ctx.enter_context(self.nc.sbuf_tensor(self.u(base), shape, dtype))

    def psum(self, ctx, base, shape, dtype=F32):
        return ctx.enter_context(self.nc.psum_tensor(self.u(base), shape, dtype))

    def dbg(self, name, shape, dtype=F32):
        if name not in self.debug:
            return None
        if name not in self.dbg_out:
            self.dbg_out[name] = self.nc.dram_tensor(
                "dbg_" + name, shape, dtype, kind="ExternalOutput").ap()
        return self.dbg_out[name]

    def barrier(self):
        bar = self.sem["bar"]
        blk = self.block
        for sec in [blk.sync, blk.gpsimd, blk.tensor, blk.vector, blk.scalar]:
            @sec
            def _(eng):
                eng.sem_inc(bar.h, 1)
        bar.n += 5
        tgt = bar.n
        for sec in [blk.sync, blk.gpsimd, blk.tensor, blk.vector, blk.scalar]:
            @sec
            def _(eng):
                eng.wait_ge(bar.h, tgt)

    def load_consts(self):
        nc, sb, d = self.nc, self.sb, self.dram
        sx, ve = self.sem["sx"], self.sem["ve"]

        @self.block.sync
        def _(sync):
            sx.inc(sync.dma_start(out=sb["identb"][:], in_=d["identb"][:, :]), 16)
            sx.inc(sync.dma_start(out=sb["identf"][:], in_=d["identf"][:, :]), 16)
            sx.inc(sync.dma_start(out=sb["triu"][:], in_=d["triu"][:, :]), 16)
            cd2 = d["cdest"]
            sx.inc(sync.dma_start(
                out=cd2[0:NSTREAM, :].rearrange("a b -> (a b)").rearrange("(p f) -> p f", p=128),
                in_=d["cdpat"][:, :]), 16)
            sx.inc(sync.dma_start(out=cd2[NSTREAM:NSTREAM + 1, :], in_=d["cdpat"][0:1, 0:3]), 16)

        @self.block.vector
        def _(v):
            v.memset(sb["ones1p"][:], 1.0)
            v.memset(sb["ones11"][:], 1.0)
            v.memset(sb["onesc"][:], 1.0)
            ve.inc(v.memset(sb["ones64"][:], 1.0), 1)
        self.consts_ve = ve.n
        self.consts_sx = sx.n
        blk = self.block
        for sec in [blk.sync, blk.gpsimd, blk.tensor, blk.vector, blk.scalar]:
            @sec
            def _(eng):
                eng.wait_ge(sx.h, self.consts_sx)
                eng.wait_ge(ve.h, self.consts_ve)

    def attention(self):
        nc, sb, d = self.nc, self.sb, self.dram
        blk = self.block
        sg, pe, ac, ve, sw = (self.sem[k] for k in ["sg", "pe", "ac", "ve", "sw"])

        with ExitStack() as ctx:
            otok = [self.sbuf(ctx, "otok", [128, 2 * D], BF16) for _ in range(2)]
            wqb = self.sbuf(ctx, "wqb", [128, 8 * D], BF16)
            wkb = self.sbuf(ctx, "wkb", [128, 8 * D], BF16)
            wvb = self.sbuf(ctx, "wvb", [128, 8 * D], BF16)
            xstage = [self.sbuf(ctx, "xs", [128, 2 * D], F32) for _ in range(2)]
            xf = self.sbuf(ctx, "xf", [128, 8 * CH], BF16)
            qt = self.sbuf(ctx, "qt", [128, 2 * D], BF16)
            kt = self.sbuf(ctx, "kt", [128, 2 * D], BF16)
            vt = self.sbuf(ctx, "vt", [128, 2 * D], BF16)
            qf = self.sbuf(ctx, "qf", [128, 8 * CH], BF16)
            kf = self.sbuf(ctx, "kf", [128, 8 * CH], BF16)
            i1 = self.sbuf(ctx, "i1", [128, 2 * CH], BF16)
            i2 = self.sbuf(ctx, "i2", [128, 2 * CH], BF16)
            zfb = self.sbuf(ctx, "zfb", [128, CH], BF16)
            ztb = self.sbuf(ctx, "ztb", [128, 128], BF16)
            s1f = self.sbuf(ctx, "s1f", [128, H * 64], F32)
            s2f = self.sbuf(ctx, "s2f", [128, H * 64], F32)
            s1b = self.sbuf(ctx, "s1b", [128, H * 64], BF16)
            s2b = self.sbuf(ctx, "s2b", [128, H * 64], BF16)

            @blk.gpsimd
            def _(g):
                for wname, wt in [("wq", wqb), ("wk", wkb), ("wv", wvb)]:
                    for kc in range(8):
                        sw.inc(g.dma_start(
                            out=wt[:, kc * D:(kc + 1) * D],
                            in_=d[wname][kc * 128:(kc + 1) * 128, :]), 16)
            w_ready = sw.n
            wcc = self.sem["po"]

            @blk.gpsimd
            def _(g):
                for src, dst in [("wg", "wgc"), ("wu", "wuc"), ("wd", "wdc")]:
                    for e_ in range(E):
                        wcc.inc(g.dma_start(
                            out=d[dst][e_ * 128:(e_ + 1) * 128, :],
                            in_=d[src][e_ * 128:(e_ + 1) * 128, :]), 16)
            self.wcache_po = wcc.n

            xgate = {}
            cqA = self.sem["cq"]
            sxc = self.sem["sx"]
            chunks = [(s, c) for s in range(2) for c in range(NCH)]

            @blk.sync
            def _(sync):
                for ci in range(min(2, len(chunks))):
                    s, c = chunks[ci]
                    for t2 in range(2):
                        r0 = s * SEQ + c * CH + 128 * t2
                        sxc.inc(sync.dma_start(out=xstage[ci % 2][:, t2 * D:(t2 + 1) * D],
                                               in_=d["x"][r0:r0 + 128, :]), 16)
                    sync.wait_ge(sxc.h, sxc.n)
                    sync.sem_inc(cqA.h, 1)
            for ci in range(min(2, len(chunks))):
                cqA.n += 1
                xgate[ci] = cqA.n

            last = {"i1v": 0, "i2v": 0, "zac": 0, "oac": 0, "s1v": 0, "s2v": 0,
                    "mix_pe": 0}
            od_sx = {}

            for ci, (s, c) in enumerate(chunks):
                par = ci % 2
                tok0 = s * SEQ + c * CH

                if c == 0:
                    @blk.vector
                    def _(v):
                        if ci > 0:
                            v.wait_ge(pe.h, last["mix_pe"])
                        for t in [s1f, s2f, s1b, s2b]:
                            ins = v.memset(t[:], 0.0)
                        ve.inc(ins, 1)
                st_reset = ve.n

                with ExitStack() as cctx:
                    pstr = [self.psum(cctx, "pstr", [128, 128], BF16) for _ in range(2)]
                    psp = [self.psum(cctx, "psp", [128, 512], F32) for _ in range(2)]

                    tr_ac0 = ac.n
                    tr_pe0 = pe.n

                    @blk.tensor
                    def _(t):
                        t.wait_ge(cqA.h, xgate[ci])
                        if ci == 0:
                            t.wait_ge(self.sem["sx"].h, self.consts_sx)
                        for i in range(16):
                            t2, kc = divmod(i, 8)
                            if i >= 2:
                                t.wait_ge(ac.h, tr_ac0 + i - 1)
                            else:
                                t.wait_ge(ac.h, tr_ac0)
                            pe.inc(t.transpose(
                                psp[i % 2][:, 0:128],
                                xstage[par][:, t2 * D + kc * 128: t2 * D + (kc + 1) * 128],
                                sb["identf"][:]), 1)

                    @blk.scalar
                    def _(a):
                        for i in range(16):
                            t2, kc = divmod(i, 8)
                            a.wait_ge(pe.h, tr_pe0 + i + 1)
                            ac.inc(a.copy(
                                out=xf[:, kc * CH + t2 * 128: kc * CH + (t2 + 1) * 128],
                                in_=psp[i % 2][:, 0:128]), 1)
                    xf_done = ac.n

                    if ci + 2 < len(chunks):
                        s2_, c2_ = chunks[ci + 2]

                        @blk.sync
                        def _(sync):
                            sync.wait_ge(pe.h, tr_pe0 + 16)
                            for t2 in range(2):
                                r0 = s2_ * SEQ + c2_ * CH + 128 * t2
                                sxc.inc(sync.dma_start(
                                    out=xstage[par][:, t2 * D:(t2 + 1) * D],
                                    in_=d["x"][r0:r0 + 128, :]), 16)
                            sync.wait_ge(sxc.h, sxc.n)
                            sync.sem_inc(cqA.h, 1)
                        cqA.n += 1
                        xgate[ci + 2] = cqA.n

                    prj_ac0 = ac.n
                    prj_pe = []

                    @blk.tensor
                    def _(t):
                        t.wait_ge(ac.h, xf_done)
                        if ci == 0:
                            t.wait_ge(sw.h, w_ready)
                        n = 0
                        for t2 in range(2):
                            for wt in [wqb, wkb, wvb]:
                                for half in range(2):
                                    if n >= 2:
                                        t.wait_ge(ac.h, prj_ac0 + n - 1)
                                    for kc in range(8):
                                        ins = t.matmul(
                                            psp[n % 2][:],
                                            xf[:, kc * CH + t2 * 128: kc * CH + (t2 + 1) * 128],
                                            wt[:, kc * D + half * 512: kc * D + (half + 1) * 512],
                                            start=(kc == 0), stop=(kc == 7))
                                    pe.inc(ins, 1)
                                    prj_pe.append(pe.n)
                                    n += 1

                    @blk.scalar
                    def _(a):
                        n = 0
                        for t2 in range(2):
                            for dst in [qt, kt, vt]:
                                for half in range(2):
                                    a.wait_ge(pe.h, prj_pe[n])
                                    ac.inc(a.copy(
                                        out=dst[:, t2 * D + half * 512: t2 * D + (half + 1) * 512],
                                        in_=psp[n % 2][:]), 1)
                                    n += 1
                    prj_done = ac.n

                    tr2_ac0 = ac.n
                    tr2_pe0 = pe.n

                    @blk.tensor
                    def _(t):
                        t.wait_ge(ac.h, prj_done)
                        for i in range(32):
                            si, rem = divmod(i, 16)
                            t2, kc = divmod(rem, 8)
                            srct = qt if si == 0 else kt
                            if i >= 2:
                                t.wait_ge(ac.h, tr2_ac0 + i - 1)
                            pe.inc(t.transpose(
                                pstr[i % 2][:],
                                srct[:, t2 * D + kc * 128: t2 * D + (kc + 1) * 128],
                                sb["identb"][:]), 1)

                    @blk.scalar
                    def _(a):
                        for i in range(32):
                            si, rem = divmod(i, 16)
                            t2, kc = divmod(rem, 8)
                            dstt = qf if si == 0 else kf
                            a.wait_ge(pe.h, tr2_pe0 + i + 1)
                            ac.inc(a.copy(
                                out=dstt[:, kc * CH + t2 * 128: kc * CH + (t2 + 1) * 128],
                                in_=pstr[i % 2][:]), 1)
                    qkf_done = ac.n

                with ExitStack() as hctx:
                    psAB = self.psum(hctx, "psAB", [128, 512], F32)
                    psZO = self.psum(hctx, "psZO", [128, 256], F32)
                    psO2 = self.psum(hctx, "psO2", [128, 128], F32)
                    psZt = self.psum(hctx, "psZt", [128, 128], F32)
                    psS = self.psum(hctx, "psS", [128, 128], F32)

                    for h in range(H):
                        po2 = (h % 2) * 64
                        kc = h // 2
                        qfh = qf[po2:po2 + 64, kc * CH:(kc + 1) * CH]
                        kfh = kf[po2:po2 + 64, kc * CH:(kc + 1) * CH]
                        s1bh = s1b[po2:po2 + 64, h * 64:(h + 1) * 64]
                        s2bh = s2b[po2:po2 + 64, h * 64:(h + 1) * 64]
                        s1fh = s1f[po2:po2 + 64, h * 64:(h + 1) * 64]
                        s2fh = s2f[po2:po2 + 64, h * 64:(h + 1) * 64]

                        def kth(t2):
                            return kt[:, t2 * D + h * 64: t2 * D + h * 64 + 64]

                        def vth(t2):
                            return vt[:, t2 * D + h * 64: t2 * D + h * 64 + 64]

                        @blk.tensor
                        def _(t):
                            t.wait_ge(ac.h, qkf_done)
                            t.wait_ge(ve.h, max(st_reset, last["i2v"]))
                            for st in range(2):
                                pe.inc(t.matmul(psAB[:, st * CH:(st + 1) * CH],
                                                kfh[:, st * 128:(st + 1) * 128],
                                                qfh[:], start=True, stop=True), 1)
                        at_pe = pe.n

                        @blk.vector
                        def _(v):
                            v.wait_ge(pe.h, at_pe)
                            for st in range(2):
                                mask = sb["triu"][:, 128 - st * 128: 256 - st * 128 + 128]
                                ve.inc(v.tensor_tensor(
                                    out=i1[:, st * CH:(st + 1) * CH],
                                    in0=psAB[:, st * CH:(st + 1) * CH],
                                    in1=mask, op=ALU.mult), 1)
                        last["i1v"] = ve.n

                        @blk.tensor
                        def _(t):
                            t.wait_ge(ve.h, last["i1v"])
                            t.wait_ge(ac.h, max(last["zac"], last["oac"]))
                            t.wait_ge(ve.h, last["s1v"])
                            for st in range(2):
                                t.matmul(psZO[po2:po2 + 64, 0:CH], kth(st),
                                         i1[:, st * CH:(st + 1) * CH],
                                         start=(st == 0), stop=False)
                            pe.inc(t.matmul(psZO[po2:po2 + 64, 0:CH], s1bh, qfh[:],
                                            start=False, stop=True), 1)
                            for rt in range(2):
                                for st in range(2):
                                    t.matmul(psZt[:, rt * 64:(rt + 1) * 64],
                                             i1[:, st * CH + rt * 128: st * CH + (rt + 1) * 128],
                                             kth(st), start=(st == 0), stop=False)
                                pe.inc(t.matmul(psZt[:, rt * 64:(rt + 1) * 64],
                                                qfh[:, rt * 128:(rt + 1) * 128], s1bh,
                                                start=False, stop=True), 1)
                            for st in range(2):
                                ins = t.matmul(psS[po2:po2 + 64, 0:64], kth(st), kth(st),
                                               start=(st == 0), stop=(st == 1))
                            pe.inc(ins, 1)
                        z_pe = pe.n

                        @blk.scalar
                        def _(a):
                            a.wait_ge(pe.h, z_pe)
                            a.copy(out=zfb[po2:po2 + 64, :], in_=psZO[po2:po2 + 64, 0:CH])
                            ac.inc(a.copy(out=ztb[:], in_=psZt[:]), 1)
                        last["zac"] = ac.n

                        @blk.vector
                        def _(v):
                            v.wait_ge(pe.h, z_pe)
                            ve.inc(v.tensor_tensor(out=s1fh, in0=s1fh,
                                                   in1=psS[po2:po2 + 64, 0:64], op=ALU.add), 1)
                            v.wait_ge(ve.h, ve.n)
                            ve.inc(v.tensor_copy(out=s1bh, in_=s1fh), 1)
                        last["s1v"] = ve.n

                        @blk.tensor
                        def _(t):
                            t.wait_ge(ac.h, last["zac"])
                            t.wait_ge(ve.h, last["i1v"])
                            for rt in range(2):
                                pe.inc(t.matmul(psAB[:, rt * CH:(rt + 1) * CH],
                                                zfb[po2:po2 + 64, rt * 128:(rt + 1) * 128],
                                                qfh[:], start=True, stop=True), 1)
                        bt_pe = pe.n

                        @blk.vector
                        def _(v):
                            v.wait_ge(pe.h, bt_pe)
                            for rt in range(2):
                                mask = sb["triu"][:, 128 - rt * 128: 256 - rt * 128 + 128]
                                ve.inc(v.tensor_tensor(
                                    out=i2[:, rt * CH:(rt + 1) * CH],
                                    in0=psAB[:, rt * CH:(rt + 1) * CH],
                                    in1=mask, op=ALU.mult), 1)
                        last["i2v"] = ve.n

                        @blk.tensor
                        def _(t):
                            t.wait_ge(ve.h, last["i2v"])
                            t.wait_ge(ac.h, last["oac"])
                            t.wait_ge(ve.h, last["s2v"])
                            for ih in range(2):
                                for rt in range(2):
                                    t.matmul(psO2[:, ih * 64:(ih + 1) * 64],
                                             i2[:, rt * CH + ih * 128: rt * CH + (ih + 1) * 128],
                                             vth(rt), start=(rt == 0), stop=False)
                                pe.inc(t.matmul(psO2[:, ih * 64:(ih + 1) * 64],
                                                qfh[:, ih * 128:(ih + 1) * 128], s2bh,
                                                start=False, stop=True), 1)
                            for rt in range(2):
                                ins = t.matmul(psS[po2:po2 + 64, 64:128], ztb[:, rt * 64:(rt + 1) * 64],
                                               vth(rt), start=(rt == 0), stop=(rt == 1))
                            pe.inc(ins, 1)
                        o_pe = pe.n

                        @blk.scalar
                        def _(a):
                            a.wait_ge(pe.h, o_pe)
                            if h == 0:
                                a.wait_ge(self.sem["sx"].h, od_sx.get(ci - 2, 0))
                            for ih in range(2):
                                ins = a.copy(
                                    out=otok[par][:, ih * D + h * 64: ih * D + h * 64 + 64],
                                    in_=psO2[:, ih * 64:(ih + 1) * 64])
                            ac.inc(ins, 1)
                        last["oac"] = ac.n

                        @blk.vector
                        def _(v):
                            v.wait_ge(pe.h, o_pe)
                            ve.inc(v.tensor_tensor(out=s2fh, in0=s2fh,
                                                   in1=psS[po2:po2 + 64, 64:128], op=ALU.add), 1)
                            v.wait_ge(ve.h, ve.n)
                            ve.inc(v.tensor_copy(out=s2bh, in_=s2fh), 1)
                        last["s2v"] = ve.n

                    last["mix_pe"] = pe.n

                @blk.sync
                def _(sync):
                    sync.wait_ge(ac.h, last["oac"])
                    for ih in range(2):
                        self.sem["sx"].inc(sync.dma_start(
                            out=d["od"][tok0 + ih * 128: tok0 + (ih + 1) * 128, :],
                            in_=otok[par][:, ih * D:(ih + 1) * D]), 16)
                od_sx[ci] = self.sem["sx"].n

            self.barrier()

    def wo_ln1(self):
        nc, sb, d = self.nc, self.sb, self.dram
        blk = self.block
        sx, sg, pe, ac, ve, sw = (self.sem[k] for k in ["sx", "sg", "pe", "ac", "ve", "sw"])

        dbg_y1 = self.dbg("y1", [NTOK, D], F32)

        with ExitStack() as ctx:
            wob = self.sbuf(ctx, "wob", [128, 8 * D], BF16)
            odrow = [self.sbuf(ctx, "odr", [128, D], BF16) for _ in range(2)]
            ofeat = [self.sbuf(ctx, "ofe", [128, D], BF16) for _ in range(2)]
            xrow = [self.sbuf(ctx, "xr", [128, D], F32) for _ in range(2)]
            sqs = [self.sbuf(ctx, "sqs", [128, D], F32) for _ in range(2)]
            ytile = [self.sbuf(ctx, "yt", [128, D], F32) for _ in range(2)]
            ybtile = [self.sbuf(ctx, "yb", [128, D], BF16) for _ in range(2)]
            stat = self.sbuf(ctx, "stat", [128, 8], F32)
            psw = [self.psum(ctx, "psw", [128, 512], F32) for _ in range(4)]
            pstr = [self.psum(ctx, "pstr", [128, 128], BF16) for _ in range(2)]

            @blk.gpsimd
            def _(g):
                for kc in range(8):
                    sw.inc(g.dma_start(out=wob[:, kc * D:(kc + 1) * D],
                                       in_=d["wo"][kc * 128:(kc + 1) * 128, :]), 16)
            wo_ready = sw.n

            x_done = {}
            od_done = {}

            @blk.sync
            def _(sync):
                for i in range(2):
                    sx.inc(sync.dma_start(out=xrow[i][:],
                                          in_=d["x"][i * 128:(i + 1) * 128, :]), 16)
                    x_done[i] = sx.n
                    sx.inc(sync.dma_start(out=odrow[i][:],
                                          in_=d["od"][i * 128:(i + 1) * 128, :]), 16)
                    od_done[i] = sx.n

            add_ve = {-2: 0, -1: 0}
            reuse_ve = {-2: 0, -1: 0}

            ofeat_ac = {}
            tr_done = {}
            for i in range(NTILE):
                par = i % 2

                otr_pe0 = pe.n
                otr_ac0 = ac.n

                @blk.tensor
                def _(t):
                    t.wait_ge(sx.h, od_done[i])
                    if i >= 1:
                        t.wait_ge(ac.h, tr_done[i - 1])
                    for kci in range(8):
                        if kci >= 2:
                            t.wait_ge(ac.h, otr_ac0 + kci - 1)
                        pe.inc(t.transpose(pstr[kci % 2][:],
                                           odrow[par][:, kci * 128:(kci + 1) * 128],
                                           sb["identb"][:]), 1)
                otr_pe = pe.n

                @blk.scalar
                def _(a):
                    if i >= 2:
                        a.wait_ge(pe.h, omm_pe[i - 2])
                    for kci in range(8):
                        a.wait_ge(pe.h, otr_pe0 + kci + 1)
                        ac.inc(a.copy(out=ofeat[par][:, kci * 128:(kci + 1) * 128],
                                      in_=pstr[kci % 2][:]), 1)
                ofeat_ac[i] = ac.n

                @blk.tensor
                def _(t):
                    if i == 0:
                        t.wait_ge(sw.h, wo_ready)
                        omm_pe = {}
                    t.wait_ge(ac.h, ofeat_ac[i])
                    t.wait_ge(ve.h, add_ve[i - 2])
                    for half in range(2):
                        for kci in range(8):
                            ins = t.matmul(
                                psw[par * 2 + half][:],
                                ofeat[par][:, kci * 128:(kci + 1) * 128],
                                wob[:, kci * D + half * 512: kci * D + (half + 1) * 512],
                                start=(kci == 0), stop=(kci == 7))
                        pe.inc(ins, 1)
                omm_pe = omm_pe if i > 0 else {}
                omm_pe[i] = pe.n
                mm_pe = pe.n

                @blk.vector
                def _(v):
                    v.wait_ge(pe.h, mm_pe)
                    v.wait_ge(self.sem["sx"].h, x_done[i])
                    v.wait_ge(ve.h, reuse_ve[i - 2])
                    for half in range(2):
                        ins = v.tensor_tensor(
                            out=ytile[par][:, half * 512:(half + 1) * 512],
                            in0=psw[par * 2 + half][:],
                            in1=xrow[par][:, half * 512:(half + 1) * 512],
                            op=ALU.add)
                    ve.inc(ins, 1)
                add_ve[i] = ve.n

                @blk.vector
                def _(v):
                    v.wait_ge(ve.h, add_ve[i])
                    ve.inc(v.tensor_reduce(stat[:, 0:1], ytile[par][:], AX.X, ALU.add), 1)
                    v.wait_ge(ve.h, ve.n)
                    ve.inc(v.tensor_scalar(stat[:, 1:2], stat[:, 0:1], -1.0 / D, None, ALU.mult), 1)
                    v.wait_ge(ve.h, ve.n)
                    ve.inc(v.tensor_scalar(ytile[par][:], ytile[par][:], stat[:, 1:2], None, ALU.add), 1)
                    v.wait_ge(ve.h, ve.n)
                    ve.inc(v.tensor_tensor(out=sqs[par][:], in0=ytile[par][:],
                                           in1=ytile[par][:], op=ALU.mult), 1)
                    v.wait_ge(ve.h, ve.n)
                    ve.inc(v.tensor_reduce(stat[:, 3:4], sqs[par][:], AX.X, ALU.add), 1)
                    v.wait_ge(ve.h, ve.n)
                    ve.inc(v.tensor_scalar(stat[:, 6:7], stat[:, 3:4], 1.0 / D, 1e-5,
                                           ALU.mult, ALU.add), 1)
                sq_ve = ve.n

                @blk.scalar
                def _(a):
                    a.wait_ge(ve.h, sq_ve)
                    ac.inc(a.sqrt(out=stat[:, 4:5], in_=stat[:, 6:7]), 1)
                rstd_ac = ac.n

                @blk.vector
                def _(v):
                    v.wait_ge(ac.h, rstd_ac)
                    ve.inc(v.reciprocal(stat[:, 5:6], stat[:, 4:5]), 1)
                    v.wait_ge(ve.h, ve.n)
                    ve.inc(v.tensor_scalar(ytile[par][:], ytile[par][:], stat[:, 5:6],
                                           None, ALU.mult), 1)
                    v.wait_ge(ve.h, ve.n)
                    ve.inc(v.tensor_copy(out=ybtile[par][:], in_=ytile[par][:]), 1)
                ln_ve = ve.n

                @blk.sync
                def _(sync):
                    sync.wait_ge(ve.h, ln_ve)
                    sx.inc(sync.dma_start(out=d["y1f"][i * 128:(i + 1) * 128, :],
                                          in_=ytile[par][:]), 16)
                    sx.inc(sync.dma_start(out=d["y1b"][i * 128:(i + 1) * 128, :],
                                          in_=ybtile[par][:]), 16)
                    if dbg_y1 is not None:
                        sx.inc(sync.dma_start(out=dbg_y1[i * 128:(i + 1) * 128, :],
                                              in_=ytile[par][:]), 16)
                    if i + 2 < NTILE:
                        sx.inc(sync.dma_start(out=xrow[par][:],
                                              in_=d["x"][(i + 2) * 128:(i + 3) * 128, :]), 16)
                        x_done[i + 2] = sx.n
                        sync.wait_ge(pe.h, otr_pe)
                        sx.inc(sync.dma_start(out=odrow[par][:],
                                              in_=d["od"][(i + 2) * 128:(i + 3) * 128, :]), 16)
                        od_done[i + 2] = sx.n
                st_sx = sx.n

                trc_ac0 = ac.n
                trp0 = pe.n

                @blk.tensor
                def _(t):
                    t.wait_ge(ve.h, ln_ve)
                    for kci in range(8):
                        if kci >= 2:
                            t.wait_ge(ac.h, trc_ac0 + kci - 1)
                        pe.inc(t.transpose(pstr[kci % 2][:],
                                           ybtile[par][:, kci * 128:(kci + 1) * 128],
                                           sb["identb"][:]), 1)

                @blk.scalar
                def _(a):
                    for kci in range(8):
                        a.wait_ge(pe.h, trp0 + kci + 1)
                        ac.inc(a.copy(
                            out=sb["y1fm"][:, kci * NTOK + i * 128: kci * NTOK + (i + 1) * 128],
                            in_=pstr[kci % 2][:]), 1)
                tr_ac = ac.n
                tr_done[i] = tr_ac

                @blk.vector
                def _(v):
                    v.wait_ge(self.sem["sx"].h, st_sx)
                    v.wait_ge(ac.h, tr_ac)
                    ve.inc(v.engine_nop(), 1)
                reuse_ve[i] = ve.n

            self.y1_sx = self.sem["sx"].n
            self.barrier()

    def vsync(self, v, ve):
        v.wait_ge(ve.h, ve.n)

    def routing(self):
        nc, sb, d = self.nc, self.sb, self.dram
        blk = self.block
        sx, sg, pe, ac, ve, sw, po = (self.sem[k] for k in
                                      ["sx", "sg", "pe", "ac", "ve", "sw", "po"])
        y1fm = sb["y1fm"]

        self.shared_ctx = ExitStack()
        self.sw_sb = {
            "swgb": self.sbuf(self.shared_ctx, "swgb", [128, 8 * D], BF16),
            "swub": self.sbuf(self.shared_ctx, "swub", [128, 8 * D], BF16),
            "swdb": self.sbuf(self.shared_ctx, "swdb", [128, 8 * D], BF16),
            "gsh": self.sbuf(self.shared_ctx, "gsh", [128, 4096], BF16),
            "hsh": self.sbuf(self.shared_ctx, "hsh", [128, 4096], BF16),
            "osh0": self.sbuf(self.shared_ctx, "osh", [128, D], BF16),
            "osh1": self.sbuf(self.shared_ctx, "osh", [128, D], BF16),
        }
        self.routing_ctx = ExitStack()
        ctx = self.routing_ctx
        ctok_all = self.sbuf(ctx, "ctok_all", [128, NTILE * 64], F32)

        wrt = self.sbuf(ctx, "wrt", [128, 512], BF16)
        ebc = self.sbuf(ctx, "ebc", [128, 64], F32)
        mT = self.sbuf(ctx, "mT", [64, NTOK], BF16)
        cX = self.sbuf(ctx, "cX", [64, NTOK], F32)
        msel_all = self.sbuf(ctx, "msel_all", [128, NTILE * 64], F32)
        coef_all = self.sbuf(ctx, "coef_all", [128, NTILE * 64], F32)
        sco = self.sbuf(ctx, "sco", [128, 64], F32)
        sfc = self.sbuf(ctx, "sfc", [128, 64], F32)
        g4 = self.sbuf(ctx, "g4", [128, 32], F32)
        g2 = self.sbuf(ctx, "g2", [128, 16], F32)
        m1 = self.sbuf(ctx, "m1", [128, 8], F32)
        m2 = self.sbuf(ctx, "m2", [128, 8], F32)
        w2 = self.sbuf(ctx, "w2", [128, 64], F32)
        gsum = self.sbuf(ctx, "gsum", [128, 8], F32)
        srt = self.sbuf(ctx, "srt", [128, 8], F32)
        rep8 = self.sbuf(ctx, "rep8", [128, 8], F32)
        g2r = self.sbuf(ctx, "g2r", [128, 8], F32)
        gsel = self.sbuf(ctx, "gsel", [128, 8], F32)
        msk = self.sbuf(ctx, "msk", [128, 64], F32)
        m8 = self.sbuf(ctx, "m8", [128, 8], F32)
        mrm = self.sbuf(ctx, "mrm", [128, 64], F32)
        wsel = self.sbuf(ctx, "wsel", [128, 64], F32)
        stat = self.sbuf(ctx, "rstat", [128, 8], F32)
        iota9 = self.sbuf(ctx, "iota9", [128, 1], F32)
        iota18 = self.sbuf(ctx, "iota18", [128, 8], F32)
        jbig = self.sbuf(ctx, "jbig", [128, 8], F32)
        js16 = self.sbuf(ctx, "js16", [128, 8], F32)
        iotat = self.sbuf(ctx, "iotat", [64, 128], F32)
        bk = self.sbuf(ctx, "bk", [128, 512], F32)
        bbc = self.sbuf(ctx, "bbc", [128, 64], F32)
        sfull = [self.sbuf(ctx, "sfull", [128, 64], F32) for _ in range(2)]
        incl = self.sbuf(ctx, "incl", [128, 64], F32)
        destf = self.sbuf(ctx, "destf", [128, 64], F32)
        jm = self.sbuf(ctx, "jm", [128, 64], F32)
        jm4 = self.sbuf(ctx, "jm4", [128, 64], F32)
        keyb = self.sbuf(ctx, "keyb", [128, 192], F32)
        k8 = self.sbuf(ctx, "k8", [128, 24], F32)
        stage = [self.sbuf(ctx, "stage", [128, 24], F32) for _ in range(2)]
        idxu = [self.sbuf(ctx, "idxu", [128, 8], U32) for _ in range(2)]

        dbg_coef = self.dbg("coef", [128, NTILE * 64], F32)
        dbg_msel = self.dbg("msel", [128, NTILE * 64], F32)
        dbg_cdest = self.dbg("cdest", [NSTREAM + 1, 3], F32)
        dbg_etab = self.dbg("etab", [1, 128], I32)
        dbg_mt = self.dbg("mT", [64, NTOK], F32)

        with ExitStack() as pctx:
            rws = self.sbuf(pctx, "rws", [64, D], F32)
            rwb = self.sbuf(pctx, "rwb", [64, D], BF16)
            ebr = self.sbuf(pctx, "ebr", [1, 64], F32)
            psR = [self.psum(pctx, "psR", [128, 64], F32) for _ in range(2)]
            pstr = [self.psum(pctx, "pstrR", [128, 128], F32) for _ in range(2)]
            psmix = self.psum(pctx, "psmix", [128, 128], F32)
            pstrb = [self.psum(pctx, "pstrb", [128, 64], BF16) for _ in range(2)]

            @blk.sync
            def _(sync):
                sx.inc(sync.dma_start(out=rws[:], in_=d["rw"][:, :]), 16)
                sx.inc(sync.dma_start(out=ebr[:], in_=d["ebias"][:, :]), 16)
            rw_sx = sx.n

            @blk.gpsimd
            def _(g):
                for wname, wt in [("swg", self.sw_sb["swgb"]), ("swu", self.sw_sb["swub"]),
                                  ("swd", self.sw_sb["swdb"])]:
                    for kc in range(8):
                        sw.inc(g.dma_start(
                            out=wt[:, kc * D:(kc + 1) * D],
                            in_=d[wname][kc * 128:(kc + 1) * 128, :]), 16)
                g.iota(iota9[:], pattern=[[0, 1]], base=0, channel_multiplier=9,
                       allow_small_or_imprecise_dtypes=True)
                g.iota(iota18[:], pattern=[[0, 8]], base=0, channel_multiplier=1,
                       allow_small_or_imprecise_dtypes=True)
                g.iota(jbig[:], pattern=[[-1, 8]], base=8,
                       channel_multiplier=0, allow_small_or_imprecise_dtypes=True)
                g.iota(js16[:], pattern=[[-16, 8]], base=128, channel_multiplier=0,
                       allow_small_or_imprecise_dtypes=True)
                po.inc(g.iota(iotat[:], pattern=[[1, 128]], base=0,
                              channel_multiplier=0,
                              allow_small_or_imprecise_dtypes=True), 1)
            iota_po = po.n
            self.sw_ready = sw.n

            @blk.vector
            def _(v):
                v.wait_ge(sx.h, rw_sx)
                v.wait_ge(po.h, iota_po)
                v.memset(rep8[:, 4:8], -1e30)
                ve.inc(v.tensor_scalar(jbig[:], jbig[:], 1048576.0, None, ALU.mult), 1)
                self.vsync(v, ve)
                ve.inc(v.tensor_copy(out=rwb[:], in_=rws[:]), 1)
            rwb_ve = ve.n

            rtr_ac0 = ac.n
            rtr_pe0 = pe.n

            @blk.tensor
            def _(t):
                t.wait_ge(ve.h, rwb_ve)
                for kc in range(8):
                    if kc >= 2:
                        t.wait_ge(ac.h, rtr_ac0 + kc - 1)
                    pe.inc(t.transpose(pstrb[kc % 2][:],
                                       rwb[:, kc * 128:(kc + 1) * 128],
                                       sb["identb"][0:64, 0:64]), 1)
                pe.inc(t.matmul(psmix[:, 0:64], sb["ones1p"][:], ebr[:],
                                start=True, stop=True), 1)

            @blk.scalar
            def _(a):
                for kc in range(8):
                    a.wait_ge(pe.h, rtr_pe0 + kc + 1)
                    ac.inc(a.copy(out=wrt[:, kc * 64:(kc + 1) * 64],
                                  in_=pstrb[kc % 2][:]), 1)
                a.wait_ge(pe.h, rtr_pe0 + 9)
                ac.inc(a.copy(out=ebc[:], in_=psmix[:, 0:64]), 1)
            setup_ac = ac.n

            sig_ac = {}
            scoused_ve = {}
            mT_ve = {}
            for i in range(NTILE):
                @blk.tensor
                def _(t):
                    t.wait_ge(ac.h, setup_ac)
                    if i >= 2:
                        t.wait_ge(ac.h, sig_ac[i - 2])
                    for kc in range(8):
                        ins = t.matmul(psR[i % 2][:],
                                       y1fm[:, kc * NTOK + i * 128: kc * NTOK + (i + 1) * 128],
                                       wrt[:, kc * 64:(kc + 1) * 64],
                                       start=(kc == 0), stop=(kc == 7))
                    pe.inc(ins, 1)
                r_pe = pe.n

                @blk.scalar
                def _(a):
                    a.wait_ge(pe.h, r_pe)
                    if i > 0:
                        a.wait_ge(ve.h, scoused_ve[i - 1])
                    ac.inc(a.activation(out=sco[:], in_=psR[i % 2][:], func=AF.Sigmoid), 1)
                sig_ac[i] = ac.n

                @blk.vector
                def _(v):
                    v.wait_ge(ac.h, sig_ac[i])
                    ve.inc(v.tensor_tensor(out=sfc[:], in0=sco[:], in1=ebc[:], op=ALU.add), 1)
                    self.vsync(v, ve)
                    v3 = sfc[:].rearrange("p (g e) -> p g e", g=8)
                    g43 = g4[:].rearrange("p (g e) -> p g e", g=8)
                    g23 = g2[:].rearrange("p (g e) -> p g e", g=8)
                    ve.inc(v.tensor_max(g43, v3[:, :, 0:4], v3[:, :, 4:8]), 1)
                    self.vsync(v, ve)
                    ve.inc(v.tensor_max(g23, g43[:, :, 0:2], g43[:, :, 2:4]), 1)
                    self.vsync(v, ve)
                    ve.inc(v.tensor_max(m1[:].rearrange("p (g e) -> p g e", g=8),
                                        g23[:, :, 0:1], g23[:, :, 1:2]), 1)
                    self.vsync(v, ve)
                    ve.inc(v.match_replace(out=w2[:], in_to_replace=m1[:],
                                           in_values=sfc[:], imm_value=-1e30), 1)
                    self.vsync(v, ve)
                    w23 = w2[:].rearrange("p (g e) -> p g e", g=8)
                    ve.inc(v.tensor_max(g43, w23[:, :, 0:4], w23[:, :, 4:8]), 1)
                    self.vsync(v, ve)
                    ve.inc(v.tensor_max(g23, g43[:, :, 0:2], g43[:, :, 2:4]), 1)
                    self.vsync(v, ve)
                    ve.inc(v.tensor_max(m2[:].rearrange("p (g e) -> p g e", g=8),
                                        g23[:, :, 0:1], g23[:, :, 1:2]), 1)
                    self.vsync(v, ve)
                    ve.inc(v.tensor_tensor(out=gsum[:], in0=m1[:], in1=m2[:], op=ALU.add), 1)
                    self.vsync(v, ve)
                    ve.inc(v.max(srt[:], gsum[:]), 1)
                    self.vsync(v, ve)
                    ve.inc(v.tensor_copy(out=rep8[:, 0:4], in_=srt[:, 0:4]), 1)
                    self.vsync(v, ve)
                    ve.inc(v.match_replace(out=g2r[:], in_to_replace=rep8[:],
                                           in_values=gsum[:], imm_value=-1e30), 1)
                    self.vsync(v, ve)
                    ve.inc(v.tensor_scalar(gsel[:], g2r[:], -1e29, None, ALU.is_lt), 1)
                    self.vsync(v, ve)
                    ve.inc(v.tensor_tensor(
                        out=msk[:].rearrange("p (g e) -> p g e", g=8),
                        in0=v3, in1=gsel[:, :, None].to_broadcast([128, 8, 8]),
                        op=ALU.mult), 1)
                    self.vsync(v, ve)
                    ve.inc(v.max(m8[:], msk[:]), 1)
                    self.vsync(v, ve)
                    ve.inc(v.match_replace(out=mrm[:], in_to_replace=m8[:],
                                           in_values=msk[:], imm_value=-1e30), 1)
                    self.vsync(v, ve)
                    ve.inc(v.tensor_scalar(msel_all[:, i * 64:(i + 1) * 64], mrm[:],
                                           -1e29, None, ALU.is_lt), 1)
                    self.vsync(v, ve)
                    ve.inc(v.tensor_tensor(out=wsel[:], in0=sco[:],
                                           in1=msel_all[:, i * 64:(i + 1) * 64],
                                           op=ALU.mult), 1)
                    scoused_ve[i] = ve.n
                    self.vsync(v, ve)
                    ve.inc(v.tensor_reduce(stat[:, 0:1], wsel[:], AX.X, ALU.add), 1)
                    self.vsync(v, ve)
                    ve.inc(v.tensor_scalar(stat[:, 1:2], stat[:, 0:1], 1e-20, None, ALU.add), 1)
                    self.vsync(v, ve)
                    ve.inc(v.reciprocal(stat[:, 2:3], stat[:, 1:2]), 1)
                    self.vsync(v, ve)
                    ve.inc(v.tensor_scalar(coef_all[:, i * 64:(i + 1) * 64], wsel[:],
                                           stat[:, 2:3], 2.5, ALU.mult, ALU.mult), 1)
                msel_ve = ve.n

                @blk.tensor
                def _(t):
                    t.wait_ge(ve.h, msel_ve)
                    if i >= 2:
                        t.wait_ge(ve.h, mT_ve[i - 2])
                    pe.inc(t.transpose(pstr[i % 2][0:64, :],
                                       msel_all[:, i * 64:(i + 1) * 64],
                                       sb["identf"][:]), 1)
                mseltp = pe.n

                @blk.vector
                def _(v):
                    v.wait_ge(pe.h, mseltp)
                    ve.inc(v.tensor_copy(out=mT[:, i * 128:(i + 1) * 128],
                                         in_=pstr[i % 2][0:64, :]), 1)
                mT_ve[i] = ve.n

            ones4k = self.sbuf(pctx, "ones4k", [64, NTOK], BF16)
            @blk.vector
            def _(v):
                ve.inc(v.memset(ones4k[:], 1.0), 1)
                self.vsync(v, ve)
                ve.inc(v.tensor_tensor_scan(cX[:], ones4k[:], mT[:], 0.0,
                                            ALU.mult, ALU.add), 1)
                self.vsync(v, ve)
                ve.inc(v.tensor_copy(out=bk[0:64, 0:1], in_=cX[0:64, NTOK - 1:NTOK]), 1)
                self.vsync(v, ve)
                ve.inc(v.tensor_tensor(out=cX[:], in0=cX[:], in1=mT[:], op=ALU.subtract), 1)
                self.vsync(v, ve)
                ve.inc(v.tensor_scalar(bk[0:64, 1:2], bk[0:64, 0:1], 511.0, None, ALU.add), 1)
                self.vsync(v, ve)
                ve.inc(v.tensor_scalar(bk[0:64, 2:3], bk[0:64, 1:2],
                                       1.0 / 512.0, -0.4990234375, ALU.mult, ALU.add), 1)
                self.vsync(v, ve)
                ve.inc(v.tensor_scalar(bk[0:64, 3:4], bk[0:64, 2:3], 8388608.0, None, ALU.add), 1)
                self.vsync(v, ve)
                ve.inc(v.tensor_scalar(bk[0:64, 4:5], bk[0:64, 3:4], -8388608.0, None, ALU.add), 1)
            tiles_ve = ve.n

            ctt_pe0 = pe.n
            ctt_ve0 = ve.n

            @blk.tensor
            def _(t):
                t.wait_ge(ve.h, tiles_ve)
                for i in range(NTILE):
                    if i >= 2:
                        t.wait_ge(ve.h, ctt_ve0 + i - 1)
                    pe.inc(t.transpose(pstr[i % 2][:, 0:64], cX[:, i * 128:(i + 1) * 128],
                                       sb["identf"][0:64, 0:64]), 1)

            @blk.vector
            def _(v):
                for i in range(NTILE):
                    v.wait_ge(pe.h, ctt_pe0 + i + 1)
                    ve.inc(v.tensor_copy(out=ctok_all[:, i * 64:(i + 1) * 64],
                                         in_=pstr[i % 2][:, 0:64]), 1)
            ctok_ve = ve.n

            @blk.tensor
            def _(t):
                t.wait_ge(ve.h, tiles_ve)
                pe.inc(t.matmul(psmix[0:1, 0:64], bk[0:64, 4:5],
                                sb["identf"][0:64, 0:64], start=True, stop=True), 1)
            trow_pe = pe.n

            @blk.vector
            def _(v):
                v.wait_ge(pe.h, trow_pe)
                ve.inc(v.tensor_copy(out=bk[0:1, 16:80], in_=psmix[0:1, 0:64]), 1)
                self.vsync(v, ve)
                ve.inc(v.tensor_tensor_scan(bk[0:1, 80:144], sb["ones64"][0:1, 0:64],
                                            bk[0:1, 16:80], 0.0, ALU.mult, ALU.add), 1)
                self.vsync(v, ve)
                ve.inc(v.tensor_tensor(out=bk[0:1, 144:208], in0=bk[0:1, 80:144],
                                       in1=bk[0:1, 16:80], op=ALU.subtract), 1)
                self.vsync(v, ve)
                ve.inc(v.tensor_scalar(bk[0:1, 208:272], bk[0:1, 144:208],
                                       512.0, None, ALU.mult), 1)
            brow_ve = ve.n

            @blk.tensor
            def _(t):
                t.wait_ge(ve.h, brow_ve)
                pe.inc(t.matmul(psmix[0:64, 64:65], bk[0:1, 144:208], sb["ones11"][:],
                                start=True, stop=True), 1)
            scol_pe = pe.n

            @blk.vector
            def _(v):
                v.wait_ge(pe.h, scol_pe)
                v.wait_ge(po.h, iota_po)
                ve.inc(v.tensor_copy(out=bk[0:64, 5:6], in_=psmix[0:64, 64:65]), 1)
                self.vsync(v, ve)
                ve.inc(v.tensor_scalar(bk[0:64, 384:512], iotat[:], bk[0:64, 5:6],
                                       None, ALU.is_ge), 1)
            A_ve = ve.n

            @blk.tensor
            def _(t):
                t.wait_ge(ve.h, A_ve)
                pe.inc(t.matmul(pstr[0][:, 0:1], bk[0:64, 384:512], sb["onesc"][:],
                                start=True, stop=True), 1)
            ecol_pe = pe.n

            @blk.vector
            def _(v):
                v.wait_ge(pe.h, ecol_pe)
                ve.inc(v.tensor_copy(out=bk[:, 6:7], in_=pstr[0][:, 0:1]), 1)
            ecol_ve = ve.n

            @blk.tensor
            def _(t):
                t.wait_ge(ve.h, ecol_ve)
                pe.inc(t.matmul(psmix[0:1, 0:128], bk[:, 6:7], sb["identf"][:],
                                start=True, stop=True), 1)
            erow_pe = pe.n

            @blk.vector
            def _(v):
                v.wait_ge(pe.h, erow_pe)
                ve.inc(v.tensor_scalar(bk[0:1, 272:400], psmix[0:1, 0:128],
                                       -1.0, None, ALU.add), 1)
                self.vsync(v, ve)
                ve.inc(v.tensor_copy(out=self.etab[:], in_=bk[0:1, 272:400]), 1)
                ve.inc(v.tensor_copy(out=self.etabf[:], in_=bk[0:1, 272:400]), 1)
                self.vsync(v, ve)
                ve.inc(v.tensor_scalar(bk[0:1, 384:512], iotat[0:1, 0:128],
                                       bk[0:1, 143:144], None, ALU.is_lt), 1)
                ve.inc(v.memset(self.zi32[:], 0.0), 1)
                self.vsync(v, ve)
                ve.inc(v.tensor_copy(out=self.alive[:], in_=bk[0:1, 384:512]), 1)
            etab_ve = ve.n
            self.alive_ve = etab_ve

            @blk.tensor
            def _(t):
                t.wait_ge(ve.h, etab_ve)
                pe.inc(t.matmul(psmix[:, 0:64], sb["ones1p"][:],
                                bk[0:1, 208:272], start=True, stop=True), 1)
            bbc_pe = pe.n

            @blk.vector
            def _(v):
                v.wait_ge(pe.h, bbc_pe)
                ve.inc(v.tensor_copy(out=bbc[:], in_=psmix[:, 0:64]), 1)
            bbc_ve = ve.n

            fcol = bk[0:64, 8:9]
            rmat = cX[0:64, 0:128]
            frow = keyb[0:1, 0:128]
            psfa = psmix[0:64, 0:128]
            psfb = pstr[0][0:1, 0:128]

            @blk.vector
            def _(v):
                v.wait_ge(ve.h, bbc_ve)
                ve.inc(v.tensor_scalar(fcol, bk[0:64, 4:5],
                                       -512.0, 512.0, ALU.mult, ALU.add), 1)
                self.vsync(v, ve)
                ve.inc(v.tensor_tensor(out=fcol, in0=bk[0:64, 0:1],
                                       in1=fcol, op=ALU.add), 1)
                self.vsync(v, ve)
                ve.inc(v.tensor_scalar(fcol, fcol, 128.5, None, ALU.is_le), 1)
            esm_ve = ve.n

            @blk.tensor
            def _(t):
                t.wait_ge(ve.h, esm_ve)
                pe.inc(t.matmul(psfa[:, :], sb["identf"][0:64, 1:65],
                                bk[0:64, 384:512], start=True, stop=True), 1)
            aup_pe = pe.n

            @blk.vector
            def _(v):
                v.wait_ge(pe.h, aup_pe)
                ve.inc(v.tensor_tensor(out=rmat, in0=bk[0:64, 384:512],
                                       in1=psfa[:, :], op=ALU.subtract), 1)
            rmat_ve = ve.n

            @blk.tensor
            def _(t):
                t.wait_ge(ve.h, rmat_ve)
                pe.inc(t.matmul(psfb[:, :], fcol, rmat,
                                start=True, stop=True), 1)
            gfast_pe = pe.n

            @blk.vector
            def _(v):
                v.wait_ge(pe.h, gfast_pe)
                ve.inc(v.tensor_tensor(out=frow[0:1, 0:127], in0=bk[0:1, 273:400],
                                       in1=bk[0:1, 272:399], op=ALU.not_equal), 1)
                ve.inc(v.memset(frow[0:1, 127:128], 1.0), 1)
                self.vsync(v, ve)
                ve.inc(v.tensor_tensor(out=frow, in0=psfb[0:1, :],
                                       in1=frow, op=ALU.mult), 1)
                self.vsync(v, ve)
                ve.inc(v.tensor_tensor(out=frow, in0=frow,
                                       in1=bk[0:1, 384:512], op=ALU.mult), 1)
                self.vsync(v, ve)
                ve.inc(v.tensor_copy(out=self.fastf[:], in_=frow), 1)
            self.alive_ve = ve.n

            if dbg_etab is not None:
                @blk.sync
                def _(sync):
                    sync.wait_ge(ve.h, etab_ve)
                    sx.inc(sync.dma_start(out=dbg_etab[:, :], in_=self.etab[:]), 16)
            if dbg_mt is not None:
                @blk.sync
                def _(sync):
                    sync.wait_ge(ve.h, bbc_ve)
                    sx.inc(sync.dma_start(out=dbg_mt[:, :], in_=cX[:]), 16)

            scat_sg = {-3: sg.n, -2: sg.n, -1: sg.n}
            gqB = self.sem["gq"]
            rgate = {-2: 0, -1: 0}
            sfB_ve = {}
            for i in range(NTILE):
                par = i % 2

                @blk.vector
                def _(v):
                    v.wait_ge(ve.h, bbc_ve)
                    if rgate[i - 2] > 0:
                        v.wait_ge(gqB.h, rgate[i - 2])
                    ve.inc(v.tensor_tensor(out=sfull[par][:],
                                           in0=ctok_all[:, i * 64:(i + 1) * 64],
                                           in1=bbc[:], op=ALU.add), 1)
                    self.vsync(v, ve)
                    msl = msel_all[:, i * 64:(i + 1) * 64]
                    ve.inc(v.tensor_tensor_scan(incl[:], sb["ones64"][:], msl, 0.0,
                                                ALU.mult, ALU.add), 1)
                    self.vsync(v, ve)
                    ve.inc(v.tensor_scalar(destf[:], incl[:], iota9[:, 0:1],
                                           float(i * 128 * 9 - 1), ALU.add, ALU.add), 1)
                    self.vsync(v, ve)
                    ve.inc(v.tensor_scalar(jm[:], incl[:], -1.0, 9.0, ALU.mult, ALU.add), 1)
                    self.vsync(v, ve)
                    ve.inc(v.tensor_tensor(out=jm[:], in0=jm[:], in1=msl, op=ALU.mult), 1)
                    self.vsync(v, ve)
                    ve.inc(v.scalar_tensor_tensor(keyb[:, 0:64], jm[:], 1048576.0,
                                                  sfull[par][:], ALU.mult, ALU.add), 1)
                    ve.inc(v.scalar_tensor_tensor(keyb[:, 64:128], jm[:], 1048576.0,
                                                  destf[:], ALU.mult, ALU.add), 1)
                    ve.inc(v.tensor_scalar(jm4[:], jm[:], 4.0, None, ALU.mult), 1)
                    self.vsync(v, ve)
                    ve.inc(v.scalar_tensor_tensor(keyb[:, 128:192],
                                                  coef_all[:, i * 64:(i + 1) * 64], 0.25,
                                                  jm4[:], ALU.mult, ALU.add), 1)
                    self.vsync(v, ve)
                    ve.inc(v.max(k8[:, 0:8], keyb[:, 0:64]), 1)
                    ve.inc(v.max(k8[:, 8:16], keyb[:, 64:128]), 1)
                    ve.inc(v.max(k8[:, 16:24], keyb[:, 128:192]), 1)
                    self.vsync(v, ve)
                    st3 = stage[par][:].rearrange("p (j t) -> p j t", t=3)
                    ve.inc(v.tensor_scalar(st3[:, :, 0:1].rearrange("p j t -> p (j t)"),
                                           iota18[:], float(i * 128), None, ALU.add), 1)
                    ve.inc(v.tensor_tensor(out=st3[:, :, 1:2].rearrange("p j t -> p (j t)"),
                                           in0=k8[:, 8:16], in1=jbig[:], op=ALU.subtract), 1)
                    ve.inc(v.scalar_tensor_tensor(st3[:, :, 2:3].rearrange("p j t -> p (j t)"),
                                                  k8[:, 16:24], 4.0, js16[:],
                                                  ALU.mult, ALU.subtract), 1)
                    self.vsync(v, ve)
                    ve.inc(v.tensor_tensor(out=k8[:, 0:8], in0=k8[:, 0:8],
                                           in1=jbig[:], op=ALU.subtract), 1)
                    self.vsync(v, ve)
                    ve.inc(v.tensor_scalar(k8[:, 0:8], k8[:, 0:8],
                                           0.0, float(NSTREAM), ALU.max, ALU.min), 1)
                    self.vsync(v, ve)
                    ve.inc(v.tensor_copy(out=idxu[par][:], in_=k8[:, 0:8]), 1)
                sfB_ve[i] = ve.n

                if not self.no_scatter:
                    @blk.gpsimd
                    def _(g):
                        g.wait_ge(ve.h, sfB_ve[i])
                        for j in range(8):
                            sg.inc(g.indirect_dma_start(
                                out=d["cdest"][:, :], out_offset=IOA(idxu[par][:, j:j + 1], 0),
                                in_=stage[par][:, j * 3:(j + 1) * 3], in_offset=None), 16)
                        g.wait_ge(sg.h, sg.n)
                        g.sem_inc(gqB.h, 1)
                    gqB.n += 1
                rgate[i] = gqB.n
                scat_sg[i] = sg.n

            if dbg_coef is not None:
                self.debug.add("msel")
                dbg_msel = self.dbg("msel", [128, NTILE * 64], F32)
                @blk.sync
                def _(sync):
                    sync.wait_ge(ve.h, sfB_ve[NTILE - 1])
                    sx.inc(sync.dma_start(out=dbg_coef[:, :], in_=coef_all[:]), 16)
                    sx.inc(sync.dma_start(out=dbg_msel[:, :], in_=msel_all[:]), 16)
            if dbg_cdest is not None:
                @blk.sync
                def _(sync):
                    sync.wait_ge(sg.h, scat_sg[NTILE - 1])
                    cdf = d["cdest"].rearrange("a b -> (a b)")
                    dbf = dbg_cdest.rearrange("a b -> (a b)")
                    sx.inc(sync.dma_start(
                        out=dbf[0:NSTREAM * 3].rearrange("(p f) -> p f", p=128),
                        in_=cdf[0:NSTREAM * 3].rearrange("(p f) -> p f", p=128)), 16)
                    sx.inc(sync.dma_start(out=dbf[NSTREAM * 3:NSTREAM * 3 + 3],
                                          in_=cdf[NSTREAM * 3:NSTREAM * 3 + 3]), 16)

        self.shared_gate = {"ve": max(bbc_ve, ctok_ve), "ac": sig_ac[NTILE - 1],
                            "sw": self.sw_ready}

    def shared_ffn(self):
        nc, sb, d = self.nc, self.sb, self.dram
        blk = self.block
        sx, sg, pe, ac, ve, sw = (self.sem[k] for k in ["sx", "sg", "pe", "ac", "ve", "sw"])
        y1fm = sb["y1fm"]
        swgb, swub, swdb = self.sw_sb["swgb"], self.sw_sb["swub"], self.sw_sb["swdb"]
        gsh, hsh = self.sw_sb["gsh"], self.sw_sb["hsh"]
        osh = [self.sw_sb["osh0"], self.sw_sb["osh1"]]

        with ExitStack() as ctx:
            psF = [self.psum(ctx, "psF", [128, 512], F32) for _ in range(4)]

            silu_ac = {}
            h_ve = {}
            o_ve = {}
            o_sx = {}
            cqS = self.sem["cq"]
            shgate = {}

            for tau in range(8):
                for ic in range(8):
                    gslice = gsh[:, ic * 512:(ic + 1) * 512]
                    hslice = hsh[:, ic * 512:(ic + 1) * 512]

                    @blk.tensor
                    def _(t):
                        if tau == 0 and ic == 0:
                            t.wait_ge(sw.h, self.shared_gate["sw"])
                            t.wait_ge(ve.h, self.shared_gate["ve"])
                            t.wait_ge(ac.h, self.shared_gate["ac"])
                        prev = (tau, ic - 2) if ic >= 2 else (tau - 1, ic + 6)
                        if prev in silu_ac:
                            t.wait_ge(ac.h, silu_ac[prev])
                        if prev in h_ve:
                            t.wait_ge(ve.h, h_ve[prev])
                        if ic < 2 and tau > 0:
                            t.wait_ge(ve.h, max(o_ve[(tau - 1, 2)], o_ve[(tau - 1, 3)]))
                        for kc in range(8):
                            ins = t.matmul(psF[2 * (ic % 2)][:],
                                           swgb[:, kc * D + ic * 128: kc * D + (ic + 1) * 128],
                                           y1fm[:, kc * NTOK + tau * 512: kc * NTOK + (tau + 1) * 512],
                                           start=(kc == 0), stop=(kc == 7))
                        pe.inc(ins, 1)
                        for kc in range(8):
                            ins = t.matmul(psF[2 * (ic % 2) + 1][:],
                                           swub[:, kc * D + ic * 128: kc * D + (ic + 1) * 128],
                                           y1fm[:, kc * NTOK + tau * 512: kc * NTOK + (tau + 1) * 512],
                                           start=(kc == 0), stop=(kc == 7))
                        pe.inc(ins, 1)
                    gu_pe = pe.n

                    @blk.scalar
                    def _(a):
                        a.wait_ge(pe.h, gu_pe)
                        ac.inc(a.activation(out=gslice, in_=psF[2 * (ic % 2)][:],
                                            func=AF.Silu), 1)
                    silu_ac[(tau, ic)] = ac.n

                    @blk.vector
                    def _(v):
                        v.wait_ge(ac.h, silu_ac[(tau, ic)])
                        ve.inc(v.tensor_tensor(out=hslice, in0=psF[2 * (ic % 2) + 1][:],
                                               in1=gslice, op=ALU.mult), 1)
                    h_ve[(tau, ic)] = ve.n

                for sub in range(4):
                    par = sub % 2

                    @blk.tensor
                    def _(t):
                        t.wait_ge(ve.h, h_ve[(tau, 7)])
                        prev = (tau, sub - 2) if sub >= 2 else (tau - 1, sub + 2)
                        if prev in o_ve:
                            t.wait_ge(ve.h, o_ve[prev])
                        for mh in range(2):
                            for ic in range(8):
                                ins = t.matmul(
                                    psF[2 * par + mh][:],
                                    hsh[:, ic * 512 + sub * 128: ic * 512 + (sub + 1) * 128],
                                    swdb[:, ic * D + mh * 512: ic * D + (mh + 1) * 512],
                                    start=(ic == 0), stop=(ic == 7))
                            pe.inc(ins, 1)
                    dd_pe = pe.n

                    @blk.vector
                    def _(v):
                        v.wait_ge(pe.h, dd_pe)
                        prev = (tau, sub - 2) if sub >= 2 else (tau - 1, sub + 2)
                        if prev in shgate:
                            v.wait_ge(cqS.h, shgate[prev])
                        for mh in range(2):
                            ins = v.tensor_copy(out=osh[par][:, mh * 512:(mh + 1) * 512],
                                                in_=psF[2 * par + mh][:])
                        ve.inc(ins, 1)
                    o_ve[(tau, sub)] = ve.n

                    @blk.sync
                    def _(sync):
                        sync.wait_ge(ve.h, o_ve[(tau, sub)])
                        r0 = (tau * 512 + sub * 128) * 9 + 8
                        sx.inc(sync.dma_start(
                            out=d["slots"][r0: r0 + 128 * 9: 9, :],
                            in_=osh[par][:]), 16)
                        sync.wait_ge(sx.h, sx.n)
                        sync.sem_inc(cqS.h, 1)
                    cqS.n += 1
                    shgate[(tau, sub)] = cqS.n
                    o_sx[(tau, sub)] = sx.n

        self.barrier()

    def moe(self):
        nc, sb, d = self.nc, self.sb, self.dram
        blk = self.block
        sx, sg, pe, ac, ve, sw = (self.sem[k] for k in ["sx", "sg", "pe", "ac", "ve", "sw"])
        gq = self.sem["gq"]
        cq = self.sem["cq"]
        gg = self.sem["gg"]
        gate = {}
        cdgate = {}
        self.routing_ctx.close()
        self.routing_ctx = None
        self.shared_ctx.close()
        self.y1fm_ctx.close()

        wgv = d["wgc"]
        wuv = d["wuc"]
        wdv = d["wdc"]

        with ExitStack() as ctx:
            wgb = [self.sbuf(ctx, "wgb", [128, 8 * EI], BF16) for _ in range(3)]
            wub = [self.sbuf(ctx, "wub", [128, 8 * EI], BF16) for _ in range(3)]
            wdb = [self.sbuf(ctx, "wdb", [128, 4 * D], BF16) for _ in range(3)]
            cdr4 = [self.sbuf(ctx, "cdr4", [4, 384], F32) for _ in range(4)]
            idxu = [self.sbuf(ctx, "idxuM", [128, 4], U32) for _ in range(4)]
            destu = [self.sbuf(ctx, "destu", [128, 4], U32) for _ in range(4)]
            coefc = [self.sbuf(ctx, "coefc", [128, 4], F32) for _ in range(4)]
            xg = [self.sbuf(ctx, "xg", [128, 4 * D], BF16) for _ in range(2)]
            xgf = [self.sbuf(ctx, "xgf", [128, 8 * 512], BF16) for _ in range(2)]
            gsl = [self.sbuf(ctx, "gsl", [128, 4 * 512], BF16) for _ in range(2)]
            hb = [self.sbuf(ctx, "hb", [128, 4 * 512], BF16) for _ in range(2)]
            ot = [self.sbuf(ctx, "ot", [128, 4 * D], BF16) for _ in range(2)]
            cd_sx = {}
            w_sg = {-3: sg.n, -2: sg.n, -1: sg.n}
            cd_pe = {}
            cd_ve = {}
            gath_sg = {-3: sg.n, -2: sg.n, -1: sg.n}
            tr_pe = {-3: 0, -2: 0, -1: 0}
            xgf_ac = {-3: 0, -2: 0, -1: 0}
            gu_pe = {-3: 0, -2: 0, -1: 0}
            h_ve = {-3: 0, -2: 0, -1: 0}
            d_pe = {-3: 0, -2: 0, -1: 0}
            ev_ve = {-3: 0, -2: 0, -1: 0}
            scat_sg = {-3: sg.n, -2: sg.n, -1: sg.n}

            base_sg = sg.n
            regs_ctx = ExitStack()
            erA = regs_ctx.enter_context(nc.gpsimd.register("erA"))
            erB = regs_ctx.enter_context(nc.gpsimd.register("erB"))
            erGz = regs_ctx.enter_context(nc.gpsimd.register("erGz"))
            erG2 = regs_ctx.enter_context(nc.gpsimd.register("erG2"))
            erT = regs_ctx.enter_context(nc.tensor.register("erT"))
            erTz = regs_ctx.enter_context(nc.tensor.register("erTz"))
            erT2 = regs_ctx.enter_context(nc.tensor.register("erT2"))
            erV = regs_ctx.enter_context(nc.vector.register("erV"))
            erVz = regs_ctx.enter_context(nc.vector.register("erVz"))
            erV2 = regs_ctx.enter_context(nc.vector.register("erV2"))
            erC = regs_ctx.enter_context(nc.scalar.register("erC"))
            erCz = regs_ctx.enter_context(nc.scalar.register("erCz"))
            erC2 = regs_ctx.enter_context(nc.scalar.register("erC2"))
            av = self.alive
            fst = self.fastf
            av_ve = self.alive_ve

            @blk.gpsimd
            def _(g):
                g.wait_ge(ve.h, av_ve)
                g.reg_load(erGz, self.zi32[0:1, 0:1])

            @blk.tensor
            def _(t):
                t.wait_ge(ve.h, av_ve)
                t.reg_load(erTz, self.zi32[0:1, 0:1])

            @blk.vector
            def _(v):
                v.wait_ge(ve.h, av_ve)
                v.reg_load(erVz, self.zi32[0:1, 0:1])

            @blk.scalar
            def _(a):
                a.wait_ge(ve.h, av_ve)
                a.reg_load(erCz, self.zi32[0:1, 0:1])
            iota1c = self.sbuf(ctx, "iota1c", [128, 1], F32)
            widx_all = self.sbuf(ctx, "widx_all", [128, NSLOT], U32)
            widx_f = self.sbuf(ctx, "widx_f", [128, NSLOT], F32)
            widf = self.sbuf(ctx, "widf", [128, 8], F32)
            po = self.sem["po"]
            io = self.sem["io"]

            @blk.gpsimd
            def _(g):
                io.inc(g.iota(iota1c[:], pattern=[[0, 1]], base=0, channel_multiplier=1,
                              allow_small_or_imprecise_dtypes=True), 1)
            iw_io = io.n

            with ExitStack() as wctx:
                pswx = self.psum(wctx, "pswx", [128, NSLOT], F32)

                @blk.tensor
                def _(t):
                    pe.inc(t.matmul(pswx[:], sb["ones1p"][:], self.etabf[:],
                                    start=True, stop=True), 1)
                wx_pe = pe.n

                @blk.vector
                def _(v):
                    v.wait_ge(pe.h, wx_pe)
                    v.wait_ge(io.h, iw_io)
                    ve.inc(v.tensor_scalar(widx_f[:], pswx[:], 128.0, None, ALU.mult), 1)
                    self.vsync(v, ve)
                    ve.inc(v.tensor_scalar(widx_f[:], widx_f[:], iota1c[:, 0:1],
                                           None, ALU.add), 1)
                    self.vsync(v, ve)
                    ve.inc(v.tensor_copy(out=widx_all[:], in_=widx_f[:]), 1)
                widx_ve = ve.n

            pscd = [self.psum(ctx, "pscd", [128, 16], F32) for _ in range(2)]
            pstr = [self.psum(ctx, "pstrM", [128, 128], BF16) for _ in range(2)]
            psF = [self.psum(ctx, "psFM", [128, 512], F32) for _ in range(4)]

            def stage_cdrow(s):
                par4 = s % 4

                @blk.sync
                def _(sync):
                    if s == 0:
                        sync.wait_ge(sg.h, base_sg)
                    if s >= 4:
                        sync.wait_ge(pe.h, cd_pe[s - 4])
                    sx.inc(sync.dma_start(
                        out=cdr4[par4][:, :],
                        in_=d["cdest"][s * 512:(s + 1) * 512, :].rearrange(
                            "(a b) t -> a (b t)", a=4)), 16)
                    sync.wait_ge(sx.h, sx.n)
                    sync.sem_inc(cq.h, 1)
                cq.n += 1
                cdgate[s] = cq.n
                cd_sx[s] = sx.n

            def stage_idx(s):
                par4 = s % 4

                @blk.tensor
                def _(t):
                    t.wait_ge(cq.h, cdgate[s])
                    if s < 2:
                        t.wait_ge(ve.h, widx_ve)
                    if s >= 2:
                        t.wait_ge(ve.h, cd_ve[s - 2])
                    for f in range(3):
                        ins = t.matmul(pscd[s % 2][:, f * 4:(f + 1) * 4],
                                       cdr4[par4][:, f: 384: 3],
                                       sb["identf"][0:4, 0:4],
                                       start=True, stop=True)
                    pe.inc(ins, 1)
                cd_pe[s] = pe.n

                @blk.vector
                def _(v):
                    v.wait_ge(pe.h, cd_pe[s])
                    prot = s - 1 if s % 2 == 0 else s - 2
                    if gate.get(prot, 0) > 0:
                        v.wait_ge(gq.h, gate[prot])
                    ve.inc(v.tensor_scalar(widf[:, 0:4], pscd[s % 2][:, 0:4],
                                           0.0, float(NTOK - 1), ALU.max, ALU.min), 1)
                    self.vsync(v, ve)
                    ve.inc(v.tensor_copy(out=idxu[par4][:], in_=widf[:, 0:4]), 1)
                    ve.inc(v.tensor_scalar(widf[:, 4:8], pscd[s % 2][:, 4:8],
                                           0.0, float(SLROWS - 1), ALU.max, ALU.min), 1)
                    self.vsync(v, ve)
                    ve.inc(v.tensor_copy(out=destu[par4][:], in_=widf[:, 4:8]), 1)
                    ve.inc(v.tensor_copy(out=coefc[par4][:], in_=pscd[s % 2][:, 8:12]), 1)
                cd_ve[s] = ve.n

            def stage_wload(s):
                par3 = s % 3

                @blk.gpsimd
                def _(g):
                    if s == 0:
                        g.wait_ge(sg.h, base_sg)
                        g.wait_ge(ve.h, widx_ve)
                        g.wait_ge(self.sem["po"].h, self.wcache_po)
                    g.wait_ge(pe.h, gu_pe[s - 3])
                    g.wait_ge(pe.h, d_pe[s - 3])

                    def emit_wload(g):
                        sg.inc(g.indirect_dma_start(
                            out=wgb[par3][:, :], out_offset=None,
                            in_=wgv[:, :], in_offset=IOA(widx_all[:, s:s + 1], 0)), 16)
                        sg.inc(g.indirect_dma_start(
                            out=wub[par3][:, :], out_offset=None,
                            in_=wuv[:, :], in_offset=IOA(widx_all[:, s:s + 1], 0)), 16)
                        sg.inc(g.indirect_dma_start(
                            out=wdb[par3][:, :], out_offset=None,
                            in_=wdv[:, :], in_offset=IOA(widx_all[:, s:s + 1], 0)), 16)

                    if s < 3:
                        emit_wload(g)
                    else:
                        g.reg_load(erA, self.etab[0:1, s:s + 1])
                        g.reg_load(erB, self.etab[0:1, s - 3:s - 2])
                        n0 = sg.n
                        with g.If_ne(erA, erB):
                            emit_wload(g)
                        delta = sg.n - n0
                        with g.Else():
                            g.sem_inc(sg.h, delta)
                w_sg[s] = sg.n

            def stage_gather(s):
                par = s % 2
                par4 = s % 4

                @blk.gpsimd
                def _(g):
                    if s == 0:
                        g.wait_ge(cq.h, cdgate[0])
                    g.reg_load(erA, av[0:1, s:s + 1])
                    g.reg_load(erG2, fst[0:1, s:s + 1])
                    n0 = gg.n
                    with g.If_ne(erA, erGz):
                        g.wait_ge(ve.h, cd_ve[s])
                        g.wait_ge(pe.h, tr_pe[s - 2])
                        with g.If_ne(erG2, erGz):
                            g.indirect_dma_start(
                                out=xg[par][:, 0:D], out_offset=None,
                                in_=d["y1b"][:, :],
                                in_offset=IOA(idxu[par4][:, 0:1], 0)
                            ).then_inc(gg.h, 16)
                            g.sem_inc(gg.h, 48)
                        with g.Else():
                            for sub in range(4):
                                gg.inc(g.indirect_dma_start(
                                    out=xg[par][:, sub * D:(sub + 1) * D], out_offset=None,
                                    in_=d["y1b"][:, :],
                                    in_offset=IOA(idxu[par4][:, sub:sub + 1], 0)), 16)
                    delta = gg.n - n0
                    with g.Else():
                        g.sem_inc(gg.h, delta)
                    if s % 2 == 1:
                        g.wait_ge(gg.h, gg.n)
                        g.sem_inc(gq.h, 1)
                if s % 2 == 1:
                    gq.n += 1
                    gate[s] = gq.n
                gath_sg[s] = sg.n

            def stage_compute(s):
                par = s % 2
                par3 = s % 3
                par4 = s % 4

                if s == 0:
                    self.barrier()
                pe0 = pe.n
                ve0 = ve.n
                ac0 = ac.n
                silu_s = {ic: ac0 + ic + 1 for ic in range(4)}
                hh_s = {ic: ve0 + 33 + ic for ic in range(4)}
                evv = {sub: ve0 + 37 + sub for sub in range(4)}
                dd_s = {sub: pe0 + 40 + 2 * (sub + 1) for sub in range(4)}

                @blk.tensor
                def _(t):
                    t.reg_load(erT, av[0:1, s:s + 1])
                    t.reg_load(erT2, fst[0:1, s:s + 1])
                    with t.If_ne(erT, erTz):
                        t.wait_ge(gq.h, gate[s if s % 2 == 1 else s + 1])
                        t.wait_ge(ve.h, xgf_ac[s - 1])
                        with t.If_ne(erT2, erTz):
                            for i in range(8):
                                if i >= 2:
                                    t.wait_ge(ve.h, ve0 + i - 1)
                                t.transpose(
                                    pstr[i % 2][:],
                                    xg[par][:, i * 128:(i + 1) * 128],
                                    sb["identb"][:]).then_inc(pe.h, 1)
                            for ic in range(4):
                                if ic == 0:
                                    t.wait_ge(ve.h, ve0 + 32)
                                    t.wait_ge(ve.h, ev_ve[s - 1])
                                if ic >= 2:
                                    t.wait_ge(ac.h, silu_s[ic - 2])
                                    t.wait_ge(ve.h, hh_s[ic - 2])
                                for kc in range(8):
                                    ins = t.matmul(
                                        psF[2 * (ic % 2)][:, 0:128],
                                        wgb[par3][:, kc * EI + ic * 128: kc * EI + (ic + 1) * 128],
                                        xgf[par][:, kc * 512: kc * 512 + 128],
                                        start=(kc == 0), stop=(kc == 7))
                                ins.then_inc(pe.h, 1)
                                for kc in range(8):
                                    ins = t.matmul(
                                        psF[2 * (ic % 2) + 1][:, 0:128],
                                        wub[par3][:, kc * EI + ic * 128: kc * EI + (ic + 1) * 128],
                                        xgf[par][:, kc * 512: kc * 512 + 128],
                                        start=(kc == 0), stop=(kc == 7))
                                ins.then_inc(pe.h, 1 if ic < 3 else 17)
                            t.wait_ge(ve.h, hh_s[3])
                            for mh in range(2):
                                for ic in range(4):
                                    ins = t.matmul(
                                        psF[mh][:],
                                        hb[par][:, ic * 512: ic * 512 + 128],
                                        wdb[par3][:, ic * D + mh * 512: ic * D + (mh + 1) * 512],
                                        start=(ic == 0), stop=(ic == 3))
                                ins.then_inc(pe.h, 1 if mh == 0 else 15)
                        tEl = t.Else()
                        tEl.__enter__()
                        for i in range(32):
                            sub, kc = divmod(i, 8)
                            if i >= 2:
                                t.wait_ge(ve.h, ve0 + i - 1)
                            pe.inc(t.transpose(
                                pstr[i % 2][:],
                                xg[par][:, sub * D + kc * 128: sub * D + (kc + 1) * 128],
                                sb["identb"][:]), 1)
                        for ic in range(4):
                            if ic == 0:
                                t.wait_ge(ve.h, ve0 + 32)
                                t.wait_ge(ve.h, ev_ve[s - 1])
                            if ic >= 2:
                                t.wait_ge(ac.h, silu_s[ic - 2])
                                t.wait_ge(ve.h, hh_s[ic - 2])
                            for kc in range(8):
                                ins = t.matmul(psF[2 * (ic % 2)][:],
                                               wgb[par3][:, kc * EI + ic * 128: kc * EI + (ic + 1) * 128],
                                               xgf[par][:, kc * 512:(kc + 1) * 512],
                                               start=(kc == 0), stop=(kc == 7))
                            pe.inc(ins, 1)
                            for kc in range(8):
                                ins = t.matmul(psF[2 * (ic % 2) + 1][:],
                                               wub[par3][:, kc * EI + ic * 128: kc * EI + (ic + 1) * 128],
                                               xgf[par][:, kc * 512:(kc + 1) * 512],
                                               start=(kc == 0), stop=(kc == 7))
                            pe.inc(ins, 1)
                        for sub in range(4):
                            if sub == 0:
                                t.wait_ge(ve.h, hh_s[3])
                            if sub >= 2:
                                t.wait_ge(ve.h, evv[sub - 2])
                            for mh in range(2):
                                for ic in range(4):
                                    ins = t.matmul(
                                        psF[2 * (sub % 2) + mh][:],
                                        hb[par][:, ic * 512 + sub * 128: ic * 512 + (sub + 1) * 128],
                                        wdb[par3][:, ic * D + mh * 512: ic * D + (mh + 1) * 512],
                                        start=(ic == 0), stop=(ic == 3))
                                pe.inc(ins, 1)
                        tEl.__exit__(None, None, None)
                    dpe = pe.n - pe0
                    with t.Else():
                        t.sem_inc(pe.h, dpe)
                assert pe.n == pe0 + 48

                @blk.vector
                def _(v):
                    v.reg_load(erV, av[0:1, s:s + 1])
                    v.reg_load(erV2, fst[0:1, s:s + 1])
                    with v.If_ne(erV, erVz):
                        v.wait_ge(pe.h, gu_pe[s - 2])
                        with v.If_ne(erV2, erVz):
                            for i in range(8):
                                v.wait_ge(pe.h, pe0 + i + 1)
                                v.tensor_copy(
                                    out=xgf[par][:, i * 512: i * 512 + 128],
                                    in_=pstr[i % 2][:]
                                ).then_inc(ve.h, 1 if i < 7 else 25)
                            for ic in range(4):
                                v.wait_ge(ac.h, silu_s[ic])
                                v.tensor_tensor(
                                    out=hb[par][:, ic * 512: ic * 512 + 128],
                                    in0=psF[2 * (ic % 2) + 1][:, 0:128],
                                    in1=gsl[par][:, ic * 512: ic * 512 + 128],
                                    op=ALU.mult).then_inc(ve.h, 1)
                            v.wait_ge(pe.h, pe0 + 48)
                            for mh in range(2):
                                v.tensor_scalar(
                                    ot[par][:, mh * 512: (mh + 1) * 512],
                                    psF[mh][:],
                                    coefc[par4][:, 0:1], None, ALU.mult
                                ).then_inc(ve.h, 1 if mh == 0 else 3)
                        vEl = v.Else()
                        vEl.__enter__()
                        for i in range(32):
                            sub, kc = divmod(i, 8)
                            v.wait_ge(pe.h, pe0 + i + 1)
                            ve.inc(v.tensor_copy(
                                out=xgf[par][:, kc * 512 + sub * 128: kc * 512 + (sub + 1) * 128],
                                in_=pstr[i % 2][:]), 1)
                        for ic in range(4):
                            v.wait_ge(ac.h, silu_s[ic])
                            ve.inc(v.tensor_tensor(
                                out=hb[par][:, ic * 512:(ic + 1) * 512],
                                in0=psF[2 * (ic % 2) + 1][:],
                                in1=gsl[par][:, ic * 512:(ic + 1) * 512],
                                op=ALU.mult), 1)
                        for sub in range(4):
                            v.wait_ge(pe.h, dd_s[sub])
                            for mh in range(2):
                                ins = v.tensor_scalar(
                                    ot[par][:, sub * D + mh * 512: sub * D + (mh + 1) * 512],
                                    psF[2 * (sub % 2) + mh][:],
                                    coefc[par4][:, sub:sub + 1], None, ALU.mult)
                            ve.inc(ins, 1)
                        vEl.__exit__(None, None, None)
                    dve = ve.n - ve0
                    with v.Else():
                        v.sem_inc(ve.h, dve)
                assert ve.n == ve0 + 40

                @blk.scalar
                def _(a):
                    a.reg_load(erC, av[0:1, s:s + 1])
                    a.reg_load(erC2, fst[0:1, s:s + 1])
                    with a.If_ne(erC, erCz):
                        with a.If_ne(erC2, erCz):
                            for ic in range(4):
                                a.wait_ge(pe.h, min(pe0 + 10 + 2 * ic, pe0 + 32))
                                a.activation(out=gsl[par][:, ic * 512: ic * 512 + 128],
                                             in_=psF[2 * (ic % 2)][:, 0:128],
                                             func=AF.Silu).then_inc(ac.h, 1)
                        with a.Else():
                            for ic in range(4):
                                a.wait_ge(pe.h, pe0 + 32 + 2 * (ic + 1))
                                ac.inc(a.activation(out=gsl[par][:, ic * 512:(ic + 1) * 512],
                                                    in_=psF[2 * (ic % 2)][:], func=AF.Silu), 1)
                    dac = ac.n - ac0
                    with a.Else():
                        a.sem_inc(ac.h, dac)
                assert ac.n == ac0 + 4

                if s == 0:
                    self.barrier()
                tr_pe[s] = pe0 + 32
                xgf_ac[s] = ve0 + 32
                gu_pe[s] = pe0 + 40
                h_ve[s] = hh_s[3]
                d_pe[s] = pe0 + 48
                ev_ve[s] = ve0 + 40

                @blk.gpsimd
                def _(g):
                    g.reg_load(erA, av[0:1, s:s + 1])
                    g.reg_load(erG2, fst[0:1, s:s + 1])
                    n0 = sg.n
                    with g.If_ne(erA, erGz):
                        g.wait_ge(ve.h, ev_ve[s])
                        with g.If_ne(erG2, erGz):
                            g.indirect_dma_start(
                                out=d["slots"][:, :],
                                out_offset=IOA(destu[par4][:, 0:1], 0),
                                in_=ot[par][:, 0:D],
                                in_offset=None).then_inc(sg.h, 16)
                            g.sem_inc(sg.h, 48)
                        with g.Else():
                            for sub in range(4):
                                sg.inc(g.indirect_dma_start(
                                    out=d["slots"][:, :],
                                    out_offset=IOA(destu[par4][:, sub:sub + 1], 0),
                                    in_=ot[par][:, sub * D:(sub + 1) * D],
                                    in_offset=None), 16)
                    dsg = sg.n - n0
                    with g.Else():
                        g.sem_inc(sg.h, dsg)
                scat_sg[s] = sg.n

            for it in range(NSLOT + 3):
                if it < NSLOT:
                    stage_cdrow(it)
                if 0 <= it - 1 < NSLOT:
                    stage_wload(it - 1)
                if 0 <= it - 2 < NSLOT:
                    stage_gather(it - 2)
                if 0 <= it - 1 < NSLOT:
                    stage_idx(it - 1)
                if 0 <= it - 3 < NSLOT:
                    stage_compute(it - 3)

            regs_ctx.close()
        self.barrier()

    def final(self):
        nc, sb, d = self.nc, self.sb, self.dram
        blk = self.block
        sx, sg, pe, ac, ve = (self.sem[k] for k in ["sx", "sg", "pe", "ac", "ve"])
        dbg_slots = self.dbg("slots", [SLROWS, D], BF16)
        if dbg_slots is not None:
            aa, bb = self.sem["sg"].n, self.sem["sx"].n
            @blk.sync
            def _(sync):
                sync.wait_ge(sg.h, aa)
                sync.wait_ge(sx.h, bb)
                for r0 in range(0, SLROWS - 128, 4096):
                    sx.inc(sync.dma_start(out=dbg_slots[r0:r0 + 4096, :],
                                          in_=d["slots"][r0:r0 + 4096, :]), 16)

        with ExitStack() as ctx:
            slb = [self.sbuf(ctx, "slb", [128, 9 * D], BF16) for _ in range(2)]
            y1r = [self.sbuf(ctx, "y1r", [128, D], F32) for _ in range(2)]
            acc = [self.sbuf(ctx, "acc", [128, 2 * D], F32) for _ in range(2)]
            ytile = [self.sbuf(ctx, "ytF", [128, D], F32) for _ in range(2)]
            sqs = [self.sbuf(ctx, "sqsF", [128, D], F32) for _ in range(2)]
            stat = self.sbuf(ctx, "statF", [128, 8], F32)

            ld_sx = {}
            sum_ve = {-2: 0, -1: 0}
            all_sg = sg.n
            all_sx = sx.n

            cq = self.sem["cq"]
            fgate = {}

            @blk.sync
            def _(sync):
                sync.wait_ge(sg.h, all_sg)
                sync.wait_ge(sx.h, all_sx)
                for i in range(2):
                    for j in range(9):
                        r0 = i * 1152 + j
                        sx.inc(sync.dma_start(
                            out=slb[i][:, j * D:(j + 1) * D],
                            in_=d["slots"][r0: r0 + 128 * 9: 9, :]), 16)
                    sx.inc(sync.dma_start(out=y1r[i][:],
                                          in_=d["y1f"][i * 128:(i + 1) * 128, :]), 16)
                    ld_sx[i] = sx.n
                    sync.wait_ge(sx.h, sx.n)
                    sync.sem_inc(cq.h, 1)
            for i in range(2):
                cq.n += 1
                fgate[i] = cq.n

            for i in range(NTILE):
                par = i % 2

                @blk.vector
                def _(v):
                    v.wait_ge(cq.h, fgate[i])
                    v.wait_ge(ve.h, sum_ve[i - 2])
                    ve.inc(v.tensor_tensor(out=acc[par][:, 0:D],
                                           in0=slb[par][:, 0:D], in1=slb[par][:, D:2 * D],
                                           op=ALU.add), 1)
                    ve.inc(v.tensor_tensor(out=acc[par][:, D:2 * D],
                                           in0=slb[par][:, 2 * D:3 * D],
                                           in1=slb[par][:, 3 * D:4 * D], op=ALU.add), 1)
                    self.vsync(v, ve)
                    ve.inc(v.tensor_tensor(out=acc[par][:, 0:D], in0=acc[par][:, 0:D],
                                           in1=acc[par][:, D:2 * D], op=ALU.add), 1)
                    ve.inc(v.tensor_tensor(out=acc[par][:, D:2 * D],
                                           in0=slb[par][:, 4 * D:5 * D],
                                           in1=slb[par][:, 5 * D:6 * D], op=ALU.add), 1)
                    self.vsync(v, ve)
                    ve.inc(v.tensor_tensor(out=acc[par][:, 0:D], in0=acc[par][:, 0:D],
                                           in1=acc[par][:, D:2 * D], op=ALU.add), 1)
                    ve.inc(v.tensor_tensor(out=acc[par][:, D:2 * D],
                                           in0=slb[par][:, 6 * D:7 * D],
                                           in1=slb[par][:, 7 * D:8 * D], op=ALU.add), 1)
                    self.vsync(v, ve)
                    ve.inc(v.tensor_tensor(out=acc[par][:, 0:D], in0=acc[par][:, 0:D],
                                           in1=acc[par][:, D:2 * D], op=ALU.add), 1)
                    self.vsync(v, ve)
                    ve.inc(v.tensor_tensor(out=acc[par][:, 0:D], in0=acc[par][:, 0:D],
                                           in1=slb[par][:, 8 * D:9 * D], op=ALU.add), 1)
                    self.vsync(v, ve)
                    ve.inc(v.tensor_tensor(out=ytile[par][:], in0=acc[par][:, 0:D],
                                           in1=y1r[par][:], op=ALU.add), 1)
                    self.vsync(v, ve)
                    ve.inc(v.tensor_reduce(stat[:, 0:1], ytile[par][:], AX.X, ALU.add), 1)
                    self.vsync(v, ve)
                    ve.inc(v.tensor_scalar(stat[:, 1:2], stat[:, 0:1], -1.0 / D, None, ALU.mult), 1)
                    self.vsync(v, ve)
                    ve.inc(v.tensor_scalar(ytile[par][:], ytile[par][:], stat[:, 1:2], None, ALU.add), 1)
                    self.vsync(v, ve)
                    ve.inc(v.tensor_tensor(out=sqs[par][:], in0=ytile[par][:],
                                           in1=ytile[par][:], op=ALU.mult), 1)
                    self.vsync(v, ve)
                    ve.inc(v.tensor_reduce(stat[:, 3:4], sqs[par][:], AX.X, ALU.add), 1)
                    self.vsync(v, ve)
                    ve.inc(v.tensor_scalar(stat[:, 6:7], stat[:, 3:4], 1.0 / D, 1e-5,
                                           ALU.mult, ALU.add), 1)
                sq_ve = ve.n

                @blk.scalar
                def _(a):
                    a.wait_ge(ve.h, sq_ve)
                    ac.inc(a.sqrt(out=stat[:, 4:5], in_=stat[:, 6:7]), 1)
                rstd_ac = ac.n

                @blk.vector
                def _(v):
                    v.wait_ge(ac.h, rstd_ac)
                    ve.inc(v.reciprocal(stat[:, 5:6], stat[:, 4:5]), 1)
                    self.vsync(v, ve)
                    ve.inc(v.tensor_scalar(ytile[par][:], ytile[par][:], stat[:, 5:6],
                                           None, ALU.mult), 1)
                sum_ve[i] = ve.n

                @blk.sync
                def _(sync):
                    sync.wait_ge(ve.h, sum_ve[i])
                    sx.inc(sync.dma_start(out=d["out"][i * 128:(i + 1) * 128, :],
                                          in_=ytile[par][:]), 16)
                    if i + 2 < NTILE:
                        for j in range(9):
                            r0 = (i + 2) * 1152 + j
                            sx.inc(sync.dma_start(
                                out=slb[par][:, j * D:(j + 1) * D],
                                in_=d["slots"][r0: r0 + 128 * 9: 9, :]), 16)
                        sx.inc(sync.dma_start(out=y1r[par][:],
                                              in_=d["y1f"][(i + 2) * 128:(i + 3) * 128, :]), 16)
                        ld_sx[i + 2] = sx.n
                        sync.wait_ge(sx.h, sx.n)
                        sync.sem_inc(cq.h, 1)
                if i + 2 < NTILE:
                    cq.n += 1
                    fgate[i + 2] = cq.n

            tgt = sx.n

            @blk.sync
            def _(sync):
                sync.wait_ge(sx.h, tgt)

    def finalize_stub(self):
        nc, d = self.nc, self.dram
        blk = self.block
        sx = self.sem["sx"]

        @blk.sync
        def _(sync):
            sync.wait_ge(sx.h, sx.n)
            for i in range(NTILE):
                sx.inc(sync.dma_start(out=d["out"][i * 128:(i + 1) * 128, :],
                                      in_=d["y1f"][i * 128:(i + 1) * 128, :]), 16)
        tgt = sx.n

        @blk.sync
        def _(sync):
            sync.wait_ge(sx.h, tgt)

    def finish(self):
        if getattr(self, "routing_ctx", None) is not None:
            try:
                self.routing_ctx.close()
            except Exception:
                pass
            self.routing_ctx = None
        self.ctx.close()
        return self.nc


def make_consts():
    identb = np.eye(128).astype(ml_dtypes.bfloat16)
    identf = np.eye(128).astype(np.float32)
    cc = np.arange(384)[None, :]
    pp = np.arange(128)[:, None]
    triu = (cc >= pp + 128).astype(ml_dtypes.bfloat16)
    cdpat = np.zeros((128, 1536), np.float32)
    cdpat[:, 1::3] = (DUMP0 + np.arange(128))[:, None]
    return {"identb": identb, "identf": identf, "triu": triu, "cdpat": cdpat}


def build_nc(debug=(), stages=("attn", "wo", "routing", "shared", "moe", "final")):
    b = B(debug=debug, no_scatter="noscat" in stages)
    b.load_consts()
    b.attention()
    b.wo_ln1()
    if "routing" in stages:
        b.routing()
    if "shared" in stages:
        b.shared_ffn()
    if "moe" in stages:
        b.moe()
    if "final" in stages:
        b.final()
    else:
        b.finalize_stub()
    b.finish()
    return b


def kernel(L0_attn_x, L0_attn_W_q, L0_attn_W_k, L0_attn_W_v, L0_attn_W_o,
           L0_ffn_router_w, L0_ffn_e_bias, L0_ffn_Wg, L0_ffn_Wu, L0_ffn_Wd,
           L0_ffn_SWg, L0_ffn_SWu, L0_ffn_SWd, _debug=(), _cores=None,
           _stages=("attn", "wo", "routing", "shared", "moe", "final")):
    x = np.ascontiguousarray(np.asarray(L0_attn_x, np.float32))
    consts = make_consts()
    common = {
        "wq": np.ascontiguousarray(L0_attn_W_q), "wk": np.ascontiguousarray(L0_attn_W_k),
        "wv": np.ascontiguousarray(L0_attn_W_v), "wo": np.ascontiguousarray(L0_attn_W_o),
        "rw": np.ascontiguousarray(L0_ffn_router_w),
        "ebias": np.ascontiguousarray(np.asarray(L0_ffn_e_bias, np.float32).reshape(1, E)),
        "wg": np.ascontiguousarray(np.asarray(L0_ffn_Wg).reshape(E, 8, 128, EI)
                                   .transpose(0, 2, 1, 3).reshape(E * 128, 8 * EI)),
        "wu": np.ascontiguousarray(np.asarray(L0_ffn_Wu).reshape(E, 8, 128, EI)
                                   .transpose(0, 2, 1, 3).reshape(E * 128, 8 * EI)),
        "wd": np.ascontiguousarray(np.asarray(L0_ffn_Wd).reshape(E, 4, 128, D)
                                   .transpose(0, 2, 1, 3).reshape(E * 128, 4 * D)),
        "swg": np.ascontiguousarray(L0_ffn_SWg), "swu": np.ascontiguousarray(L0_ffn_SWu),
        "swd": np.ascontiguousarray(L0_ffn_SWd),
        **consts,
    }
    cores = list(range(N_CORES)) if _cores is None else _cores
    n = len(cores)
    in_maps = []
    for i in range(n):
        m = dict(common)
        m["x"] = np.ascontiguousarray(x[2 * i:2 * i + 2].reshape(NTOK, D))
        in_maps.append(m)
    b = build_nc(debug=_debug, stages=_stages)
    res = run_bass_kernel_spmd(b.nc, in_maps, core_ids=cores)
    outs = [res.results[i]["out"].reshape(2, SEQ, D) for i in range(n)]
    full = np.concatenate(outs, axis=0).astype(np.float32)
    if _debug:
        return full, [res.results[i] for i in range(n)]
    return full


# version 20
# speedup vs baseline: 1.2123x; 1.2123x over previous
import numpy as np
import ml_dtypes
from contextlib import ExitStack

import concourse.bass as bass
import concourse.mybir as mybir
from concourse.bass_utils import run_bass_kernel_spmd

F32 = mybir.dt.float32
BF16 = mybir.dt.bfloat16
I32 = mybir.dt.int32
U32 = mybir.dt.uint32
AF = mybir.ActivationFunctionType
ALU = mybir.AluOpType
AX = mybir.AxisListType
IOA = bass.IndirectOffsetOnAxis

N_CORES = 8
NTOK = 4096
SEQ = 2048
D = 1024
H = 16
CH = 256
NCH = SEQ // CH
NTILE = NTOK // 128
E = 64
EI = 512
SCAP = 512
NSLOT = 128
NSTREAM = NSLOT * SCAP
DUMP0 = NTOK * 9
SLROWS = DUMP0 + 128


class Counter:
    def __init__(self, nc, name):
        self.h = nc.alloc_semaphore(name=name)
        self.n = 0

    def inc(self, instr, k=1):
        instr.then_inc(self.h, k)
        self.n += k
        return instr


class B:
    def __init__(self, debug=(), no_scatter=False):
        self.no_scatter = no_scatter
        self.debug = set(debug)
        nc = bass.Bass("TRN2", target_bir_lowering=False, debug=False)
        self.nc = nc
        self.ctx = ExitStack()
        self._uid = 0
        d = self.dram = {}
        d["x"] = nc.dram_tensor("x", [NTOK, D], F32, kind="ExternalInput").ap()
        for w in ["wq", "wk", "wv", "wo", "swg", "swu", "swd"]:
            d[w] = nc.dram_tensor(w, [D, D], F32, kind="ExternalInput").ap()
        d["rw"] = nc.dram_tensor("rw", [E, D], F32, kind="ExternalInput").ap()
        d["ebias"] = nc.dram_tensor("ebias", [1, E], F32, kind="ExternalInput").ap()
        d["wg"] = nc.dram_tensor("wg", [E * 128, 8 * EI], F32, kind="ExternalInput").ap()
        d["wu"] = nc.dram_tensor("wu", [E * 128, 8 * EI], F32, kind="ExternalInput").ap()
        d["wd"] = nc.dram_tensor("wd", [E * 128, 4 * D], F32, kind="ExternalInput").ap()
        d["identb"] = nc.dram_tensor("identb", [128, 128], BF16, kind="ExternalInput").ap()
        d["identf"] = nc.dram_tensor("identf", [128, 128], F32, kind="ExternalInput").ap()
        d["triu"] = nc.dram_tensor("triu", [128, 384], BF16, kind="ExternalInput").ap()
        d["cdpat"] = nc.dram_tensor("cdpat", [128, 1536], F32, kind="ExternalInput").ap()
        d["out"] = nc.dram_tensor("out", [NTOK, D], F32, kind="ExternalOutput").ap()
        d["od"] = nc.dram_tensor("od", [NTOK, D], BF16, kind="Internal").ap()
        d["y1f"] = nc.dram_tensor("y1f", [NTOK, D], F32, kind="Internal").ap()
        d["y1b"] = nc.dram_tensor("y1b", [NTOK, D], BF16, kind="Internal").ap()
        d["cdest"] = nc.dram_tensor("cdest", [NSTREAM + 1, 3], F32, kind="Internal").ap()
        d["wgc"] = nc.dram_tensor("wgc", [E * 128, 8 * EI], BF16, kind="Internal").ap()
        d["wuc"] = nc.dram_tensor("wuc", [E * 128, 8 * EI], BF16, kind="Internal").ap()
        d["wdc"] = nc.dram_tensor("wdc", [E * 128, 4 * D], BF16, kind="Internal").ap()
        d["slots"] = nc.dram_tensor("slots", [SLROWS, D], BF16, kind="Internal").ap()
        self.dbg_out = {}

        e = self.ctx.enter_context
        self.sem = {k: Counter(nc, k) for k in
                    ["sx", "sg", "sw", "pe", "ve", "ac", "po", "bar", "gq", "cq", "io",
                     "gg"]}
        sb = self.sb = {}
        sb["identb"] = e(nc.sbuf_tensor("sb_identb", [128, 128], BF16))
        sb["identf"] = e(nc.sbuf_tensor("sb_identf", [128, 128], F32))
        sb["triu"] = e(nc.sbuf_tensor("sb_triu", [128, 384], BF16))
        sb["ones1p"] = e(nc.sbuf_tensor("ones1p", [1, 128], F32))
        sb["ones11"] = e(nc.sbuf_tensor("ones11", [1, 1], F32))
        sb["onesc"] = e(nc.sbuf_tensor("onesc", [64, 1], F32))
        sb["ones64"] = e(nc.sbuf_tensor("ones64", [128, 64], F32))
        self.etab = e(nc.sbuf_tensor("etab", [1, 128], I32))
        self.etabf = e(nc.sbuf_tensor("etabf", [1, 128], F32))
        self.alive = e(nc.sbuf_tensor("alive", [1, 128], I32))
        self.fastf = e(nc.sbuf_tensor("fastf", [1, 128], I32))
        self.zi32 = e(nc.sbuf_tensor("zi32", [1, 4], I32))
        self.y1fm_ctx = ExitStack()
        sb["y1fm"] = self.y1fm_ctx.enter_context(nc.sbuf_tensor("y1fm", [128, 8 * NTOK], BF16))
        self.block = e(nc.Block())

    def u(self, base):
        self._uid += 1
        return f"{base}_{self._uid}"

    def sbuf(self, ctx, base, shape, dtype):
        return ctx.enter_context(self.nc.sbuf_tensor(self.u(base), shape, dtype))

    def psum(self, ctx, base, shape, dtype=F32):
        return ctx.enter_context(self.nc.psum_tensor(self.u(base), shape, dtype))

    def dbg(self, name, shape, dtype=F32):
        if name not in self.debug:
            return None
        if name not in self.dbg_out:
            self.dbg_out[name] = self.nc.dram_tensor(
                "dbg_" + name, shape, dtype, kind="ExternalOutput").ap()
        return self.dbg_out[name]

    def barrier(self):
        bar = self.sem["bar"]
        blk = self.block
        for sec in [blk.sync, blk.gpsimd, blk.tensor, blk.vector, blk.scalar]:
            @sec
            def _(eng):
                eng.sem_inc(bar.h, 1)
        bar.n += 5
        tgt = bar.n
        for sec in [blk.sync, blk.gpsimd, blk.tensor, blk.vector, blk.scalar]:
            @sec
            def _(eng):
                eng.wait_ge(bar.h, tgt)

    def load_consts(self):
        nc, sb, d = self.nc, self.sb, self.dram
        sx, ve = self.sem["sx"], self.sem["ve"]

        @self.block.sync
        def _(sync):
            sx.inc(sync.dma_start(out=sb["identb"][:], in_=d["identb"][:, :]), 16)
            sx.inc(sync.dma_start(out=sb["identf"][:], in_=d["identf"][:, :]), 16)
            sx.inc(sync.dma_start(out=sb["triu"][:], in_=d["triu"][:, :]), 16)
            cd2 = d["cdest"]
            sx.inc(sync.dma_start(
                out=cd2[0:NSTREAM, :].rearrange("a b -> (a b)").rearrange("(p f) -> p f", p=128),
                in_=d["cdpat"][:, :]), 16)
            sx.inc(sync.dma_start(out=cd2[NSTREAM:NSTREAM + 1, :], in_=d["cdpat"][0:1, 0:3]), 16)

        @self.block.vector
        def _(v):
            v.memset(sb["ones1p"][:], 1.0)
            v.memset(sb["ones11"][:], 1.0)
            v.memset(sb["onesc"][:], 1.0)
            ve.inc(v.memset(sb["ones64"][:], 1.0), 1)
        self.consts_ve = ve.n
        self.consts_sx = sx.n
        blk = self.block
        for sec in [blk.sync, blk.gpsimd, blk.tensor, blk.vector, blk.scalar]:
            @sec
            def _(eng):
                eng.wait_ge(sx.h, self.consts_sx)
                eng.wait_ge(ve.h, self.consts_ve)

    def attention(self):
        nc, sb, d = self.nc, self.sb, self.dram
        blk = self.block
        sg, pe, ac, ve, sw = (self.sem[k] for k in ["sg", "pe", "ac", "ve", "sw"])

        with ExitStack() as ctx:
            otok = [self.sbuf(ctx, "otok", [128, 2 * D], BF16) for _ in range(2)]
            wqb = self.sbuf(ctx, "wqb", [128, 8 * D], BF16)
            wkb = self.sbuf(ctx, "wkb", [128, 8 * D], BF16)
            wvb = self.sbuf(ctx, "wvb", [128, 8 * D], BF16)
            xstage = [self.sbuf(ctx, "xs", [128, 2 * D], F32) for _ in range(2)]
            xf = self.sbuf(ctx, "xf", [128, 8 * CH], BF16)
            qt = self.sbuf(ctx, "qt", [128, 2 * D], BF16)
            kt = self.sbuf(ctx, "kt", [128, 2 * D], BF16)
            vt = self.sbuf(ctx, "vt", [128, 2 * D], BF16)
            qf = self.sbuf(ctx, "qf", [128, 8 * CH], BF16)
            kf = self.sbuf(ctx, "kf", [128, 8 * CH], BF16)
            i1 = self.sbuf(ctx, "i1", [128, 2 * CH], BF16)
            i2 = self.sbuf(ctx, "i2", [128, 2 * CH], BF16)
            zfb = self.sbuf(ctx, "zfb", [128, CH], BF16)
            ztb = self.sbuf(ctx, "ztb", [128, 128], BF16)
            s1f = self.sbuf(ctx, "s1f", [128, H * 64], F32)
            s2f = self.sbuf(ctx, "s2f", [128, H * 64], F32)
            s1b = self.sbuf(ctx, "s1b", [128, H * 64], BF16)
            s2b = self.sbuf(ctx, "s2b", [128, H * 64], BF16)

            @blk.gpsimd
            def _(g):
                for wname, wt in [("wq", wqb), ("wk", wkb), ("wv", wvb)]:
                    for kc in range(8):
                        sw.inc(g.dma_start(
                            out=wt[:, kc * D:(kc + 1) * D],
                            in_=d[wname][kc * 128:(kc + 1) * 128, :]), 16)
            w_ready = sw.n
            wcc = self.sem["po"]

            @blk.gpsimd
            def _(g):
                for src, dst in [("wg", "wgc"), ("wu", "wuc"), ("wd", "wdc")]:
                    for e_ in range(E):
                        wcc.inc(g.dma_start(
                            out=d[dst][e_ * 128:(e_ + 1) * 128, :],
                            in_=d[src][e_ * 128:(e_ + 1) * 128, :]), 16)
            self.wcache_po = wcc.n

            xgate = {}
            cqA = self.sem["cq"]
            sxc = self.sem["sx"]
            chunks = [(s, c) for s in range(2) for c in range(NCH)]

            @blk.sync
            def _(sync):
                for ci in range(min(2, len(chunks))):
                    s, c = chunks[ci]
                    for t2 in range(2):
                        r0 = s * SEQ + c * CH + 128 * t2
                        sxc.inc(sync.dma_start(out=xstage[ci % 2][:, t2 * D:(t2 + 1) * D],
                                               in_=d["x"][r0:r0 + 128, :]), 16)
                    sync.wait_ge(sxc.h, sxc.n)
                    sync.sem_inc(cqA.h, 1)
            for ci in range(min(2, len(chunks))):
                cqA.n += 1
                xgate[ci] = cqA.n

            last = {"i1v": 0, "i2v": 0, "zac": 0, "oac": 0, "s1v": 0, "s2v": 0,
                    "mix_pe": 0}
            od_sx = {}

            for ci, (s, c) in enumerate(chunks):
                par = ci % 2
                tok0 = s * SEQ + c * CH

                if c == 0:
                    @blk.vector
                    def _(v):
                        if ci > 0:
                            v.wait_ge(pe.h, last["mix_pe"])
                        for t in [s1f, s2f, s1b, s2b]:
                            ins = v.memset(t[:], 0.0)
                        ve.inc(ins, 1)
                st_reset = ve.n

                with ExitStack() as cctx:
                    pstr = [self.psum(cctx, "pstr", [128, 128], BF16) for _ in range(2)]
                    psp = [self.psum(cctx, "psp", [128, 512], F32) for _ in range(2)]

                    tr_ac0 = ac.n
                    tr_pe0 = pe.n

                    @blk.tensor
                    def _(t):
                        t.wait_ge(cqA.h, xgate[ci])
                        if ci == 0:
                            t.wait_ge(self.sem["sx"].h, self.consts_sx)
                        for i in range(16):
                            t2, kc = divmod(i, 8)
                            if i >= 2:
                                t.wait_ge(ac.h, tr_ac0 + i - 1)
                            else:
                                t.wait_ge(ac.h, tr_ac0)
                            pe.inc(t.transpose(
                                psp[i % 2][:, 0:128],
                                xstage[par][:, t2 * D + kc * 128: t2 * D + (kc + 1) * 128],
                                sb["identf"][:]), 1)

                    @blk.scalar
                    def _(a):
                        for i in range(16):
                            t2, kc = divmod(i, 8)
                            a.wait_ge(pe.h, tr_pe0 + i + 1)
                            ac.inc(a.copy(
                                out=xf[:, kc * CH + t2 * 128: kc * CH + (t2 + 1) * 128],
                                in_=psp[i % 2][:, 0:128]), 1)
                    xf_done = ac.n

                    if ci + 2 < len(chunks):
                        s2_, c2_ = chunks[ci + 2]

                        @blk.sync
                        def _(sync):
                            sync.wait_ge(pe.h, tr_pe0 + 16)
                            for t2 in range(2):
                                r0 = s2_ * SEQ + c2_ * CH + 128 * t2
                                sxc.inc(sync.dma_start(
                                    out=xstage[par][:, t2 * D:(t2 + 1) * D],
                                    in_=d["x"][r0:r0 + 128, :]), 16)
                            sync.wait_ge(sxc.h, sxc.n)
                            sync.sem_inc(cqA.h, 1)
                        cqA.n += 1
                        xgate[ci + 2] = cqA.n

                    prj_ac0 = ac.n
                    prj_pe = []

                    @blk.tensor
                    def _(t):
                        t.wait_ge(ac.h, xf_done)
                        if ci == 0:
                            t.wait_ge(sw.h, w_ready)
                        n = 0
                        for t2 in range(2):
                            for wt in [wqb, wkb, wvb]:
                                for half in range(2):
                                    if n >= 2:
                                        t.wait_ge(ac.h, prj_ac0 + n - 1)
                                    for kc in range(8):
                                        ins = t.matmul(
                                            psp[n % 2][:],
                                            xf[:, kc * CH + t2 * 128: kc * CH + (t2 + 1) * 128],
                                            wt[:, kc * D + half * 512: kc * D + (half + 1) * 512],
                                            start=(kc == 0), stop=(kc == 7))
                                    pe.inc(ins, 1)
                                    prj_pe.append(pe.n)
                                    n += 1

                    @blk.scalar
                    def _(a):
                        n = 0
                        for t2 in range(2):
                            for dst in [qt, kt, vt]:
                                for half in range(2):
                                    a.wait_ge(pe.h, prj_pe[n])
                                    ac.inc(a.copy(
                                        out=dst[:, t2 * D + half * 512: t2 * D + (half + 1) * 512],
                                        in_=psp[n % 2][:]), 1)
                                    n += 1
                    prj_done = ac.n

                    tr2_ac0 = ac.n
                    tr2_pe0 = pe.n

                    @blk.tensor
                    def _(t):
                        t.wait_ge(ac.h, prj_done)
                        for i in range(32):
                            si, rem = divmod(i, 16)
                            t2, kc = divmod(rem, 8)
                            srct = qt if si == 0 else kt
                            if i >= 2:
                                t.wait_ge(ac.h, tr2_ac0 + i - 1)
                            pe.inc(t.transpose(
                                pstr[i % 2][:],
                                srct[:, t2 * D + kc * 128: t2 * D + (kc + 1) * 128],
                                sb["identb"][:]), 1)

                    @blk.scalar
                    def _(a):
                        for i in range(32):
                            si, rem = divmod(i, 16)
                            t2, kc = divmod(rem, 8)
                            dstt = qf if si == 0 else kf
                            a.wait_ge(pe.h, tr2_pe0 + i + 1)
                            ac.inc(a.copy(
                                out=dstt[:, kc * CH + t2 * 128: kc * CH + (t2 + 1) * 128],
                                in_=pstr[i % 2][:]), 1)
                    qkf_done = ac.n

                with ExitStack() as hctx:
                    psAB = self.psum(hctx, "psAB", [128, 512], F32)
                    psZO = self.psum(hctx, "psZO", [128, 256], F32)
                    psO2 = self.psum(hctx, "psO2", [128, 128], F32)
                    psZt = self.psum(hctx, "psZt", [128, 128], F32)
                    psS = self.psum(hctx, "psS", [128, 128], F32)

                    for h in range(H):
                        po2 = (h % 2) * 64
                        kc = h // 2
                        qfh = qf[po2:po2 + 64, kc * CH:(kc + 1) * CH]
                        kfh = kf[po2:po2 + 64, kc * CH:(kc + 1) * CH]
                        s1bh = s1b[po2:po2 + 64, h * 64:(h + 1) * 64]
                        s2bh = s2b[po2:po2 + 64, h * 64:(h + 1) * 64]
                        s1fh = s1f[po2:po2 + 64, h * 64:(h + 1) * 64]
                        s2fh = s2f[po2:po2 + 64, h * 64:(h + 1) * 64]

                        def kth(t2):
                            return kt[:, t2 * D + h * 64: t2 * D + h * 64 + 64]

                        def vth(t2):
                            return vt[:, t2 * D + h * 64: t2 * D + h * 64 + 64]

                        @blk.tensor
                        def _(t):
                            t.wait_ge(ac.h, qkf_done)
                            t.wait_ge(ve.h, max(st_reset, last["i2v"]))
                            for st in range(2):
                                pe.inc(t.matmul(psAB[:, st * CH:(st + 1) * CH],
                                                kfh[:, st * 128:(st + 1) * 128],
                                                qfh[:], start=True, stop=True), 1)
                        at_pe = pe.n

                        @blk.vector
                        def _(v):
                            v.wait_ge(pe.h, at_pe)
                            for st in range(2):
                                mask = sb["triu"][:, 128 - st * 128: 256 - st * 128 + 128]
                                ve.inc(v.tensor_tensor(
                                    out=i1[:, st * CH:(st + 1) * CH],
                                    in0=psAB[:, st * CH:(st + 1) * CH],
                                    in1=mask, op=ALU.mult), 1)
                        last["i1v"] = ve.n

                        @blk.tensor
                        def _(t):
                            t.wait_ge(ve.h, last["i1v"])
                            t.wait_ge(ac.h, max(last["zac"], last["oac"]))
                            t.wait_ge(ve.h, last["s1v"])
                            for st in range(2):
                                t.matmul(psZO[po2:po2 + 64, 0:CH], kth(st),
                                         i1[:, st * CH:(st + 1) * CH],
                                         start=(st == 0), stop=False)
                            pe.inc(t.matmul(psZO[po2:po2 + 64, 0:CH], s1bh, qfh[:],
                                            start=False, stop=True), 1)
                            for rt in range(2):
                                for st in range(2):
                                    t.matmul(psZt[:, rt * 64:(rt + 1) * 64],
                                             i1[:, st * CH + rt * 128: st * CH + (rt + 1) * 128],
                                             kth(st), start=(st == 0), stop=False)
                                pe.inc(t.matmul(psZt[:, rt * 64:(rt + 1) * 64],
                                                qfh[:, rt * 128:(rt + 1) * 128], s1bh,
                                                start=False, stop=True), 1)
                            for st in range(2):
                                ins = t.matmul(psS[po2:po2 + 64, 0:64], kth(st), kth(st),
                                               start=(st == 0), stop=(st == 1))
                            pe.inc(ins, 1)
                        z_pe = pe.n

                        @blk.scalar
                        def _(a):
                            a.wait_ge(pe.h, z_pe)
                            a.copy(out=zfb[po2:po2 + 64, :], in_=psZO[po2:po2 + 64, 0:CH])
                            ac.inc(a.copy(out=ztb[:], in_=psZt[:]), 1)
                        last["zac"] = ac.n

                        @blk.vector
                        def _(v):
                            v.wait_ge(pe.h, z_pe)
                            ve.inc(v.tensor_tensor(out=s1fh, in0=s1fh,
                                                   in1=psS[po2:po2 + 64, 0:64], op=ALU.add), 1)
                            v.wait_ge(ve.h, ve.n)
                            ve.inc(v.tensor_copy(out=s1bh, in_=s1fh), 1)
                        last["s1v"] = ve.n

                        @blk.tensor
                        def _(t):
                            t.wait_ge(ac.h, last["zac"])
                            t.wait_ge(ve.h, last["i1v"])
                            for rt in range(2):
                                pe.inc(t.matmul(psAB[:, rt * CH:(rt + 1) * CH],
                                                zfb[po2:po2 + 64, rt * 128:(rt + 1) * 128],
                                                qfh[:], start=True, stop=True), 1)
                        bt_pe = pe.n

                        @blk.vector
                        def _(v):
                            v.wait_ge(pe.h, bt_pe)
                            for rt in range(2):
                                mask = sb["triu"][:, 128 - rt * 128: 256 - rt * 128 + 128]
                                ve.inc(v.tensor_tensor(
                                    out=i2[:, rt * CH:(rt + 1) * CH],
                                    in0=psAB[:, rt * CH:(rt + 1) * CH],
                                    in1=mask, op=ALU.mult), 1)
                        last["i2v"] = ve.n

                        @blk.tensor
                        def _(t):
                            t.wait_ge(ve.h, last["i2v"])
                            t.wait_ge(ac.h, last["oac"])
                            t.wait_ge(ve.h, last["s2v"])
                            for ih in range(2):
                                for rt in range(2):
                                    t.matmul(psO2[:, ih * 64:(ih + 1) * 64],
                                             i2[:, rt * CH + ih * 128: rt * CH + (ih + 1) * 128],
                                             vth(rt), start=(rt == 0), stop=False)
                                pe.inc(t.matmul(psO2[:, ih * 64:(ih + 1) * 64],
                                                qfh[:, ih * 128:(ih + 1) * 128], s2bh,
                                                start=False, stop=True), 1)
                            for rt in range(2):
                                ins = t.matmul(psS[po2:po2 + 64, 64:128], ztb[:, rt * 64:(rt + 1) * 64],
                                               vth(rt), start=(rt == 0), stop=(rt == 1))
                            pe.inc(ins, 1)
                        o_pe = pe.n

                        @blk.scalar
                        def _(a):
                            a.wait_ge(pe.h, o_pe)
                            if h == 0:
                                a.wait_ge(self.sem["sx"].h, od_sx.get(ci - 2, 0))
                            for ih in range(2):
                                ins = a.copy(
                                    out=otok[par][:, ih * D + h * 64: ih * D + h * 64 + 64],
                                    in_=psO2[:, ih * 64:(ih + 1) * 64])
                            ac.inc(ins, 1)
                        last["oac"] = ac.n

                        @blk.vector
                        def _(v):
                            v.wait_ge(pe.h, o_pe)
                            ve.inc(v.tensor_tensor(out=s2fh, in0=s2fh,
                                                   in1=psS[po2:po2 + 64, 64:128], op=ALU.add), 1)
                            v.wait_ge(ve.h, ve.n)
                            ve.inc(v.tensor_copy(out=s2bh, in_=s2fh), 1)
                        last["s2v"] = ve.n

                    last["mix_pe"] = pe.n

                @blk.sync
                def _(sync):
                    sync.wait_ge(ac.h, last["oac"])
                    for ih in range(2):
                        self.sem["sx"].inc(sync.dma_start(
                            out=d["od"][tok0 + ih * 128: tok0 + (ih + 1) * 128, :],
                            in_=otok[par][:, ih * D:(ih + 1) * D]), 16)
                od_sx[ci] = self.sem["sx"].n

            self.barrier()

    def wo_ln1(self):
        nc, sb, d = self.nc, self.sb, self.dram
        blk = self.block
        sx, sg, pe, ac, ve, sw = (self.sem[k] for k in ["sx", "sg", "pe", "ac", "ve", "sw"])

        dbg_y1 = self.dbg("y1", [NTOK, D], F32)

        with ExitStack() as ctx:
            wob = self.sbuf(ctx, "wob", [128, 8 * D], BF16)
            odrow = [self.sbuf(ctx, "odr", [128, D], BF16) for _ in range(2)]
            ofeat = [self.sbuf(ctx, "ofe", [128, D], BF16) for _ in range(2)]
            xrow = [self.sbuf(ctx, "xr", [128, D], F32) for _ in range(2)]
            sqs = [self.sbuf(ctx, "sqs", [128, D], F32) for _ in range(2)]
            ytile = [self.sbuf(ctx, "yt", [128, D], F32) for _ in range(2)]
            ybtile = [self.sbuf(ctx, "yb", [128, D], BF16) for _ in range(2)]
            stat = self.sbuf(ctx, "stat", [128, 8], F32)
            psw = [self.psum(ctx, "psw", [128, 512], F32) for _ in range(4)]
            pstr = [self.psum(ctx, "pstr", [128, 128], BF16) for _ in range(2)]

            @blk.gpsimd
            def _(g):
                for kc in range(8):
                    sw.inc(g.dma_start(out=wob[:, kc * D:(kc + 1) * D],
                                       in_=d["wo"][kc * 128:(kc + 1) * 128, :]), 16)
            wo_ready = sw.n

            x_done = {}
            od_done = {}

            @blk.sync
            def _(sync):
                for i in range(2):
                    sx.inc(sync.dma_start(out=xrow[i][:],
                                          in_=d["x"][i * 128:(i + 1) * 128, :]), 16)
                    x_done[i] = sx.n
                    sx.inc(sync.dma_start(out=odrow[i][:],
                                          in_=d["od"][i * 128:(i + 1) * 128, :]), 16)
                    od_done[i] = sx.n

            add_ve = {-2: 0, -1: 0}
            reuse_ve = {-2: 0, -1: 0}

            ofeat_ac = {}
            tr_done = {}
            for i in range(NTILE):
                par = i % 2

                otr_pe0 = pe.n
                otr_ac0 = ac.n

                @blk.tensor
                def _(t):
                    t.wait_ge(sx.h, od_done[i])
                    if i >= 1:
                        t.wait_ge(ac.h, tr_done[i - 1])
                    for kci in range(8):
                        if kci >= 2:
                            t.wait_ge(ac.h, otr_ac0 + kci - 1)
                        pe.inc(t.transpose(pstr[kci % 2][:],
                                           odrow[par][:, kci * 128:(kci + 1) * 128],
                                           sb["identb"][:]), 1)
                otr_pe = pe.n

                @blk.scalar
                def _(a):
                    if i >= 2:
                        a.wait_ge(pe.h, omm_pe[i - 2])
                    for kci in range(8):
                        a.wait_ge(pe.h, otr_pe0 + kci + 1)
                        ac.inc(a.copy(out=ofeat[par][:, kci * 128:(kci + 1) * 128],
                                      in_=pstr[kci % 2][:]), 1)
                ofeat_ac[i] = ac.n

                @blk.tensor
                def _(t):
                    if i == 0:
                        t.wait_ge(sw.h, wo_ready)
                        omm_pe = {}
                    t.wait_ge(ac.h, ofeat_ac[i])
                    t.wait_ge(ve.h, add_ve[i - 2])
                    for half in range(2):
                        for kci in range(8):
                            ins = t.matmul(
                                psw[par * 2 + half][:],
                                ofeat[par][:, kci * 128:(kci + 1) * 128],
                                wob[:, kci * D + half * 512: kci * D + (half + 1) * 512],
                                start=(kci == 0), stop=(kci == 7))
                        pe.inc(ins, 1)
                omm_pe = omm_pe if i > 0 else {}
                omm_pe[i] = pe.n
                mm_pe = pe.n

                @blk.vector
                def _(v):
                    v.wait_ge(pe.h, mm_pe)
                    v.wait_ge(self.sem["sx"].h, x_done[i])
                    v.wait_ge(ve.h, reuse_ve[i - 2])
                    for half in range(2):
                        ins = v.tensor_tensor(
                            out=ytile[par][:, half * 512:(half + 1) * 512],
                            in0=psw[par * 2 + half][:],
                            in1=xrow[par][:, half * 512:(half + 1) * 512],
                            op=ALU.add)
                    ve.inc(ins, 1)
                add_ve[i] = ve.n

                @blk.vector
                def _(v):
                    v.wait_ge(ve.h, add_ve[i])
                    ve.inc(v.tensor_reduce(stat[:, 0:1], ytile[par][:], AX.X, ALU.add), 1)
                    v.wait_ge(ve.h, ve.n)
                    ve.inc(v.tensor_scalar(stat[:, 1:2], stat[:, 0:1], -1.0 / D, None, ALU.mult), 1)
                    v.wait_ge(ve.h, ve.n)
                    ve.inc(v.tensor_scalar(ytile[par][:], ytile[par][:], stat[:, 1:2], None, ALU.add), 1)
                    v.wait_ge(ve.h, ve.n)
                    ve.inc(v.tensor_tensor(out=sqs[par][:], in0=ytile[par][:],
                                           in1=ytile[par][:], op=ALU.mult), 1)
                    v.wait_ge(ve.h, ve.n)
                    ve.inc(v.tensor_reduce(stat[:, 3:4], sqs[par][:], AX.X, ALU.add), 1)
                    v.wait_ge(ve.h, ve.n)
                    ve.inc(v.tensor_scalar(stat[:, 6:7], stat[:, 3:4], 1.0 / D, 1e-5,
                                           ALU.mult, ALU.add), 1)
                sq_ve = ve.n

                @blk.scalar
                def _(a):
                    a.wait_ge(ve.h, sq_ve)
                    ac.inc(a.sqrt(out=stat[:, 4:5], in_=stat[:, 6:7]), 1)
                rstd_ac = ac.n

                @blk.vector
                def _(v):
                    v.wait_ge(ac.h, rstd_ac)
                    ve.inc(v.reciprocal(stat[:, 5:6], stat[:, 4:5]), 1)
                    v.wait_ge(ve.h, ve.n)
                    ve.inc(v.tensor_scalar(ytile[par][:], ytile[par][:], stat[:, 5:6],
                                           None, ALU.mult), 1)
                    v.wait_ge(ve.h, ve.n)
                    ve.inc(v.tensor_copy(out=ybtile[par][:], in_=ytile[par][:]), 1)
                ln_ve = ve.n

                @blk.sync
                def _(sync):
                    sync.wait_ge(ve.h, ln_ve)
                    sx.inc(sync.dma_start(out=d["y1f"][i * 128:(i + 1) * 128, :],
                                          in_=ytile[par][:]), 16)
                    sx.inc(sync.dma_start(out=d["y1b"][i * 128:(i + 1) * 128, :],
                                          in_=ybtile[par][:]), 16)
                    if dbg_y1 is not None:
                        sx.inc(sync.dma_start(out=dbg_y1[i * 128:(i + 1) * 128, :],
                                              in_=ytile[par][:]), 16)
                    if i + 2 < NTILE:
                        sx.inc(sync.dma_start(out=xrow[par][:],
                                              in_=d["x"][(i + 2) * 128:(i + 3) * 128, :]), 16)
                        x_done[i + 2] = sx.n
                        sync.wait_ge(pe.h, otr_pe)
                        sx.inc(sync.dma_start(out=odrow[par][:],
                                              in_=d["od"][(i + 2) * 128:(i + 3) * 128, :]), 16)
                        od_done[i + 2] = sx.n
                st_sx = sx.n

                trc_ac0 = ac.n
                trp0 = pe.n

                @blk.tensor
                def _(t):
                    t.wait_ge(ve.h, ln_ve)
                    for kci in range(8):
                        if kci >= 2:
                            t.wait_ge(ac.h, trc_ac0 + kci - 1)
                        pe.inc(t.transpose(pstr[kci % 2][:],
                                           ybtile[par][:, kci * 128:(kci + 1) * 128],
                                           sb["identb"][:]), 1)

                @blk.scalar
                def _(a):
                    for kci in range(8):
                        a.wait_ge(pe.h, trp0 + kci + 1)
                        ac.inc(a.copy(
                            out=sb["y1fm"][:, kci * NTOK + i * 128: kci * NTOK + (i + 1) * 128],
                            in_=pstr[kci % 2][:]), 1)
                tr_ac = ac.n
                tr_done[i] = tr_ac

                @blk.vector
                def _(v):
                    v.wait_ge(self.sem["sx"].h, st_sx)
                    v.wait_ge(ac.h, tr_ac)
                    ve.inc(v.engine_nop(), 1)
                reuse_ve[i] = ve.n

            self.y1_sx = self.sem["sx"].n
            self.barrier()

    def vsync(self, v, ve):
        v.wait_ge(ve.h, ve.n)

    def routing(self):
        nc, sb, d = self.nc, self.sb, self.dram
        blk = self.block
        sx, sg, pe, ac, ve, sw, po = (self.sem[k] for k in
                                      ["sx", "sg", "pe", "ac", "ve", "sw", "po"])
        y1fm = sb["y1fm"]

        self.shared_ctx = ExitStack()
        self.sw_sb = {
            "swgb": self.sbuf(self.shared_ctx, "swgb", [128, 8 * D], BF16),
            "swub": self.sbuf(self.shared_ctx, "swub", [128, 8 * D], BF16),
            "swdb": self.sbuf(self.shared_ctx, "swdb", [128, 8 * D], BF16),
            "gsh": self.sbuf(self.shared_ctx, "gsh", [128, 4096], BF16),
            "hsh": self.sbuf(self.shared_ctx, "hsh", [128, 4096], BF16),
            "osh0": self.sbuf(self.shared_ctx, "osh", [128, D], BF16),
            "osh1": self.sbuf(self.shared_ctx, "osh", [128, D], BF16),
        }
        self.routing_ctx = ExitStack()
        ctx = self.routing_ctx
        ctok_all = self.sbuf(ctx, "ctok_all", [128, NTILE * 64], F32)

        wrt = self.sbuf(ctx, "wrt", [128, 512], BF16)
        ebc = self.sbuf(ctx, "ebc", [128, 64], F32)
        mT = self.sbuf(ctx, "mT", [64, NTOK], BF16)
        cX = self.sbuf(ctx, "cX", [64, NTOK], F32)
        msel_all = self.sbuf(ctx, "msel_all", [128, NTILE * 64], F32)
        coef_all = self.sbuf(ctx, "coef_all", [128, NTILE * 64], F32)
        sco = self.sbuf(ctx, "sco", [128, 64], F32)
        sfc = self.sbuf(ctx, "sfc", [128, 64], F32)
        g4 = self.sbuf(ctx, "g4", [128, 32], F32)
        g2 = self.sbuf(ctx, "g2", [128, 16], F32)
        m1 = self.sbuf(ctx, "m1", [128, 8], F32)
        m2 = self.sbuf(ctx, "m2", [128, 8], F32)
        w2 = self.sbuf(ctx, "w2", [128, 64], F32)
        gsum = self.sbuf(ctx, "gsum", [128, 8], F32)
        srt = self.sbuf(ctx, "srt", [128, 8], F32)
        rep8 = self.sbuf(ctx, "rep8", [128, 8], F32)
        g2r = self.sbuf(ctx, "g2r", [128, 8], F32)
        gsel = self.sbuf(ctx, "gsel", [128, 8], F32)
        msk = self.sbuf(ctx, "msk", [128, 64], F32)
        m8 = self.sbuf(ctx, "m8", [128, 8], F32)
        mrm = self.sbuf(ctx, "mrm", [128, 64], F32)
        wsel = self.sbuf(ctx, "wsel", [128, 64], F32)
        stat = self.sbuf(ctx, "rstat", [128, 8], F32)
        iota9 = self.sbuf(ctx, "iota9", [128, 1], F32)
        iota18 = self.sbuf(ctx, "iota18", [128, 8], F32)
        jbig = self.sbuf(ctx, "jbig", [128, 8], F32)
        js16 = self.sbuf(ctx, "js16", [128, 8], F32)
        iotat = self.sbuf(ctx, "iotat", [64, 128], F32)
        bk = self.sbuf(ctx, "bk", [128, 512], F32)
        bbc = self.sbuf(ctx, "bbc", [128, 64], F32)
        sfull = [self.sbuf(ctx, "sfull", [128, 64], F32) for _ in range(2)]
        incl = self.sbuf(ctx, "incl", [128, 64], F32)
        destf = self.sbuf(ctx, "destf", [128, 64], F32)
        jm = self.sbuf(ctx, "jm", [128, 64], F32)
        jm4 = self.sbuf(ctx, "jm4", [128, 64], F32)
        keyb = self.sbuf(ctx, "keyb", [128, 192], F32)
        k8 = self.sbuf(ctx, "k8", [128, 24], F32)
        stage = [self.sbuf(ctx, "stage", [128, 24], F32) for _ in range(2)]
        idxu = [self.sbuf(ctx, "idxu", [128, 8], U32) for _ in range(2)]

        dbg_coef = self.dbg("coef", [128, NTILE * 64], F32)
        dbg_msel = self.dbg("msel", [128, NTILE * 64], F32)
        dbg_cdest = self.dbg("cdest", [NSTREAM + 1, 3], F32)
        dbg_etab = self.dbg("etab", [1, 128], I32)
        dbg_mt = self.dbg("mT", [64, NTOK], F32)

        with ExitStack() as pctx:
            rws = self.sbuf(pctx, "rws", [64, D], F32)
            rwb = self.sbuf(pctx, "rwb", [64, D], BF16)
            ebr = self.sbuf(pctx, "ebr", [1, 64], F32)
            psR = [self.psum(pctx, "psR", [128, 64], F32) for _ in range(2)]
            pstr = [self.psum(pctx, "pstrR", [128, 128], F32) for _ in range(2)]
            psmix = self.psum(pctx, "psmix", [128, 128], F32)
            pstrb = [self.psum(pctx, "pstrb", [128, 64], BF16) for _ in range(2)]

            @blk.sync
            def _(sync):
                sx.inc(sync.dma_start(out=rws[:], in_=d["rw"][:, :]), 16)
                sx.inc(sync.dma_start(out=ebr[:], in_=d["ebias"][:, :]), 16)
            rw_sx = sx.n

            @blk.gpsimd
            def _(g):
                for wname, wt in [("swg", self.sw_sb["swgb"]), ("swu", self.sw_sb["swub"]),
                                  ("swd", self.sw_sb["swdb"])]:
                    for kc in range(8):
                        sw.inc(g.dma_start(
                            out=wt[:, kc * D:(kc + 1) * D],
                            in_=d[wname][kc * 128:(kc + 1) * 128, :]), 16)
                g.iota(iota9[:], pattern=[[0, 1]], base=0, channel_multiplier=9,
                       allow_small_or_imprecise_dtypes=True)
                g.iota(iota18[:], pattern=[[0, 8]], base=0, channel_multiplier=1,
                       allow_small_or_imprecise_dtypes=True)
                g.iota(jbig[:], pattern=[[-1, 8]], base=8,
                       channel_multiplier=0, allow_small_or_imprecise_dtypes=True)
                g.iota(js16[:], pattern=[[-16, 8]], base=128, channel_multiplier=0,
                       allow_small_or_imprecise_dtypes=True)
                po.inc(g.iota(iotat[:], pattern=[[1, 128]], base=0,
                              channel_multiplier=0,
                              allow_small_or_imprecise_dtypes=True), 1)
            iota_po = po.n
            self.sw_ready = sw.n

            @blk.vector
            def _(v):
                v.wait_ge(sx.h, rw_sx)
                v.wait_ge(po.h, iota_po)
                v.memset(rep8[:, 4:8], -1e30)
                ve.inc(v.tensor_scalar(jbig[:], jbig[:], 1048576.0, None, ALU.mult), 1)
                self.vsync(v, ve)
                ve.inc(v.tensor_copy(out=rwb[:], in_=rws[:]), 1)
            rwb_ve = ve.n

            rtr_ac0 = ac.n
            rtr_pe0 = pe.n

            @blk.tensor
            def _(t):
                t.wait_ge(ve.h, rwb_ve)
                for kc in range(8):
                    if kc >= 2:
                        t.wait_ge(ac.h, rtr_ac0 + kc - 1)
                    pe.inc(t.transpose(pstrb[kc % 2][:],
                                       rwb[:, kc * 128:(kc + 1) * 128],
                                       sb["identb"][0:64, 0:64]), 1)
                pe.inc(t.matmul(psmix[:, 0:64], sb["ones1p"][:], ebr[:],
                                start=True, stop=True), 1)

            @blk.scalar
            def _(a):
                for kc in range(8):
                    a.wait_ge(pe.h, rtr_pe0 + kc + 1)
                    ac.inc(a.copy(out=wrt[:, kc * 64:(kc + 1) * 64],
                                  in_=pstrb[kc % 2][:]), 1)
                a.wait_ge(pe.h, rtr_pe0 + 9)
                ac.inc(a.copy(out=ebc[:], in_=psmix[:, 0:64]), 1)
            setup_ac = ac.n

            sig_ac = {}
            scoused_ve = {}
            mT_ve = {}
            for i in range(NTILE):
                @blk.tensor
                def _(t):
                    t.wait_ge(ac.h, setup_ac)
                    if i >= 2:
                        t.wait_ge(ac.h, sig_ac[i - 2])
                    for kc in range(8):
                        ins = t.matmul(psR[i % 2][:],
                                       y1fm[:, kc * NTOK + i * 128: kc * NTOK + (i + 1) * 128],
                                       wrt[:, kc * 64:(kc + 1) * 64],
                                       start=(kc == 0), stop=(kc == 7))
                    pe.inc(ins, 1)
                r_pe = pe.n

                @blk.scalar
                def _(a):
                    a.wait_ge(pe.h, r_pe)
                    if i > 0:
                        a.wait_ge(ve.h, scoused_ve[i - 1])
                    ac.inc(a.activation(out=sco[:], in_=psR[i % 2][:], func=AF.Sigmoid), 1)
                sig_ac[i] = ac.n

                @blk.vector
                def _(v):
                    v.wait_ge(ac.h, sig_ac[i])
                    ve.inc(v.tensor_tensor(out=sfc[:], in0=sco[:], in1=ebc[:], op=ALU.add), 1)
                    self.vsync(v, ve)
                    v3 = sfc[:].rearrange("p (g e) -> p g e", g=8)
                    g43 = g4[:].rearrange("p (g e) -> p g e", g=8)
                    g23 = g2[:].rearrange("p (g e) -> p g e", g=8)
                    ve.inc(v.tensor_max(g43, v3[:, :, 0:4], v3[:, :, 4:8]), 1)
                    self.vsync(v, ve)
                    ve.inc(v.tensor_max(g23, g43[:, :, 0:2], g43[:, :, 2:4]), 1)
                    self.vsync(v, ve)
                    ve.inc(v.tensor_max(m1[:].rearrange("p (g e) -> p g e", g=8),
                                        g23[:, :, 0:1], g23[:, :, 1:2]), 1)
                    self.vsync(v, ve)
                    ve.inc(v.match_replace(out=w2[:], in_to_replace=m1[:],
                                           in_values=sfc[:], imm_value=-1e30), 1)
                    self.vsync(v, ve)
                    w23 = w2[:].rearrange("p (g e) -> p g e", g=8)
                    ve.inc(v.tensor_max(g43, w23[:, :, 0:4], w23[:, :, 4:8]), 1)
                    self.vsync(v, ve)
                    ve.inc(v.tensor_max(g23, g43[:, :, 0:2], g43[:, :, 2:4]), 1)
                    self.vsync(v, ve)
                    ve.inc(v.tensor_max(m2[:].rearrange("p (g e) -> p g e", g=8),
                                        g23[:, :, 0:1], g23[:, :, 1:2]), 1)
                    self.vsync(v, ve)
                    ve.inc(v.tensor_tensor(out=gsum[:], in0=m1[:], in1=m2[:], op=ALU.add), 1)
                    self.vsync(v, ve)
                    ve.inc(v.max(srt[:], gsum[:]), 1)
                    self.vsync(v, ve)
                    ve.inc(v.tensor_copy(out=rep8[:, 0:4], in_=srt[:, 0:4]), 1)
                    self.vsync(v, ve)
                    ve.inc(v.match_replace(out=g2r[:], in_to_replace=rep8[:],
                                           in_values=gsum[:], imm_value=-1e30), 1)
                    self.vsync(v, ve)
                    ve.inc(v.tensor_scalar(gsel[:], g2r[:], -1e29, None, ALU.is_lt), 1)
                    self.vsync(v, ve)
                    ve.inc(v.tensor_tensor(
                        out=msk[:].rearrange("p (g e) -> p g e", g=8),
                        in0=v3, in1=gsel[:, :, None].to_broadcast([128, 8, 8]),
                        op=ALU.mult), 1)
                    self.vsync(v, ve)
                    ve.inc(v.max(m8[:], msk[:]), 1)
                    self.vsync(v, ve)
                    ve.inc(v.match_replace(out=mrm[:], in_to_replace=m8[:],
                                           in_values=msk[:], imm_value=-1e30), 1)
                    self.vsync(v, ve)
                    ve.inc(v.tensor_scalar(msel_all[:, i * 64:(i + 1) * 64], mrm[:],
                                           -1e29, None, ALU.is_lt), 1)
                    self.vsync(v, ve)
                    ve.inc(v.tensor_tensor(out=wsel[:], in0=sco[:],
                                           in1=msel_all[:, i * 64:(i + 1) * 64],
                                           op=ALU.mult), 1)
                    scoused_ve[i] = ve.n
                    self.vsync(v, ve)
                    ve.inc(v.tensor_reduce(stat[:, 0:1], wsel[:], AX.X, ALU.add), 1)
                    self.vsync(v, ve)
                    ve.inc(v.tensor_scalar(stat[:, 1:2], stat[:, 0:1], 1e-20, None, ALU.add), 1)
                    self.vsync(v, ve)
                    ve.inc(v.reciprocal(stat[:, 2:3], stat[:, 1:2]), 1)
                    self.vsync(v, ve)
                    ve.inc(v.tensor_scalar(coef_all[:, i * 64:(i + 1) * 64], wsel[:],
                                           stat[:, 2:3], 2.5, ALU.mult, ALU.mult), 1)
                msel_ve = ve.n

                @blk.tensor
                def _(t):
                    t.wait_ge(ve.h, msel_ve)
                    if i >= 2:
                        t.wait_ge(ve.h, mT_ve[i - 2])
                    pe.inc(t.transpose(pstr[i % 2][0:64, :],
                                       msel_all[:, i * 64:(i + 1) * 64],
                                       sb["identf"][:]), 1)
                mseltp = pe.n

                @blk.vector
                def _(v):
                    v.wait_ge(pe.h, mseltp)
                    ve.inc(v.tensor_copy(out=mT[:, i * 128:(i + 1) * 128],
                                         in_=pstr[i % 2][0:64, :]), 1)
                mT_ve[i] = ve.n

            ones4k = self.sbuf(pctx, "ones4k", [64, NTOK], BF16)
            @blk.vector
            def _(v):
                ve.inc(v.memset(ones4k[:], 1.0), 1)
                self.vsync(v, ve)
                ve.inc(v.tensor_tensor_scan(cX[:], ones4k[:], mT[:], 0.0,
                                            ALU.mult, ALU.add), 1)
                self.vsync(v, ve)
                ve.inc(v.tensor_copy(out=bk[0:64, 0:1], in_=cX[0:64, NTOK - 1:NTOK]), 1)
                self.vsync(v, ve)
                ve.inc(v.tensor_tensor(out=cX[:], in0=cX[:], in1=mT[:], op=ALU.subtract), 1)
                self.vsync(v, ve)
                ve.inc(v.tensor_scalar(bk[0:64, 1:2], bk[0:64, 0:1], 511.0, None, ALU.add), 1)
                self.vsync(v, ve)
                ve.inc(v.tensor_scalar(bk[0:64, 2:3], bk[0:64, 1:2],
                                       1.0 / 512.0, -0.4990234375, ALU.mult, ALU.add), 1)
                self.vsync(v, ve)
                ve.inc(v.tensor_scalar(bk[0:64, 3:4], bk[0:64, 2:3], 8388608.0, None, ALU.add), 1)
                self.vsync(v, ve)
                ve.inc(v.tensor_scalar(bk[0:64, 4:5], bk[0:64, 3:4], -8388608.0, None, ALU.add), 1)
            tiles_ve = ve.n

            ctt_pe0 = pe.n
            ctt_ve0 = ve.n

            @blk.tensor
            def _(t):
                t.wait_ge(ve.h, tiles_ve)
                for i in range(NTILE):
                    if i >= 2:
                        t.wait_ge(ve.h, ctt_ve0 + i - 1)
                    pe.inc(t.transpose(pstr[i % 2][:, 0:64], cX[:, i * 128:(i + 1) * 128],
                                       sb["identf"][0:64, 0:64]), 1)

            @blk.vector
            def _(v):
                for i in range(NTILE):
                    v.wait_ge(pe.h, ctt_pe0 + i + 1)
                    ve.inc(v.tensor_copy(out=ctok_all[:, i * 64:(i + 1) * 64],
                                         in_=pstr[i % 2][:, 0:64]), 1)
            ctok_ve = ve.n

            @blk.tensor
            def _(t):
                t.wait_ge(ve.h, tiles_ve)
                pe.inc(t.matmul(psmix[0:1, 0:64], bk[0:64, 4:5],
                                sb["identf"][0:64, 0:64], start=True, stop=True), 1)
            trow_pe = pe.n

            @blk.vector
            def _(v):
                v.wait_ge(pe.h, trow_pe)
                ve.inc(v.tensor_copy(out=bk[0:1, 16:80], in_=psmix[0:1, 0:64]), 1)
                self.vsync(v, ve)
                ve.inc(v.tensor_tensor_scan(bk[0:1, 80:144], sb["ones64"][0:1, 0:64],
                                            bk[0:1, 16:80], 0.0, ALU.mult, ALU.add), 1)
                self.vsync(v, ve)
                ve.inc(v.tensor_tensor(out=bk[0:1, 144:208], in0=bk[0:1, 80:144],
                                       in1=bk[0:1, 16:80], op=ALU.subtract), 1)
                self.vsync(v, ve)
                ve.inc(v.tensor_scalar(bk[0:1, 208:272], bk[0:1, 144:208],
                                       512.0, None, ALU.mult), 1)
            brow_ve = ve.n

            @blk.tensor
            def _(t):
                t.wait_ge(ve.h, brow_ve)
                pe.inc(t.matmul(psmix[0:64, 64:65], bk[0:1, 144:208], sb["ones11"][:],
                                start=True, stop=True), 1)
            scol_pe = pe.n

            @blk.vector
            def _(v):
                v.wait_ge(pe.h, scol_pe)
                v.wait_ge(po.h, iota_po)
                ve.inc(v.tensor_copy(out=bk[0:64, 5:6], in_=psmix[0:64, 64:65]), 1)
                self.vsync(v, ve)
                ve.inc(v.tensor_scalar(bk[0:64, 384:512], iotat[:], bk[0:64, 5:6],
                                       None, ALU.is_ge), 1)
            A_ve = ve.n

            @blk.tensor
            def _(t):
                t.wait_ge(ve.h, A_ve)
                pe.inc(t.matmul(pstr[0][:, 0:1], bk[0:64, 384:512], sb["onesc"][:],
                                start=True, stop=True), 1)
            ecol_pe = pe.n

            @blk.vector
            def _(v):
                v.wait_ge(pe.h, ecol_pe)
                ve.inc(v.tensor_copy(out=bk[:, 6:7], in_=pstr[0][:, 0:1]), 1)
            ecol_ve = ve.n

            @blk.tensor
            def _(t):
                t.wait_ge(ve.h, ecol_ve)
                pe.inc(t.matmul(psmix[0:1, 0:128], bk[:, 6:7], sb["identf"][:],
                                start=True, stop=True), 1)
            erow_pe = pe.n

            @blk.vector
            def _(v):
                v.wait_ge(pe.h, erow_pe)
                ve.inc(v.tensor_scalar(bk[0:1, 272:400], psmix[0:1, 0:128],
                                       -1.0, None, ALU.add), 1)
                self.vsync(v, ve)
                ve.inc(v.tensor_copy(out=self.etab[:], in_=bk[0:1, 272:400]), 1)
                ve.inc(v.tensor_copy(out=self.etabf[:], in_=bk[0:1, 272:400]), 1)
                self.vsync(v, ve)
                ve.inc(v.tensor_scalar(bk[0:1, 384:512], iotat[0:1, 0:128],
                                       bk[0:1, 143:144], None, ALU.is_lt), 1)
                ve.inc(v.memset(self.zi32[:], 0.0), 1)
                self.vsync(v, ve)
                ve.inc(v.tensor_copy(out=self.alive[:], in_=bk[0:1, 384:512]), 1)
            etab_ve = ve.n
            self.alive_ve = etab_ve

            @blk.tensor
            def _(t):
                t.wait_ge(ve.h, etab_ve)
                pe.inc(t.matmul(psmix[:, 0:64], sb["ones1p"][:],
                                bk[0:1, 208:272], start=True, stop=True), 1)
            bbc_pe = pe.n

            @blk.vector
            def _(v):
                v.wait_ge(pe.h, bbc_pe)
                ve.inc(v.tensor_copy(out=bbc[:], in_=psmix[:, 0:64]), 1)
            bbc_ve = ve.n

            fcol = bk[0:64, 8:9]
            rmat = cX[0:64, 0:128]
            frow = keyb[0:1, 0:128]
            psfa = psmix[0:64, 0:128]
            psfb = pstr[0][0:1, 0:128]

            @blk.vector
            def _(v):
                v.wait_ge(ve.h, bbc_ve)
                ve.inc(v.tensor_scalar(fcol, bk[0:64, 4:5],
                                       -512.0, 512.0, ALU.mult, ALU.add), 1)
                self.vsync(v, ve)
                ve.inc(v.tensor_tensor(out=fcol, in0=bk[0:64, 0:1],
                                       in1=fcol, op=ALU.add), 1)
                self.vsync(v, ve)
                ve.inc(v.tensor_scalar(fcol, fcol, 128.5, None, ALU.is_le), 1)
            esm_ve = ve.n

            @blk.tensor
            def _(t):
                t.wait_ge(ve.h, esm_ve)
                pe.inc(t.matmul(psfa[:, :], sb["identf"][0:64, 1:65],
                                bk[0:64, 384:512], start=True, stop=True), 1)
            aup_pe = pe.n

            @blk.vector
            def _(v):
                v.wait_ge(pe.h, aup_pe)
                ve.inc(v.tensor_tensor(out=rmat, in0=bk[0:64, 384:512],
                                       in1=psfa[:, :], op=ALU.subtract), 1)
            rmat_ve = ve.n

            @blk.tensor
            def _(t):
                t.wait_ge(ve.h, rmat_ve)
                pe.inc(t.matmul(psfb[:, :], fcol, rmat,
                                start=True, stop=True), 1)
            gfast_pe = pe.n

            @blk.vector
            def _(v):
                v.wait_ge(pe.h, gfast_pe)
                ve.inc(v.tensor_tensor(out=frow[0:1, 0:127], in0=bk[0:1, 273:400],
                                       in1=bk[0:1, 272:399], op=ALU.not_equal), 1)
                ve.inc(v.memset(frow[0:1, 127:128], 1.0), 1)
                self.vsync(v, ve)
                ve.inc(v.tensor_tensor(out=frow, in0=psfb[0:1, :],
                                       in1=frow, op=ALU.mult), 1)
                self.vsync(v, ve)
                ve.inc(v.tensor_tensor(out=frow, in0=frow,
                                       in1=bk[0:1, 384:512], op=ALU.mult), 1)
                self.vsync(v, ve)
                ve.inc(v.tensor_copy(out=self.fastf[:], in_=frow), 1)
            self.alive_ve = ve.n

            if dbg_etab is not None:
                @blk.sync
                def _(sync):
                    sync.wait_ge(ve.h, etab_ve)
                    sx.inc(sync.dma_start(out=dbg_etab[:, :], in_=self.etab[:]), 16)
            if dbg_mt is not None:
                @blk.sync
                def _(sync):
                    sync.wait_ge(ve.h, bbc_ve)
                    sx.inc(sync.dma_start(out=dbg_mt[:, :], in_=cX[:]), 16)

            scat_sg = {-3: sg.n, -2: sg.n, -1: sg.n}
            gqB = self.sem["gq"]
            rgate = {-2: 0, -1: 0}
            sfB_ve = {}
            for i in range(NTILE):
                par = i % 2

                @blk.vector
                def _(v):
                    v.wait_ge(ve.h, bbc_ve)
                    if rgate[i - 2] > 0:
                        v.wait_ge(gqB.h, rgate[i - 2])
                    ve.inc(v.tensor_tensor(out=sfull[par][:],
                                           in0=ctok_all[:, i * 64:(i + 1) * 64],
                                           in1=bbc[:], op=ALU.add), 1)
                    self.vsync(v, ve)
                    msl = msel_all[:, i * 64:(i + 1) * 64]
                    ve.inc(v.tensor_tensor_scan(incl[:], sb["ones64"][:], msl, 0.0,
                                                ALU.mult, ALU.add), 1)
                    self.vsync(v, ve)
                    ve.inc(v.tensor_scalar(destf[:], incl[:], iota9[:, 0:1],
                                           float(i * 128 * 9 - 1), ALU.add, ALU.add), 1)
                    self.vsync(v, ve)
                    ve.inc(v.tensor_scalar(jm[:], incl[:], -1.0, 9.0, ALU.mult, ALU.add), 1)
                    self.vsync(v, ve)
                    ve.inc(v.tensor_tensor(out=jm[:], in0=jm[:], in1=msl, op=ALU.mult), 1)
                    self.vsync(v, ve)
                    ve.inc(v.scalar_tensor_tensor(keyb[:, 0:64], jm[:], 1048576.0,
                                                  sfull[par][:], ALU.mult, ALU.add), 1)
                    ve.inc(v.scalar_tensor_tensor(keyb[:, 64:128], jm[:], 1048576.0,
                                                  destf[:], ALU.mult, ALU.add), 1)
                    ve.inc(v.tensor_scalar(jm4[:], jm[:], 4.0, None, ALU.mult), 1)
                    self.vsync(v, ve)
                    ve.inc(v.scalar_tensor_tensor(keyb[:, 128:192],
                                                  coef_all[:, i * 64:(i + 1) * 64], 0.25,
                                                  jm4[:], ALU.mult, ALU.add), 1)
                    self.vsync(v, ve)
                    ve.inc(v.max(k8[:, 0:8], keyb[:, 0:64]), 1)
                    ve.inc(v.max(k8[:, 8:16], keyb[:, 64:128]), 1)
                    ve.inc(v.max(k8[:, 16:24], keyb[:, 128:192]), 1)
                    self.vsync(v, ve)
                    st3 = stage[par][:].rearrange("p (j t) -> p j t", t=3)
                    ve.inc(v.tensor_scalar(st3[:, :, 0:1].rearrange("p j t -> p (j t)"),
                                           iota18[:], float(i * 128), None, ALU.add), 1)
                    ve.inc(v.tensor_tensor(out=st3[:, :, 1:2].rearrange("p j t -> p (j t)"),
                                           in0=k8[:, 8:16], in1=jbig[:], op=ALU.subtract), 1)
                    ve.inc(v.scalar_tensor_tensor(st3[:, :, 2:3].rearrange("p j t -> p (j t)"),
                                                  k8[:, 16:24], 4.0, js16[:],
                                                  ALU.mult, ALU.subtract), 1)
                    self.vsync(v, ve)
                    ve.inc(v.tensor_tensor(out=k8[:, 0:8], in0=k8[:, 0:8],
                                           in1=jbig[:], op=ALU.subtract), 1)
                    self.vsync(v, ve)
                    ve.inc(v.tensor_scalar(k8[:, 0:8], k8[:, 0:8],
                                           0.0, float(NSTREAM), ALU.max, ALU.min), 1)
                    self.vsync(v, ve)
                    ve.inc(v.tensor_copy(out=idxu[par][:], in_=k8[:, 0:8]), 1)
                sfB_ve[i] = ve.n

                if not self.no_scatter:
                    @blk.gpsimd
                    def _(g):
                        g.wait_ge(ve.h, sfB_ve[i])
                        for j in range(8):
                            sg.inc(g.indirect_dma_start(
                                out=d["cdest"][:, :], out_offset=IOA(idxu[par][:, j:j + 1], 0),
                                in_=stage[par][:, j * 3:(j + 1) * 3], in_offset=None), 16)
                        g.wait_ge(sg.h, sg.n)
                        g.sem_inc(gqB.h, 1)
                    gqB.n += 1
                rgate[i] = gqB.n
                scat_sg[i] = sg.n

            if dbg_coef is not None:
                self.debug.add("msel")
                dbg_msel = self.dbg("msel", [128, NTILE * 64], F32)
                @blk.sync
                def _(sync):
                    sync.wait_ge(ve.h, sfB_ve[NTILE - 1])
                    sx.inc(sync.dma_start(out=dbg_coef[:, :], in_=coef_all[:]), 16)
                    sx.inc(sync.dma_start(out=dbg_msel[:, :], in_=msel_all[:]), 16)
            if dbg_cdest is not None:
                @blk.sync
                def _(sync):
                    sync.wait_ge(sg.h, scat_sg[NTILE - 1])
                    cdf = d["cdest"].rearrange("a b -> (a b)")
                    dbf = dbg_cdest.rearrange("a b -> (a b)")
                    sx.inc(sync.dma_start(
                        out=dbf[0:NSTREAM * 3].rearrange("(p f) -> p f", p=128),
                        in_=cdf[0:NSTREAM * 3].rearrange("(p f) -> p f", p=128)), 16)
                    sx.inc(sync.dma_start(out=dbf[NSTREAM * 3:NSTREAM * 3 + 3],
                                          in_=cdf[NSTREAM * 3:NSTREAM * 3 + 3]), 16)

        self.shared_gate = {"ve": max(bbc_ve, ctok_ve), "ac": sig_ac[NTILE - 1],
                            "sw": self.sw_ready}

    def shared_ffn(self):
        nc, sb, d = self.nc, self.sb, self.dram
        blk = self.block
        sx, sg, pe, ac, ve, sw = (self.sem[k] for k in ["sx", "sg", "pe", "ac", "ve", "sw"])
        y1fm = sb["y1fm"]
        swgb, swub, swdb = self.sw_sb["swgb"], self.sw_sb["swub"], self.sw_sb["swdb"]
        gsh, hsh = self.sw_sb["gsh"], self.sw_sb["hsh"]
        osh = [self.sw_sb["osh0"], self.sw_sb["osh1"]]

        with ExitStack() as ctx:
            psF = [self.psum(ctx, "psF", [128, 512], F32) for _ in range(4)]

            silu_ac = {}
            h_ve = {}
            o_ve = {}
            o_sx = {}
            cqS = self.sem["cq"]
            shgate = {}

            for tau in range(8):
                for ic in range(8):
                    gslice = gsh[:, ic * 512:(ic + 1) * 512]
                    hslice = hsh[:, ic * 512:(ic + 1) * 512]

                    @blk.tensor
                    def _(t):
                        if tau == 0 and ic == 0:
                            t.wait_ge(sw.h, self.shared_gate["sw"])
                            t.wait_ge(ve.h, self.shared_gate["ve"])
                            t.wait_ge(ac.h, self.shared_gate["ac"])
                        prev = (tau, ic - 2) if ic >= 2 else (tau - 1, ic + 6)
                        if prev in silu_ac:
                            t.wait_ge(ac.h, silu_ac[prev])
                        if prev in h_ve:
                            t.wait_ge(ve.h, h_ve[prev])
                        if ic < 2 and tau > 0:
                            t.wait_ge(ve.h, max(o_ve[(tau - 1, 2)], o_ve[(tau - 1, 3)]))
                        for kc in range(8):
                            ins = t.matmul(psF[2 * (ic % 2)][:],
                                           swgb[:, kc * D + ic * 128: kc * D + (ic + 1) * 128],
                                           y1fm[:, kc * NTOK + tau * 512: kc * NTOK + (tau + 1) * 512],
                                           start=(kc == 0), stop=(kc == 7))
                        pe.inc(ins, 1)
                        for kc in range(8):
                            ins = t.matmul(psF[2 * (ic % 2) + 1][:],
                                           swub[:, kc * D + ic * 128: kc * D + (ic + 1) * 128],
                                           y1fm[:, kc * NTOK + tau * 512: kc * NTOK + (tau + 1) * 512],
                                           start=(kc == 0), stop=(kc == 7))
                        pe.inc(ins, 1)
                    gu_pe = pe.n

                    @blk.scalar
                    def _(a):
                        a.wait_ge(pe.h, gu_pe)
                        ac.inc(a.activation(out=gslice, in_=psF[2 * (ic % 2)][:],
                                            func=AF.Silu), 1)
                    silu_ac[(tau, ic)] = ac.n

                    @blk.vector
                    def _(v):
                        v.wait_ge(ac.h, silu_ac[(tau, ic)])
                        ve.inc(v.tensor_tensor(out=hslice, in0=psF[2 * (ic % 2) + 1][:],
                                               in1=gslice, op=ALU.mult), 1)
                    h_ve[(tau, ic)] = ve.n

                for sub in range(4):
                    par = sub % 2

                    @blk.tensor
                    def _(t):
                        t.wait_ge(ve.h, h_ve[(tau, 7)])
                        prev = (tau, sub - 2) if sub >= 2 else (tau - 1, sub + 2)
                        if prev in o_ve:
                            t.wait_ge(ve.h, o_ve[prev])
                        for mh in range(2):
                            for ic in range(8):
                                ins = t.matmul(
                                    psF[2 * par + mh][:],
                                    hsh[:, ic * 512 + sub * 128: ic * 512 + (sub + 1) * 128],
                                    swdb[:, ic * D + mh * 512: ic * D + (mh + 1) * 512],
                                    start=(ic == 0), stop=(ic == 7))
                            pe.inc(ins, 1)
                    dd_pe = pe.n

                    @blk.vector
                    def _(v):
                        v.wait_ge(pe.h, dd_pe)
                        prev = (tau, sub - 2) if sub >= 2 else (tau - 1, sub + 2)
                        if prev in shgate:
                            v.wait_ge(cqS.h, shgate[prev])
                        for mh in range(2):
                            ins = v.tensor_copy(out=osh[par][:, mh * 512:(mh + 1) * 512],
                                                in_=psF[2 * par + mh][:])
                        ve.inc(ins, 1)
                    o_ve[(tau, sub)] = ve.n

                    @blk.sync
                    def _(sync):
                        sync.wait_ge(ve.h, o_ve[(tau, sub)])
                        r0 = (tau * 512 + sub * 128) * 9 + 8
                        sx.inc(sync.dma_start(
                            out=d["slots"][r0: r0 + 128 * 9: 9, :],
                            in_=osh[par][:]), 16)
                        sync.wait_ge(sx.h, sx.n)
                        sync.sem_inc(cqS.h, 1)
                    cqS.n += 1
                    shgate[(tau, sub)] = cqS.n
                    o_sx[(tau, sub)] = sx.n

        self.barrier()

    def moe(self):
        nc, sb, d = self.nc, self.sb, self.dram
        blk = self.block
        sx, sg, pe, ac, ve, sw = (self.sem[k] for k in ["sx", "sg", "pe", "ac", "ve", "sw"])
        gq = self.sem["gq"]
        cq = self.sem["cq"]
        gg = self.sem["gg"]
        gate = {}
        cdgate = {}
        self.routing_ctx.close()
        self.routing_ctx = None
        self.shared_ctx.close()
        self.y1fm_ctx.close()

        wgv = d["wgc"]
        wuv = d["wuc"]
        wdv = d["wdc"]

        with ExitStack() as ctx:
            wgb = [self.sbuf(ctx, "wgb", [128, 8 * EI], BF16) for _ in range(3)]
            wub = [self.sbuf(ctx, "wub", [128, 8 * EI], BF16) for _ in range(3)]
            wdb = [self.sbuf(ctx, "wdb", [128, 4 * D], BF16) for _ in range(3)]
            cdr4 = [self.sbuf(ctx, "cdr4", [4, 384], F32) for _ in range(4)]
            idxu = [self.sbuf(ctx, "idxuM", [128, 4], U32) for _ in range(4)]
            destu = [self.sbuf(ctx, "destu", [128, 4], U32) for _ in range(4)]
            coefc = [self.sbuf(ctx, "coefc", [128, 4], F32) for _ in range(4)]
            xg = [self.sbuf(ctx, "xg", [128, 4 * D], BF16) for _ in range(2)]
            xgf = [self.sbuf(ctx, "xgf", [128, 8 * 512], BF16) for _ in range(2)]
            gsl = [self.sbuf(ctx, "gsl", [128, 4 * 512], BF16) for _ in range(2)]
            hb = [self.sbuf(ctx, "hb", [128, 4 * 512], BF16) for _ in range(2)]
            ot = [self.sbuf(ctx, "ot", [128, 4 * D], BF16) for _ in range(2)]
            cd_sx = {}
            w_sg = {-3: sg.n, -2: sg.n, -1: sg.n}
            cd_pe = {}
            cd_ve = {}
            gath_sg = {-3: sg.n, -2: sg.n, -1: sg.n}
            tr_pe = {-3: 0, -2: 0, -1: 0}
            xgf_ac = {-3: 0, -2: 0, -1: 0}
            gu_pe = {-3: 0, -2: 0, -1: 0}
            h_ve = {-3: 0, -2: 0, -1: 0}
            d_pe = {-3: 0, -2: 0, -1: 0}
            ev_ve = {-3: 0, -2: 0, -1: 0}
            scat_sg = {-3: sg.n, -2: sg.n, -1: sg.n}

            base_sg = sg.n
            regs_ctx = ExitStack()
            erA = regs_ctx.enter_context(nc.gpsimd.register("erA"))
            erB = regs_ctx.enter_context(nc.gpsimd.register("erB"))
            erGz = regs_ctx.enter_context(nc.gpsimd.register("erGz"))
            erG2 = regs_ctx.enter_context(nc.gpsimd.register("erG2"))
            erT = regs_ctx.enter_context(nc.tensor.register("erT"))
            erTz = regs_ctx.enter_context(nc.tensor.register("erTz"))
            erT2 = regs_ctx.enter_context(nc.tensor.register("erT2"))
            erV = regs_ctx.enter_context(nc.vector.register("erV"))
            erVz = regs_ctx.enter_context(nc.vector.register("erVz"))
            erV2 = regs_ctx.enter_context(nc.vector.register("erV2"))
            erC = regs_ctx.enter_context(nc.scalar.register("erC"))
            erCz = regs_ctx.enter_context(nc.scalar.register("erCz"))
            erC2 = regs_ctx.enter_context(nc.scalar.register("erC2"))
            av = self.alive
            fst = self.fastf
            av_ve = self.alive_ve

            @blk.gpsimd
            def _(g):
                g.wait_ge(ve.h, av_ve)
                g.reg_load(erGz, self.zi32[0:1, 0:1])

            @blk.tensor
            def _(t):
                t.wait_ge(ve.h, av_ve)
                t.reg_load(erTz, self.zi32[0:1, 0:1])

            @blk.vector
            def _(v):
                v.wait_ge(ve.h, av_ve)
                v.reg_load(erVz, self.zi32[0:1, 0:1])

            @blk.scalar
            def _(a):
                a.wait_ge(ve.h, av_ve)
                a.reg_load(erCz, self.zi32[0:1, 0:1])
            iota1c = self.sbuf(ctx, "iota1c", [128, 1], F32)
            widx_all = self.sbuf(ctx, "widx_all", [128, NSLOT], U32)
            widx_f = self.sbuf(ctx, "widx_f", [128, NSLOT], F32)
            widf = self.sbuf(ctx, "widf", [128, 8], F32)
            po = self.sem["po"]
            io = self.sem["io"]

            @blk.gpsimd
            def _(g):
                io.inc(g.iota(iota1c[:], pattern=[[0, 1]], base=0, channel_multiplier=1,
                              allow_small_or_imprecise_dtypes=True), 1)
            iw_io = io.n

            with ExitStack() as wctx:
                pswx = self.psum(wctx, "pswx", [128, NSLOT], F32)

                @blk.tensor
                def _(t):
                    pe.inc(t.matmul(pswx[:], sb["ones1p"][:], self.etabf[:],
                                    start=True, stop=True), 1)
                wx_pe = pe.n

                @blk.vector
                def _(v):
                    v.wait_ge(pe.h, wx_pe)
                    v.wait_ge(io.h, iw_io)
                    ve.inc(v.tensor_scalar(widx_f[:], pswx[:], 128.0, None, ALU.mult), 1)
                    self.vsync(v, ve)
                    ve.inc(v.tensor_scalar(widx_f[:], widx_f[:], iota1c[:, 0:1],
                                           None, ALU.add), 1)
                    self.vsync(v, ve)
                    ve.inc(v.tensor_copy(out=widx_all[:], in_=widx_f[:]), 1)
                widx_ve = ve.n

            pscd = [self.psum(ctx, "pscd", [128, 16], F32) for _ in range(2)]
            pstr = [self.psum(ctx, "pstrM", [128, 128], BF16) for _ in range(2)]
            psF = [self.psum(ctx, "psFM", [128, 512], F32) for _ in range(4)]

            def stage_cdrow(s):
                par4 = s % 4

                @blk.sync
                def _(sync):
                    if s == 0:
                        sync.wait_ge(sg.h, base_sg)
                    if s >= 4:
                        sync.wait_ge(pe.h, cd_pe[s - 4])
                    sx.inc(sync.dma_start(
                        out=cdr4[par4][:, :],
                        in_=d["cdest"][s * 512:(s + 1) * 512, :].rearrange(
                            "(a b) t -> a (b t)", a=4)), 16)
                    sync.wait_ge(sx.h, sx.n)
                    sync.sem_inc(cq.h, 1)
                cq.n += 1
                cdgate[s] = cq.n
                cd_sx[s] = sx.n

            def stage_idx(s):
                par4 = s % 4

                @blk.tensor
                def _(t):
                    t.wait_ge(cq.h, cdgate[s])
                    if s < 2:
                        t.wait_ge(ve.h, widx_ve)
                    if s >= 2:
                        t.wait_ge(ve.h, cd_ve[s - 2])
                    for f in range(3):
                        ins = t.matmul(pscd[s % 2][:, f * 4:(f + 1) * 4],
                                       cdr4[par4][:, f: 384: 3],
                                       sb["identf"][0:4, 0:4],
                                       start=True, stop=True)
                    pe.inc(ins, 1)
                cd_pe[s] = pe.n

                @blk.vector
                def _(v):
                    v.wait_ge(pe.h, cd_pe[s])
                    prot = s - 1 if s % 2 == 0 else s - 2
                    if gate.get(prot, 0) > 0:
                        v.wait_ge(gq.h, gate[prot])
                    ve.inc(v.tensor_scalar(widf[:, 0:4], pscd[s % 2][:, 0:4],
                                           0.0, float(NTOK - 1), ALU.max, ALU.min), 1)
                    self.vsync(v, ve)
                    ve.inc(v.tensor_copy(out=idxu[par4][:], in_=widf[:, 0:4]), 1)
                    ve.inc(v.tensor_scalar(widf[:, 4:8], pscd[s % 2][:, 4:8],
                                           0.0, float(SLROWS - 1), ALU.max, ALU.min), 1)
                    self.vsync(v, ve)
                    ve.inc(v.tensor_copy(out=destu[par4][:], in_=widf[:, 4:8]), 1)
                    ve.inc(v.tensor_copy(out=coefc[par4][:], in_=pscd[s % 2][:, 8:12]), 1)
                cd_ve[s] = ve.n

            def stage_wload(s):
                par3 = s % 3

                @blk.gpsimd
                def _(g):
                    if s == 0:
                        g.wait_ge(sg.h, base_sg)
                        g.wait_ge(ve.h, widx_ve)
                        g.wait_ge(self.sem["po"].h, self.wcache_po)
                    g.wait_ge(pe.h, gu_pe[s - 3])
                    g.wait_ge(pe.h, d_pe[s - 3])

                    def emit_wload(g):
                        sg.inc(g.indirect_dma_start(
                            out=wgb[par3][:, :], out_offset=None,
                            in_=wgv[:, :], in_offset=IOA(widx_all[:, s:s + 1], 0)), 16)
                        sg.inc(g.indirect_dma_start(
                            out=wub[par3][:, :], out_offset=None,
                            in_=wuv[:, :], in_offset=IOA(widx_all[:, s:s + 1], 0)), 16)
                        sg.inc(g.indirect_dma_start(
                            out=wdb[par3][:, :], out_offset=None,
                            in_=wdv[:, :], in_offset=IOA(widx_all[:, s:s + 1], 0)), 16)

                    if s < 3:
                        emit_wload(g)
                    else:
                        g.reg_load(erA, self.etab[0:1, s:s + 1])
                        g.reg_load(erB, self.etab[0:1, s - 3:s - 2])
                        n0 = sg.n
                        with g.If_ne(erA, erB):
                            emit_wload(g)
                        delta = sg.n - n0
                        with g.Else():
                            g.sem_inc(sg.h, delta)
                w_sg[s] = sg.n

            def stage_gather(s):
                par = s % 2
                par4 = s % 4

                @blk.gpsimd
                def _(g):
                    if s == 0:
                        g.wait_ge(cq.h, cdgate[0])
                    g.reg_load(erA, av[0:1, s:s + 1])
                    g.reg_load(erG2, fst[0:1, s:s + 1])
                    n0 = gg.n
                    with g.If_ne(erA, erGz):
                        g.wait_ge(ve.h, cd_ve[s])
                        g.wait_ge(pe.h, tr_pe[s - 2])
                        with g.If_ne(erG2, erGz):
                            g.indirect_dma_start(
                                out=xg[par][:, 0:D], out_offset=None,
                                in_=d["y1b"][:, :],
                                in_offset=IOA(idxu[par4][:, 0:1], 0)
                            ).then_inc(gg.h, 16)
                            g.sem_inc(gg.h, 48)
                        with g.Else():
                            for sub in range(4):
                                gg.inc(g.indirect_dma_start(
                                    out=xg[par][:, sub * D:(sub + 1) * D], out_offset=None,
                                    in_=d["y1b"][:, :],
                                    in_offset=IOA(idxu[par4][:, sub:sub + 1], 0)), 16)
                    delta = gg.n - n0
                    with g.Else():
                        g.sem_inc(gg.h, delta)
                    if s % 2 == 1:
                        g.wait_ge(gg.h, gg.n)
                        g.sem_inc(gq.h, 1)
                if s % 2 == 1:
                    gq.n += 1
                    gate[s] = gq.n
                gath_sg[s] = sg.n

            def stage_compute(s):
                par = s % 2
                par3 = s % 3
                par4 = s % 4

                if s == 0:
                    self.barrier()
                pe0 = pe.n
                ve0 = ve.n
                ac0 = ac.n
                silu_s = {ic: ac0 + ic + 1 for ic in range(4)}
                hh_s = {ic: ve0 + 33 + ic for ic in range(4)}
                evv = {sub: ve0 + 37 + sub for sub in range(4)}
                dd_s = {sub: pe0 + 40 + 2 * (sub + 1) for sub in range(4)}

                @blk.tensor
                def _(t):
                    t.reg_load(erT, av[0:1, s:s + 1])
                    t.reg_load(erT2, fst[0:1, s:s + 1])
                    with t.If_ne(erT, erTz):
                        t.wait_ge(gq.h, gate[s if s % 2 == 1 else s + 1])
                        t.wait_ge(ve.h, xgf_ac[s - 1])
                        with t.If_ne(erT2, erTz):
                            for i in range(8):
                                if i >= 2:
                                    t.wait_ge(ve.h, ve0 + i - 1)
                                t.transpose(
                                    pstr[i % 2][:],
                                    xg[par][:, i * 128:(i + 1) * 128],
                                    sb["identb"][:]).then_inc(pe.h, 1)
                            for ic in range(4):
                                if ic == 0:
                                    t.wait_ge(ve.h, ve0 + 32)
                                    t.wait_ge(ve.h, ev_ve[s - 1])
                                if ic >= 2:
                                    t.wait_ge(ac.h, silu_s[ic - 2])
                                    t.wait_ge(ve.h, hh_s[ic - 2])
                                for kc in range(8):
                                    ins = t.matmul(
                                        psF[2 * (ic % 2)][:, 0:128],
                                        wgb[par3][:, kc * EI + ic * 128: kc * EI + (ic + 1) * 128],
                                        xgf[par][:, kc * 512: kc * 512 + 128],
                                        start=(kc == 0), stop=(kc == 7))
                                ins.then_inc(pe.h, 1)
                                for kc in range(8):
                                    ins = t.matmul(
                                        psF[2 * (ic % 2) + 1][:, 0:128],
                                        wub[par3][:, kc * EI + ic * 128: kc * EI + (ic + 1) * 128],
                                        xgf[par][:, kc * 512: kc * 512 + 128],
                                        start=(kc == 0), stop=(kc == 7))
                                ins.then_inc(pe.h, 1 if ic < 3 else 17)
                            t.wait_ge(ve.h, hh_s[3])
                            for mh in range(2):
                                for ic in range(4):
                                    ins = t.matmul(
                                        psF[mh][:],
                                        hb[par][:, ic * 512: ic * 512 + 128],
                                        wdb[par3][:, ic * D + mh * 512: ic * D + (mh + 1) * 512],
                                        start=(ic == 0), stop=(ic == 3))
                                ins.then_inc(pe.h, 1 if mh == 0 else 15)
                        tEl = t.Else()
                        tEl.__enter__()
                        for i in range(32):
                            sub, kc = divmod(i, 8)
                            if i >= 2:
                                t.wait_ge(ve.h, ve0 + i - 1)
                            pe.inc(t.transpose(
                                pstr[i % 2][:],
                                xg[par][:, sub * D + kc * 128: sub * D + (kc + 1) * 128],
                                sb["identb"][:]), 1)
                        for ic in range(4):
                            if ic == 0:
                                t.wait_ge(ve.h, ve0 + 32)
                                t.wait_ge(ve.h, ev_ve[s - 1])
                            if ic >= 2:
                                t.wait_ge(ac.h, silu_s[ic - 2])
                                t.wait_ge(ve.h, hh_s[ic - 2])
                            for kc in range(8):
                                ins = t.matmul(psF[2 * (ic % 2)][:],
                                               wgb[par3][:, kc * EI + ic * 128: kc * EI + (ic + 1) * 128],
                                               xgf[par][:, kc * 512:(kc + 1) * 512],
                                               start=(kc == 0), stop=(kc == 7))
                            pe.inc(ins, 1)
                            for kc in range(8):
                                ins = t.matmul(psF[2 * (ic % 2) + 1][:],
                                               wub[par3][:, kc * EI + ic * 128: kc * EI + (ic + 1) * 128],
                                               xgf[par][:, kc * 512:(kc + 1) * 512],
                                               start=(kc == 0), stop=(kc == 7))
                            pe.inc(ins, 1)
                        for sub in range(4):
                            if sub == 0:
                                t.wait_ge(ve.h, hh_s[3])
                            if sub >= 2:
                                t.wait_ge(ve.h, evv[sub - 2])
                            for mh in range(2):
                                for ic in range(4):
                                    ins = t.matmul(
                                        psF[2 * (sub % 2) + mh][:],
                                        hb[par][:, ic * 512 + sub * 128: ic * 512 + (sub + 1) * 128],
                                        wdb[par3][:, ic * D + mh * 512: ic * D + (mh + 1) * 512],
                                        start=(ic == 0), stop=(ic == 3))
                                pe.inc(ins, 1)
                        tEl.__exit__(None, None, None)
                    dpe = pe.n - pe0
                    with t.Else():
                        t.sem_inc(pe.h, dpe)
                assert pe.n == pe0 + 48

                @blk.vector
                def _(v):
                    v.reg_load(erV, av[0:1, s:s + 1])
                    v.reg_load(erV2, fst[0:1, s:s + 1])
                    with v.If_ne(erV, erVz):
                        v.wait_ge(pe.h, gu_pe[s - 2])
                        with v.If_ne(erV2, erVz):
                            for i in range(8):
                                v.wait_ge(pe.h, pe0 + i + 1)
                                v.tensor_copy(
                                    out=xgf[par][:, i * 512: i * 512 + 128],
                                    in_=pstr[i % 2][:]
                                ).then_inc(ve.h, 1 if i < 7 else 25)
                            for ic in range(4):
                                v.wait_ge(ac.h, silu_s[ic])
                                v.tensor_tensor(
                                    out=hb[par][:, ic * 512: ic * 512 + 128],
                                    in0=psF[2 * (ic % 2) + 1][:, 0:128],
                                    in1=gsl[par][:, ic * 512: ic * 512 + 128],
                                    op=ALU.mult).then_inc(ve.h, 1)
                            v.wait_ge(pe.h, pe0 + 48)
                            for mh in range(2):
                                v.tensor_scalar(
                                    ot[par][:, mh * 512: (mh + 1) * 512],
                                    psF[mh][:],
                                    coefc[par4][:, 0:1], None, ALU.mult
                                ).then_inc(ve.h, 1 if mh == 0 else 3)
                        vEl = v.Else()
                        vEl.__enter__()
                        for i in range(32):
                            sub, kc = divmod(i, 8)
                            v.wait_ge(pe.h, pe0 + i + 1)
                            ve.inc(v.tensor_copy(
                                out=xgf[par][:, kc * 512 + sub * 128: kc * 512 + (sub + 1) * 128],
                                in_=pstr[i % 2][:]), 1)
                        for ic in range(4):
                            v.wait_ge(ac.h, silu_s[ic])
                            ve.inc(v.tensor_tensor(
                                out=hb[par][:, ic * 512:(ic + 1) * 512],
                                in0=psF[2 * (ic % 2) + 1][:],
                                in1=gsl[par][:, ic * 512:(ic + 1) * 512],
                                op=ALU.mult), 1)
                        for sub in range(4):
                            v.wait_ge(pe.h, dd_s[sub])
                            for mh in range(2):
                                ins = v.tensor_scalar(
                                    ot[par][:, sub * D + mh * 512: sub * D + (mh + 1) * 512],
                                    psF[2 * (sub % 2) + mh][:],
                                    coefc[par4][:, sub:sub + 1], None, ALU.mult)
                            ve.inc(ins, 1)
                        vEl.__exit__(None, None, None)
                    dve = ve.n - ve0
                    with v.Else():
                        v.sem_inc(ve.h, dve)
                assert ve.n == ve0 + 40

                @blk.scalar
                def _(a):
                    a.reg_load(erC, av[0:1, s:s + 1])
                    a.reg_load(erC2, fst[0:1, s:s + 1])
                    with a.If_ne(erC, erCz):
                        with a.If_ne(erC2, erCz):
                            for ic in range(4):
                                a.wait_ge(pe.h, min(pe0 + 10 + 2 * ic, pe0 + 32))
                                a.activation(out=gsl[par][:, ic * 512: ic * 512 + 128],
                                             in_=psF[2 * (ic % 2)][:, 0:128],
                                             func=AF.Silu).then_inc(ac.h, 1)
                        with a.Else():
                            for ic in range(4):
                                a.wait_ge(pe.h, pe0 + 32 + 2 * (ic + 1))
                                ac.inc(a.activation(out=gsl[par][:, ic * 512:(ic + 1) * 512],
                                                    in_=psF[2 * (ic % 2)][:], func=AF.Silu), 1)
                    dac = ac.n - ac0
                    with a.Else():
                        a.sem_inc(ac.h, dac)
                assert ac.n == ac0 + 4

                if s == 0:
                    self.barrier()
                tr_pe[s] = pe0 + 32
                xgf_ac[s] = ve0 + 32
                gu_pe[s] = pe0 + 40
                h_ve[s] = hh_s[3]
                d_pe[s] = pe0 + 48
                ev_ve[s] = ve0 + 40

                @blk.gpsimd
                def _(g):
                    g.reg_load(erA, av[0:1, s:s + 1])
                    g.reg_load(erG2, fst[0:1, s:s + 1])
                    n0 = sg.n
                    with g.If_ne(erA, erGz):
                        g.wait_ge(ve.h, ev_ve[s])
                        with g.If_ne(erG2, erGz):
                            g.indirect_dma_start(
                                out=d["slots"][:, :],
                                out_offset=IOA(destu[par4][:, 0:1], 0),
                                in_=ot[par][:, 0:D],
                                in_offset=None).then_inc(sg.h, 16)
                            g.sem_inc(sg.h, 48)
                        with g.Else():
                            for sub in range(4):
                                sg.inc(g.indirect_dma_start(
                                    out=d["slots"][:, :],
                                    out_offset=IOA(destu[par4][:, sub:sub + 1], 0),
                                    in_=ot[par][:, sub * D:(sub + 1) * D],
                                    in_offset=None), 16)
                    dsg = sg.n - n0
                    with g.Else():
                        g.sem_inc(sg.h, dsg)
                scat_sg[s] = sg.n

            for it in range(NSLOT + 3):
                if it < NSLOT:
                    stage_cdrow(it)
                if 0 <= it - 2 < NSLOT:
                    stage_gather(it - 2)
                if 0 <= it - 1 < NSLOT:
                    stage_wload(it - 1)
                if 0 <= it - 1 < NSLOT:
                    stage_idx(it - 1)
                if 0 <= it - 3 < NSLOT:
                    stage_compute(it - 3)

            regs_ctx.close()
        self.barrier()

    def final(self):
        nc, sb, d = self.nc, self.sb, self.dram
        blk = self.block
        sx, sg, pe, ac, ve = (self.sem[k] for k in ["sx", "sg", "pe", "ac", "ve"])
        dbg_slots = self.dbg("slots", [SLROWS, D], BF16)
        if dbg_slots is not None:
            aa, bb = self.sem["sg"].n, self.sem["sx"].n
            @blk.sync
            def _(sync):
                sync.wait_ge(sg.h, aa)
                sync.wait_ge(sx.h, bb)
                for r0 in range(0, SLROWS - 128, 4096):
                    sx.inc(sync.dma_start(out=dbg_slots[r0:r0 + 4096, :],
                                          in_=d["slots"][r0:r0 + 4096, :]), 16)

        with ExitStack() as ctx:
            slb = [self.sbuf(ctx, "slb", [128, 9 * D], BF16) for _ in range(2)]
            y1r = [self.sbuf(ctx, "y1r", [128, D], F32) for _ in range(2)]
            acc = [self.sbuf(ctx, "acc", [128, 2 * D], F32) for _ in range(2)]
            ytile = [self.sbuf(ctx, "ytF", [128, D], F32) for _ in range(2)]
            sqs = [self.sbuf(ctx, "sqsF", [128, D], F32) for _ in range(2)]
            stat = self.sbuf(ctx, "statF", [128, 8], F32)

            ld_sx = {}
            sum_ve = {-2: 0, -1: 0}
            all_sg = sg.n
            all_sx = sx.n

            cq = self.sem["cq"]
            fgate = {}

            @blk.sync
            def _(sync):
                sync.wait_ge(sg.h, all_sg)
                sync.wait_ge(sx.h, all_sx)
                for i in range(2):
                    for j in range(9):
                        r0 = i * 1152 + j
                        sx.inc(sync.dma_start(
                            out=slb[i][:, j * D:(j + 1) * D],
                            in_=d["slots"][r0: r0 + 128 * 9: 9, :]), 16)
                    sx.inc(sync.dma_start(out=y1r[i][:],
                                          in_=d["y1f"][i * 128:(i + 1) * 128, :]), 16)
                    ld_sx[i] = sx.n
                    sync.wait_ge(sx.h, sx.n)
                    sync.sem_inc(cq.h, 1)
            for i in range(2):
                cq.n += 1
                fgate[i] = cq.n

            for i in range(NTILE):
                par = i % 2

                @blk.vector
                def _(v):
                    v.wait_ge(cq.h, fgate[i])
                    v.wait_ge(ve.h, sum_ve[i - 2])
                    ve.inc(v.tensor_tensor(out=acc[par][:, 0:D],
                                           in0=slb[par][:, 0:D], in1=slb[par][:, D:2 * D],
                                           op=ALU.add), 1)
                    ve.inc(v.tensor_tensor(out=acc[par][:, D:2 * D],
                                           in0=slb[par][:, 2 * D:3 * D],
                                           in1=slb[par][:, 3 * D:4 * D], op=ALU.add), 1)
                    self.vsync(v, ve)
                    ve.inc(v.tensor_tensor(out=acc[par][:, 0:D], in0=acc[par][:, 0:D],
                                           in1=acc[par][:, D:2 * D], op=ALU.add), 1)
                    ve.inc(v.tensor_tensor(out=acc[par][:, D:2 * D],
                                           in0=slb[par][:, 4 * D:5 * D],
                                           in1=slb[par][:, 5 * D:6 * D], op=ALU.add), 1)
                    self.vsync(v, ve)
                    ve.inc(v.tensor_tensor(out=acc[par][:, 0:D], in0=acc[par][:, 0:D],
                                           in1=acc[par][:, D:2 * D], op=ALU.add), 1)
                    ve.inc(v.tensor_tensor(out=acc[par][:, D:2 * D],
                                           in0=slb[par][:, 6 * D:7 * D],
                                           in1=slb[par][:, 7 * D:8 * D], op=ALU.add), 1)
                    self.vsync(v, ve)
                    ve.inc(v.tensor_tensor(out=acc[par][:, 0:D], in0=acc[par][:, 0:D],
                                           in1=acc[par][:, D:2 * D], op=ALU.add), 1)
                    self.vsync(v, ve)
                    ve.inc(v.tensor_tensor(out=acc[par][:, 0:D], in0=acc[par][:, 0:D],
                                           in1=slb[par][:, 8 * D:9 * D], op=ALU.add), 1)
                    self.vsync(v, ve)
                    ve.inc(v.tensor_tensor(out=ytile[par][:], in0=acc[par][:, 0:D],
                                           in1=y1r[par][:], op=ALU.add), 1)
                    self.vsync(v, ve)
                    ve.inc(v.tensor_reduce(stat[:, 0:1], ytile[par][:], AX.X, ALU.add), 1)
                    self.vsync(v, ve)
                    ve.inc(v.tensor_scalar(stat[:, 1:2], stat[:, 0:1], -1.0 / D, None, ALU.mult), 1)
                    self.vsync(v, ve)
                    ve.inc(v.tensor_scalar(ytile[par][:], ytile[par][:], stat[:, 1:2], None, ALU.add), 1)
                    self.vsync(v, ve)
                    ve.inc(v.tensor_tensor(out=sqs[par][:], in0=ytile[par][:],
                                           in1=ytile[par][:], op=ALU.mult), 1)
                    self.vsync(v, ve)
                    ve.inc(v.tensor_reduce(stat[:, 3:4], sqs[par][:], AX.X, ALU.add), 1)
                    self.vsync(v, ve)
                    ve.inc(v.tensor_scalar(stat[:, 6:7], stat[:, 3:4], 1.0 / D, 1e-5,
                                           ALU.mult, ALU.add), 1)
                sq_ve = ve.n

                @blk.scalar
                def _(a):
                    a.wait_ge(ve.h, sq_ve)
                    ac.inc(a.sqrt(out=stat[:, 4:5], in_=stat[:, 6:7]), 1)
                rstd_ac = ac.n

                @blk.vector
                def _(v):
                    v.wait_ge(ac.h, rstd_ac)
                    ve.inc(v.reciprocal(stat[:, 5:6], stat[:, 4:5]), 1)
                    self.vsync(v, ve)
                    ve.inc(v.tensor_scalar(ytile[par][:], ytile[par][:], stat[:, 5:6],
                                           None, ALU.mult), 1)
                sum_ve[i] = ve.n

                @blk.sync
                def _(sync):
                    sync.wait_ge(ve.h, sum_ve[i])
                    sx.inc(sync.dma_start(out=d["out"][i * 128:(i + 1) * 128, :],
                                          in_=ytile[par][:]), 16)
                    if i + 2 < NTILE:
                        for j in range(9):
                            r0 = (i + 2) * 1152 + j
                            sx.inc(sync.dma_start(
                                out=slb[par][:, j * D:(j + 1) * D],
                                in_=d["slots"][r0: r0 + 128 * 9: 9, :]), 16)
                        sx.inc(sync.dma_start(out=y1r[par][:],
                                              in_=d["y1f"][(i + 2) * 128:(i + 3) * 128, :]), 16)
                        ld_sx[i + 2] = sx.n
                        sync.wait_ge(sx.h, sx.n)
                        sync.sem_inc(cq.h, 1)
                if i + 2 < NTILE:
                    cq.n += 1
                    fgate[i + 2] = cq.n

            tgt = sx.n

            @blk.sync
            def _(sync):
                sync.wait_ge(sx.h, tgt)

    def finalize_stub(self):
        nc, d = self.nc, self.dram
        blk = self.block
        sx = self.sem["sx"]

        @blk.sync
        def _(sync):
            sync.wait_ge(sx.h, sx.n)
            for i in range(NTILE):
                sx.inc(sync.dma_start(out=d["out"][i * 128:(i + 1) * 128, :],
                                      in_=d["y1f"][i * 128:(i + 1) * 128, :]), 16)
        tgt = sx.n

        @blk.sync
        def _(sync):
            sync.wait_ge(sx.h, tgt)

    def finish(self):
        if getattr(self, "routing_ctx", None) is not None:
            try:
                self.routing_ctx.close()
            except Exception:
                pass
            self.routing_ctx = None
        self.ctx.close()
        return self.nc


def make_consts():
    identb = np.eye(128).astype(ml_dtypes.bfloat16)
    identf = np.eye(128).astype(np.float32)
    cc = np.arange(384)[None, :]
    pp = np.arange(128)[:, None]
    triu = (cc >= pp + 128).astype(ml_dtypes.bfloat16)
    cdpat = np.zeros((128, 1536), np.float32)
    cdpat[:, 1::3] = (DUMP0 + np.arange(128))[:, None]
    return {"identb": identb, "identf": identf, "triu": triu, "cdpat": cdpat}


def build_nc(debug=(), stages=("attn", "wo", "routing", "shared", "moe", "final")):
    b = B(debug=debug, no_scatter="noscat" in stages)
    b.load_consts()
    b.attention()
    b.wo_ln1()
    if "routing" in stages:
        b.routing()
    if "shared" in stages:
        b.shared_ffn()
    if "moe" in stages:
        b.moe()
    if "final" in stages:
        b.final()
    else:
        b.finalize_stub()
    b.finish()
    return b


def kernel(L0_attn_x, L0_attn_W_q, L0_attn_W_k, L0_attn_W_v, L0_attn_W_o,
           L0_ffn_router_w, L0_ffn_e_bias, L0_ffn_Wg, L0_ffn_Wu, L0_ffn_Wd,
           L0_ffn_SWg, L0_ffn_SWu, L0_ffn_SWd, _debug=(), _cores=None,
           _stages=("attn", "wo", "routing", "shared", "moe", "final")):
    x = np.ascontiguousarray(np.asarray(L0_attn_x, np.float32))
    consts = make_consts()
    common = {
        "wq": np.ascontiguousarray(L0_attn_W_q), "wk": np.ascontiguousarray(L0_attn_W_k),
        "wv": np.ascontiguousarray(L0_attn_W_v), "wo": np.ascontiguousarray(L0_attn_W_o),
        "rw": np.ascontiguousarray(L0_ffn_router_w),
        "ebias": np.ascontiguousarray(np.asarray(L0_ffn_e_bias, np.float32).reshape(1, E)),
        "wg": np.ascontiguousarray(np.asarray(L0_ffn_Wg).reshape(E, 8, 128, EI)
                                   .transpose(0, 2, 1, 3).reshape(E * 128, 8 * EI)),
        "wu": np.ascontiguousarray(np.asarray(L0_ffn_Wu).reshape(E, 8, 128, EI)
                                   .transpose(0, 2, 1, 3).reshape(E * 128, 8 * EI)),
        "wd": np.ascontiguousarray(np.asarray(L0_ffn_Wd).reshape(E, 4, 128, D)
                                   .transpose(0, 2, 1, 3).reshape(E * 128, 4 * D)),
        "swg": np.ascontiguousarray(L0_ffn_SWg), "swu": np.ascontiguousarray(L0_ffn_SWu),
        "swd": np.ascontiguousarray(L0_ffn_SWd),
        **consts,
    }
    cores = list(range(N_CORES)) if _cores is None else _cores
    n = len(cores)
    in_maps = []
    for i in range(n):
        m = dict(common)
        m["x"] = np.ascontiguousarray(x[2 * i:2 * i + 2].reshape(NTOK, D))
        in_maps.append(m)
    b = build_nc(debug=_debug, stages=_stages)
    res = run_bass_kernel_spmd(b.nc, in_maps, core_ids=cores)
    outs = [res.results[i]["out"].reshape(2, SEQ, D) for i in range(n)]
    full = np.concatenate(outs, axis=0).astype(np.float32)
    if _debug:
        return full, [res.results[i] for i in range(n)]
    return full


# version 21 (final)
# speedup vs baseline: 1.2173x; 1.0041x over previous
import numpy as np
import ml_dtypes
from contextlib import ExitStack

import concourse.bass as bass
import concourse.mybir as mybir
from concourse.bass_utils import run_bass_kernel_spmd

F32 = mybir.dt.float32
BF16 = mybir.dt.bfloat16
I32 = mybir.dt.int32
U32 = mybir.dt.uint32
AF = mybir.ActivationFunctionType
ALU = mybir.AluOpType
AX = mybir.AxisListType
IOA = bass.IndirectOffsetOnAxis

N_CORES = 8
NTOK = 4096
SEQ = 2048
D = 1024
H = 16
CH = 256
NCH = SEQ // CH
NTILE = NTOK // 128
E = 64
EI = 512
SCAP = 512
NSLOT = 128
NSTREAM = NSLOT * SCAP
DUMP0 = NTOK * 9
SLROWS = DUMP0 + 128


class Counter:
    def __init__(self, nc, name):
        self.h = nc.alloc_semaphore(name=name)
        self.n = 0

    def inc(self, instr, k=1):
        instr.then_inc(self.h, k)
        self.n += k
        return instr


class B:
    def __init__(self, debug=(), no_scatter=False):
        self.no_scatter = no_scatter
        self.debug = set(debug)
        nc = bass.Bass("TRN2", target_bir_lowering=False, debug=False)
        self.nc = nc
        self.ctx = ExitStack()
        self._uid = 0
        d = self.dram = {}
        d["x"] = nc.dram_tensor("x", [NTOK, D], F32, kind="ExternalInput").ap()
        for w in ["wq", "wk", "wv", "wo", "swg", "swu", "swd"]:
            d[w] = nc.dram_tensor(w, [D, D], F32, kind="ExternalInput").ap()
        d["rw"] = nc.dram_tensor("rw", [E, D], F32, kind="ExternalInput").ap()
        d["ebias"] = nc.dram_tensor("ebias", [1, E], F32, kind="ExternalInput").ap()
        d["wg"] = nc.dram_tensor("wg", [E * 128, 8 * EI], F32, kind="ExternalInput").ap()
        d["wu"] = nc.dram_tensor("wu", [E * 128, 8 * EI], F32, kind="ExternalInput").ap()
        d["wd"] = nc.dram_tensor("wd", [E * 128, 4 * D], F32, kind="ExternalInput").ap()
        d["identb"] = nc.dram_tensor("identb", [128, 128], BF16, kind="ExternalInput").ap()
        d["identf"] = nc.dram_tensor("identf", [128, 128], F32, kind="ExternalInput").ap()
        d["triu"] = nc.dram_tensor("triu", [128, 384], BF16, kind="ExternalInput").ap()
        d["cdpat"] = nc.dram_tensor("cdpat", [128, 1536], F32, kind="ExternalInput").ap()
        d["out"] = nc.dram_tensor("out", [NTOK, D], F32, kind="ExternalOutput").ap()
        d["od"] = nc.dram_tensor("od", [NTOK, D], BF16, kind="Internal").ap()
        d["y1f"] = nc.dram_tensor("y1f", [NTOK, D], F32, kind="Internal").ap()
        d["y1b"] = nc.dram_tensor("y1b", [NTOK, D], BF16, kind="Internal").ap()
        d["cdest"] = nc.dram_tensor("cdest", [NSTREAM + 1, 3], F32, kind="Internal").ap()
        d["wgc"] = nc.dram_tensor("wgc", [E * 128, 8 * EI], BF16, kind="Internal").ap()
        d["wuc"] = nc.dram_tensor("wuc", [E * 128, 8 * EI], BF16, kind="Internal").ap()
        d["wdc"] = nc.dram_tensor("wdc", [E * 128, 4 * D], BF16, kind="Internal").ap()
        d["slots"] = nc.dram_tensor("slots", [SLROWS, D], BF16, kind="Internal").ap()
        self.dbg_out = {}

        e = self.ctx.enter_context
        self.sem = {k: Counter(nc, k) for k in
                    ["sx", "sg", "sw", "pe", "ve", "ac", "po", "bar", "gq", "cq", "io",
                     "gg"]}
        sb = self.sb = {}
        sb["identb"] = e(nc.sbuf_tensor("sb_identb", [128, 128], BF16))
        sb["identf"] = e(nc.sbuf_tensor("sb_identf", [128, 128], F32))
        sb["triu"] = e(nc.sbuf_tensor("sb_triu", [128, 384], BF16))
        sb["ones1p"] = e(nc.sbuf_tensor("ones1p", [1, 128], F32))
        sb["ones11"] = e(nc.sbuf_tensor("ones11", [1, 1], F32))
        sb["onesc"] = e(nc.sbuf_tensor("onesc", [64, 1], F32))
        sb["ones64"] = e(nc.sbuf_tensor("ones64", [128, 64], F32))
        self.etab = e(nc.sbuf_tensor("etab", [1, 128], I32))
        self.etabf = e(nc.sbuf_tensor("etabf", [1, 128], F32))
        self.alive = e(nc.sbuf_tensor("alive", [1, 128], I32))
        self.fastf = e(nc.sbuf_tensor("fastf", [1, 128], I32))
        self.zi32 = e(nc.sbuf_tensor("zi32", [1, 4], I32))
        self.y1fm_ctx = ExitStack()
        sb["y1fm"] = self.y1fm_ctx.enter_context(nc.sbuf_tensor("y1fm", [128, 8 * NTOK], BF16))
        self.block = e(nc.Block())

    def u(self, base):
        self._uid += 1
        return f"{base}_{self._uid}"

    def sbuf(self, ctx, base, shape, dtype):
        return ctx.enter_context(self.nc.sbuf_tensor(self.u(base), shape, dtype))

    def psum(self, ctx, base, shape, dtype=F32):
        return ctx.enter_context(self.nc.psum_tensor(self.u(base), shape, dtype))

    def dbg(self, name, shape, dtype=F32):
        if name not in self.debug:
            return None
        if name not in self.dbg_out:
            self.dbg_out[name] = self.nc.dram_tensor(
                "dbg_" + name, shape, dtype, kind="ExternalOutput").ap()
        return self.dbg_out[name]

    def barrier(self):
        bar = self.sem["bar"]
        blk = self.block
        for sec in [blk.sync, blk.gpsimd, blk.tensor, blk.vector, blk.scalar]:
            @sec
            def _(eng):
                eng.sem_inc(bar.h, 1)
        bar.n += 5
        tgt = bar.n
        for sec in [blk.sync, blk.gpsimd, blk.tensor, blk.vector, blk.scalar]:
            @sec
            def _(eng):
                eng.wait_ge(bar.h, tgt)

    def load_consts(self):
        nc, sb, d = self.nc, self.sb, self.dram
        sx, ve = self.sem["sx"], self.sem["ve"]

        @self.block.sync
        def _(sync):
            sx.inc(sync.dma_start(out=sb["identb"][:], in_=d["identb"][:, :]), 16)
            sx.inc(sync.dma_start(out=sb["identf"][:], in_=d["identf"][:, :]), 16)
            sx.inc(sync.dma_start(out=sb["triu"][:], in_=d["triu"][:, :]), 16)
            cd2 = d["cdest"]
            sx.inc(sync.dma_start(
                out=cd2[0:NSTREAM, :].rearrange("a b -> (a b)").rearrange("(p f) -> p f", p=128),
                in_=d["cdpat"][:, :]), 16)
            sx.inc(sync.dma_start(out=cd2[NSTREAM:NSTREAM + 1, :], in_=d["cdpat"][0:1, 0:3]), 16)

        @self.block.vector
        def _(v):
            v.memset(sb["ones1p"][:], 1.0)
            v.memset(sb["ones11"][:], 1.0)
            v.memset(sb["onesc"][:], 1.0)
            ve.inc(v.memset(sb["ones64"][:], 1.0), 1)
        self.consts_ve = ve.n
        self.consts_sx = sx.n
        blk = self.block
        for sec in [blk.sync, blk.gpsimd, blk.tensor, blk.vector, blk.scalar]:
            @sec
            def _(eng):
                eng.wait_ge(sx.h, self.consts_sx)
                eng.wait_ge(ve.h, self.consts_ve)

    def attention(self):
        nc, sb, d = self.nc, self.sb, self.dram
        blk = self.block
        sg, pe, ac, ve, sw = (self.sem[k] for k in ["sg", "pe", "ac", "ve", "sw"])

        with ExitStack() as ctx:
            otok = [self.sbuf(ctx, "otok", [128, 2 * D], BF16) for _ in range(2)]
            wqb = self.sbuf(ctx, "wqb", [128, 8 * D], BF16)
            wkb = self.sbuf(ctx, "wkb", [128, 8 * D], BF16)
            wvb = self.sbuf(ctx, "wvb", [128, 8 * D], BF16)
            xstage = [self.sbuf(ctx, "xs", [128, 2 * D], F32) for _ in range(2)]
            xf = self.sbuf(ctx, "xf", [128, 8 * CH], BF16)
            qt = self.sbuf(ctx, "qt", [128, 2 * D], BF16)
            kt = self.sbuf(ctx, "kt", [128, 2 * D], BF16)
            vt = self.sbuf(ctx, "vt", [128, 2 * D], BF16)
            qf = self.sbuf(ctx, "qf", [128, 8 * CH], BF16)
            kf = self.sbuf(ctx, "kf", [128, 8 * CH], BF16)
            i1 = self.sbuf(ctx, "i1", [128, 2 * CH], BF16)
            i2 = self.sbuf(ctx, "i2", [128, 2 * CH], BF16)
            zfb = self.sbuf(ctx, "zfb", [128, CH], BF16)
            ztb = self.sbuf(ctx, "ztb", [128, 128], BF16)
            s1f = self.sbuf(ctx, "s1f", [128, H * 64], F32)
            s2f = self.sbuf(ctx, "s2f", [128, H * 64], F32)
            s1b = self.sbuf(ctx, "s1b", [128, H * 64], BF16)
            s2b = self.sbuf(ctx, "s2b", [128, H * 64], BF16)

            @blk.gpsimd
            def _(g):
                for wname, wt in [("wq", wqb), ("wk", wkb), ("wv", wvb)]:
                    for kc in range(8):
                        sw.inc(g.dma_start(
                            out=wt[:, kc * D:(kc + 1) * D],
                            in_=d[wname][kc * 128:(kc + 1) * 128, :]), 16)
            w_ready = sw.n
            wcc = self.sem["po"]

            @blk.gpsimd
            def _(g):
                for src, dst in [("wg", "wgc"), ("wu", "wuc"), ("wd", "wdc")]:
                    for e_ in range(E):
                        wcc.inc(g.dma_start(
                            out=d[dst][e_ * 128:(e_ + 1) * 128, :],
                            in_=d[src][e_ * 128:(e_ + 1) * 128, :]), 16)
            self.wcache_po = wcc.n

            xgate = {}
            cqA = self.sem["cq"]
            sxc = self.sem["sx"]
            chunks = [(s, c) for s in range(2) for c in range(NCH)]

            @blk.sync
            def _(sync):
                for ci in range(min(2, len(chunks))):
                    s, c = chunks[ci]
                    for t2 in range(2):
                        r0 = s * SEQ + c * CH + 128 * t2
                        sxc.inc(sync.dma_start(out=xstage[ci % 2][:, t2 * D:(t2 + 1) * D],
                                               in_=d["x"][r0:r0 + 128, :]), 16)
                    sync.wait_ge(sxc.h, sxc.n)
                    sync.sem_inc(cqA.h, 1)
            for ci in range(min(2, len(chunks))):
                cqA.n += 1
                xgate[ci] = cqA.n

            last = {"i1v": 0, "i2v": 0, "zac": 0, "oac": 0, "s1v": 0, "s2v": 0,
                    "mix_pe": 0}
            od_sx = {}

            for ci, (s, c) in enumerate(chunks):
                par = ci % 2
                tok0 = s * SEQ + c * CH

                if c == 0:
                    @blk.vector
                    def _(v):
                        if ci > 0:
                            v.wait_ge(pe.h, last["mix_pe"])
                        for t in [s1f, s2f, s1b, s2b]:
                            ins = v.memset(t[:], 0.0)
                        ve.inc(ins, 1)
                st_reset = ve.n

                with ExitStack() as cctx:
                    pstr = [self.psum(cctx, "pstr", [128, 128], BF16) for _ in range(2)]
                    psp = [self.psum(cctx, "psp", [128, 512], F32) for _ in range(2)]

                    tr_ac0 = ac.n
                    tr_pe0 = pe.n

                    @blk.tensor
                    def _(t):
                        t.wait_ge(cqA.h, xgate[ci])
                        if ci == 0:
                            t.wait_ge(self.sem["sx"].h, self.consts_sx)
                        for i in range(16):
                            t2, kc = divmod(i, 8)
                            if i >= 2:
                                t.wait_ge(ac.h, tr_ac0 + i - 1)
                            else:
                                t.wait_ge(ac.h, tr_ac0)
                            pe.inc(t.transpose(
                                psp[i % 2][:, 0:128],
                                xstage[par][:, t2 * D + kc * 128: t2 * D + (kc + 1) * 128],
                                sb["identf"][:]), 1)

                    @blk.scalar
                    def _(a):
                        for i in range(16):
                            t2, kc = divmod(i, 8)
                            a.wait_ge(pe.h, tr_pe0 + i + 1)
                            ac.inc(a.copy(
                                out=xf[:, kc * CH + t2 * 128: kc * CH + (t2 + 1) * 128],
                                in_=psp[i % 2][:, 0:128]), 1)
                    xf_done = ac.n

                    if ci + 2 < len(chunks):
                        s2_, c2_ = chunks[ci + 2]

                        @blk.sync
                        def _(sync):
                            sync.wait_ge(pe.h, tr_pe0 + 16)
                            for t2 in range(2):
                                r0 = s2_ * SEQ + c2_ * CH + 128 * t2
                                sxc.inc(sync.dma_start(
                                    out=xstage[par][:, t2 * D:(t2 + 1) * D],
                                    in_=d["x"][r0:r0 + 128, :]), 16)
                            sync.wait_ge(sxc.h, sxc.n)
                            sync.sem_inc(cqA.h, 1)
                        cqA.n += 1
                        xgate[ci + 2] = cqA.n

                    prj_ac0 = ac.n
                    prj_pe = []

                    @blk.tensor
                    def _(t):
                        t.wait_ge(ac.h, xf_done)
                        if ci == 0:
                            t.wait_ge(sw.h, w_ready)
                        n = 0
                        for t2 in range(2):
                            for wt in [wqb, wkb, wvb]:
                                for half in range(2):
                                    if n >= 2:
                                        t.wait_ge(ac.h, prj_ac0 + n - 1)
                                    for kc in range(8):
                                        ins = t.matmul(
                                            psp[n % 2][:],
                                            xf[:, kc * CH + t2 * 128: kc * CH + (t2 + 1) * 128],
                                            wt[:, kc * D + half * 512: kc * D + (half + 1) * 512],
                                            start=(kc == 0), stop=(kc == 7))
                                    pe.inc(ins, 1)
                                    prj_pe.append(pe.n)
                                    n += 1

                    @blk.scalar
                    def _(a):
                        n = 0
                        for t2 in range(2):
                            for dst in [qt, kt, vt]:
                                for half in range(2):
                                    a.wait_ge(pe.h, prj_pe[n])
                                    ac.inc(a.copy(
                                        out=dst[:, t2 * D + half * 512: t2 * D + (half + 1) * 512],
                                        in_=psp[n % 2][:]), 1)
                                    n += 1
                    prj_done = ac.n

                    tr2_ac0 = ac.n
                    tr2_pe0 = pe.n

                    @blk.tensor
                    def _(t):
                        t.wait_ge(ac.h, prj_done)
                        for i in range(32):
                            si, rem = divmod(i, 16)
                            t2, kc = divmod(rem, 8)
                            srct = qt if si == 0 else kt
                            if i >= 2:
                                t.wait_ge(ac.h, tr2_ac0 + i - 1)
                            pe.inc(t.transpose(
                                pstr[i % 2][:],
                                srct[:, t2 * D + kc * 128: t2 * D + (kc + 1) * 128],
                                sb["identb"][:]), 1)

                    @blk.scalar
                    def _(a):
                        for i in range(32):
                            si, rem = divmod(i, 16)
                            t2, kc = divmod(rem, 8)
                            dstt = qf if si == 0 else kf
                            a.wait_ge(pe.h, tr2_pe0 + i + 1)
                            ac.inc(a.copy(
                                out=dstt[:, kc * CH + t2 * 128: kc * CH + (t2 + 1) * 128],
                                in_=pstr[i % 2][:]), 1)
                    qkf_done = ac.n

                with ExitStack() as hctx:
                    psAB = self.psum(hctx, "psAB", [128, 512], F32)
                    psZO = self.psum(hctx, "psZO", [128, 256], F32)
                    psO2 = self.psum(hctx, "psO2", [128, 128], F32)
                    psZt = self.psum(hctx, "psZt", [128, 128], F32)
                    psS = self.psum(hctx, "psS", [128, 128], F32)

                    for h in range(H):
                        po2 = (h % 2) * 64
                        kc = h // 2
                        qfh = qf[po2:po2 + 64, kc * CH:(kc + 1) * CH]
                        kfh = kf[po2:po2 + 64, kc * CH:(kc + 1) * CH]
                        s1bh = s1b[po2:po2 + 64, h * 64:(h + 1) * 64]
                        s2bh = s2b[po2:po2 + 64, h * 64:(h + 1) * 64]
                        s1fh = s1f[po2:po2 + 64, h * 64:(h + 1) * 64]
                        s2fh = s2f[po2:po2 + 64, h * 64:(h + 1) * 64]

                        def kth(t2):
                            return kt[:, t2 * D + h * 64: t2 * D + h * 64 + 64]

                        def vth(t2):
                            return vt[:, t2 * D + h * 64: t2 * D + h * 64 + 64]

                        @blk.tensor
                        def _(t):
                            t.wait_ge(ac.h, qkf_done)
                            t.wait_ge(ve.h, max(st_reset, last["i2v"]))
                            for st in range(2):
                                pe.inc(t.matmul(psAB[:, st * CH:(st + 1) * CH],
                                                kfh[:, st * 128:(st + 1) * 128],
                                                qfh[:], start=True, stop=True), 1)
                        at_pe = pe.n

                        @blk.vector
                        def _(v):
                            v.wait_ge(pe.h, at_pe)
                            for st in range(2):
                                mask = sb["triu"][:, 128 - st * 128: 256 - st * 128 + 128]
                                ve.inc(v.tensor_tensor(
                                    out=i1[:, st * CH:(st + 1) * CH],
                                    in0=psAB[:, st * CH:(st + 1) * CH],
                                    in1=mask, op=ALU.mult), 1)
                        last["i1v"] = ve.n

                        @blk.tensor
                        def _(t):
                            t.wait_ge(ve.h, last["i1v"])
                            t.wait_ge(ac.h, max(last["zac"], last["oac"]))
                            t.wait_ge(ve.h, last["s1v"])
                            for st in range(2):
                                t.matmul(psZO[po2:po2 + 64, 0:CH], kth(st),
                                         i1[:, st * CH:(st + 1) * CH],
                                         start=(st == 0), stop=False)
                            pe.inc(t.matmul(psZO[po2:po2 + 64, 0:CH], s1bh, qfh[:],
                                            start=False, stop=True), 1)
                            for rt in range(2):
                                for st in range(2):
                                    t.matmul(psZt[:, rt * 64:(rt + 1) * 64],
                                             i1[:, st * CH + rt * 128: st * CH + (rt + 1) * 128],
                                             kth(st), start=(st == 0), stop=False)
                                pe.inc(t.matmul(psZt[:, rt * 64:(rt + 1) * 64],
                                                qfh[:, rt * 128:(rt + 1) * 128], s1bh,
                                                start=False, stop=True), 1)
                            for st in range(2):
                                ins = t.matmul(psS[po2:po2 + 64, 0:64], kth(st), kth(st),
                                               start=(st == 0), stop=(st == 1))
                            pe.inc(ins, 1)
                        z_pe = pe.n

                        @blk.scalar
                        def _(a):
                            a.wait_ge(pe.h, z_pe)
                            a.copy(out=zfb[po2:po2 + 64, :], in_=psZO[po2:po2 + 64, 0:CH])
                            ac.inc(a.copy(out=ztb[:], in_=psZt[:]), 1)
                        last["zac"] = ac.n

                        @blk.vector
                        def _(v):
                            v.wait_ge(pe.h, z_pe)
                            ve.inc(v.tensor_tensor(out=s1fh, in0=s1fh,
                                                   in1=psS[po2:po2 + 64, 0:64], op=ALU.add), 1)
                            v.wait_ge(ve.h, ve.n)
                            ve.inc(v.tensor_copy(out=s1bh, in_=s1fh), 1)
                        last["s1v"] = ve.n

                        @blk.tensor
                        def _(t):
                            t.wait_ge(ac.h, last["zac"])
                            t.wait_ge(ve.h, last["i1v"])
                            for rt in range(2):
                                pe.inc(t.matmul(psAB[:, rt * CH:(rt + 1) * CH],
                                                zfb[po2:po2 + 64, rt * 128:(rt + 1) * 128],
                                                qfh[:], start=True, stop=True), 1)
                        bt_pe = pe.n

                        @blk.vector
                        def _(v):
                            v.wait_ge(pe.h, bt_pe)
                            for rt in range(2):
                                mask = sb["triu"][:, 128 - rt * 128: 256 - rt * 128 + 128]
                                ve.inc(v.tensor_tensor(
                                    out=i2[:, rt * CH:(rt + 1) * CH],
                                    in0=psAB[:, rt * CH:(rt + 1) * CH],
                                    in1=mask, op=ALU.mult), 1)
                        last["i2v"] = ve.n

                        @blk.tensor
                        def _(t):
                            t.wait_ge(ve.h, last["i2v"])
                            t.wait_ge(ac.h, last["oac"])
                            t.wait_ge(ve.h, last["s2v"])
                            for ih in range(2):
                                for rt in range(2):
                                    t.matmul(psO2[:, ih * 64:(ih + 1) * 64],
                                             i2[:, rt * CH + ih * 128: rt * CH + (ih + 1) * 128],
                                             vth(rt), start=(rt == 0), stop=False)
                                pe.inc(t.matmul(psO2[:, ih * 64:(ih + 1) * 64],
                                                qfh[:, ih * 128:(ih + 1) * 128], s2bh,
                                                start=False, stop=True), 1)
                            for rt in range(2):
                                ins = t.matmul(psS[po2:po2 + 64, 64:128], ztb[:, rt * 64:(rt + 1) * 64],
                                               vth(rt), start=(rt == 0), stop=(rt == 1))
                            pe.inc(ins, 1)
                        o_pe = pe.n

                        @blk.scalar
                        def _(a):
                            a.wait_ge(pe.h, o_pe)
                            if h == 0:
                                a.wait_ge(self.sem["sx"].h, od_sx.get(ci - 2, 0))
                            for ih in range(2):
                                ins = a.copy(
                                    out=otok[par][:, ih * D + h * 64: ih * D + h * 64 + 64],
                                    in_=psO2[:, ih * 64:(ih + 1) * 64])
                            ac.inc(ins, 1)
                        last["oac"] = ac.n

                        @blk.vector
                        def _(v):
                            v.wait_ge(pe.h, o_pe)
                            ve.inc(v.tensor_tensor(out=s2fh, in0=s2fh,
                                                   in1=psS[po2:po2 + 64, 64:128], op=ALU.add), 1)
                            v.wait_ge(ve.h, ve.n)
                            ve.inc(v.tensor_copy(out=s2bh, in_=s2fh), 1)
                        last["s2v"] = ve.n

                    last["mix_pe"] = pe.n

                @blk.sync
                def _(sync):
                    sync.wait_ge(ac.h, last["oac"])
                    for ih in range(2):
                        self.sem["sx"].inc(sync.dma_start(
                            out=d["od"][tok0 + ih * 128: tok0 + (ih + 1) * 128, :],
                            in_=otok[par][:, ih * D:(ih + 1) * D]), 16)
                od_sx[ci] = self.sem["sx"].n

            self.barrier()

    def wo_ln1(self):
        nc, sb, d = self.nc, self.sb, self.dram
        blk = self.block
        sx, sg, pe, ac, ve, sw = (self.sem[k] for k in ["sx", "sg", "pe", "ac", "ve", "sw"])

        dbg_y1 = self.dbg("y1", [NTOK, D], F32)

        with ExitStack() as ctx:
            wob = self.sbuf(ctx, "wob", [128, 8 * D], BF16)
            odrow = [self.sbuf(ctx, "odr", [128, D], BF16) for _ in range(2)]
            ofeat = [self.sbuf(ctx, "ofe", [128, D], BF16) for _ in range(2)]
            xrow = [self.sbuf(ctx, "xr", [128, D], F32) for _ in range(2)]
            sqs = [self.sbuf(ctx, "sqs", [128, D], F32) for _ in range(2)]
            ytile = [self.sbuf(ctx, "yt", [128, D], F32) for _ in range(2)]
            ybtile = [self.sbuf(ctx, "yb", [128, D], BF16) for _ in range(2)]
            stat = self.sbuf(ctx, "stat", [128, 8], F32)
            psw = [self.psum(ctx, "psw", [128, 512], F32) for _ in range(4)]
            pstr = [self.psum(ctx, "pstr", [128, 128], BF16) for _ in range(2)]

            @blk.gpsimd
            def _(g):
                for kc in range(8):
                    sw.inc(g.dma_start(out=wob[:, kc * D:(kc + 1) * D],
                                       in_=d["wo"][kc * 128:(kc + 1) * 128, :]), 16)
            wo_ready = sw.n

            x_done = {}
            od_done = {}

            @blk.sync
            def _(sync):
                for i in range(2):
                    sx.inc(sync.dma_start(out=xrow[i][:],
                                          in_=d["x"][i * 128:(i + 1) * 128, :]), 16)
                    x_done[i] = sx.n
                    sx.inc(sync.dma_start(out=odrow[i][:],
                                          in_=d["od"][i * 128:(i + 1) * 128, :]), 16)
                    od_done[i] = sx.n

            add_ve = {-2: 0, -1: 0}
            reuse_ve = {-2: 0, -1: 0}

            ofeat_ac = {}
            tr_done = {}
            for i in range(NTILE):
                par = i % 2

                otr_pe0 = pe.n
                otr_ac0 = ac.n

                @blk.tensor
                def _(t):
                    t.wait_ge(sx.h, od_done[i])
                    if i >= 1:
                        t.wait_ge(ac.h, tr_done[i - 1])
                    for kci in range(8):
                        if kci >= 2:
                            t.wait_ge(ac.h, otr_ac0 + kci - 1)
                        pe.inc(t.transpose(pstr[kci % 2][:],
                                           odrow[par][:, kci * 128:(kci + 1) * 128],
                                           sb["identb"][:]), 1)
                otr_pe = pe.n

                @blk.scalar
                def _(a):
                    if i >= 2:
                        a.wait_ge(pe.h, omm_pe[i - 2])
                    for kci in range(8):
                        a.wait_ge(pe.h, otr_pe0 + kci + 1)
                        ac.inc(a.copy(out=ofeat[par][:, kci * 128:(kci + 1) * 128],
                                      in_=pstr[kci % 2][:]), 1)
                ofeat_ac[i] = ac.n

                @blk.tensor
                def _(t):
                    if i == 0:
                        t.wait_ge(sw.h, wo_ready)
                        omm_pe = {}
                    t.wait_ge(ac.h, ofeat_ac[i])
                    t.wait_ge(ve.h, add_ve[i - 2])
                    for half in range(2):
                        for kci in range(8):
                            ins = t.matmul(
                                psw[par * 2 + half][:],
                                ofeat[par][:, kci * 128:(kci + 1) * 128],
                                wob[:, kci * D + half * 512: kci * D + (half + 1) * 512],
                                start=(kci == 0), stop=(kci == 7))
                        pe.inc(ins, 1)
                omm_pe = omm_pe if i > 0 else {}
                omm_pe[i] = pe.n
                mm_pe = pe.n

                @blk.vector
                def _(v):
                    v.wait_ge(pe.h, mm_pe)
                    v.wait_ge(self.sem["sx"].h, x_done[i])
                    v.wait_ge(ve.h, reuse_ve[i - 2])
                    for half in range(2):
                        ins = v.tensor_tensor(
                            out=ytile[par][:, half * 512:(half + 1) * 512],
                            in0=psw[par * 2 + half][:],
                            in1=xrow[par][:, half * 512:(half + 1) * 512],
                            op=ALU.add)
                    ve.inc(ins, 1)
                add_ve[i] = ve.n

                @blk.vector
                def _(v):
                    v.wait_ge(ve.h, add_ve[i])
                    ve.inc(v.tensor_reduce(stat[:, 0:1], ytile[par][:], AX.X, ALU.add), 1)
                    v.wait_ge(ve.h, ve.n)
                    ve.inc(v.tensor_scalar(stat[:, 1:2], stat[:, 0:1], -1.0 / D, None, ALU.mult), 1)
                    v.wait_ge(ve.h, ve.n)
                    ve.inc(v.tensor_scalar(ytile[par][:], ytile[par][:], stat[:, 1:2], None, ALU.add), 1)
                    v.wait_ge(ve.h, ve.n)
                    ve.inc(v.tensor_tensor(out=sqs[par][:], in0=ytile[par][:],
                                           in1=ytile[par][:], op=ALU.mult), 1)
                    v.wait_ge(ve.h, ve.n)
                    ve.inc(v.tensor_reduce(stat[:, 3:4], sqs[par][:], AX.X, ALU.add), 1)
                    v.wait_ge(ve.h, ve.n)
                    ve.inc(v.tensor_scalar(stat[:, 6:7], stat[:, 3:4], 1.0 / D, 1e-5,
                                           ALU.mult, ALU.add), 1)
                sq_ve = ve.n

                @blk.scalar
                def _(a):
                    a.wait_ge(ve.h, sq_ve)
                    ac.inc(a.sqrt(out=stat[:, 4:5], in_=stat[:, 6:7]), 1)
                rstd_ac = ac.n

                @blk.vector
                def _(v):
                    v.wait_ge(ac.h, rstd_ac)
                    ve.inc(v.reciprocal(stat[:, 5:6], stat[:, 4:5]), 1)
                    v.wait_ge(ve.h, ve.n)
                    ve.inc(v.tensor_scalar(ytile[par][:], ytile[par][:], stat[:, 5:6],
                                           None, ALU.mult), 1)
                    v.wait_ge(ve.h, ve.n)
                    ve.inc(v.tensor_copy(out=ybtile[par][:], in_=ytile[par][:]), 1)
                ln_ve = ve.n

                @blk.sync
                def _(sync):
                    sync.wait_ge(ve.h, ln_ve)
                    sx.inc(sync.dma_start(out=d["y1f"][i * 128:(i + 1) * 128, :],
                                          in_=ytile[par][:]), 16)
                    sx.inc(sync.dma_start(out=d["y1b"][i * 128:(i + 1) * 128, :],
                                          in_=ybtile[par][:]), 16)
                    if dbg_y1 is not None:
                        sx.inc(sync.dma_start(out=dbg_y1[i * 128:(i + 1) * 128, :],
                                              in_=ytile[par][:]), 16)
                    if i + 2 < NTILE:
                        sx.inc(sync.dma_start(out=xrow[par][:],
                                              in_=d["x"][(i + 2) * 128:(i + 3) * 128, :]), 16)
                        x_done[i + 2] = sx.n
                        sync.wait_ge(pe.h, otr_pe)
                        sx.inc(sync.dma_start(out=odrow[par][:],
                                              in_=d["od"][(i + 2) * 128:(i + 3) * 128, :]), 16)
                        od_done[i + 2] = sx.n
                st_sx = sx.n

                trc_ac0 = ac.n
                trp0 = pe.n

                @blk.tensor
                def _(t):
                    t.wait_ge(ve.h, ln_ve)
                    for kci in range(8):
                        if kci >= 2:
                            t.wait_ge(ac.h, trc_ac0 + kci - 1)
                        pe.inc(t.transpose(pstr[kci % 2][:],
                                           ybtile[par][:, kci * 128:(kci + 1) * 128],
                                           sb["identb"][:]), 1)

                @blk.scalar
                def _(a):
                    for kci in range(8):
                        a.wait_ge(pe.h, trp0 + kci + 1)
                        ac.inc(a.copy(
                            out=sb["y1fm"][:, kci * NTOK + i * 128: kci * NTOK + (i + 1) * 128],
                            in_=pstr[kci % 2][:]), 1)
                tr_ac = ac.n
                tr_done[i] = tr_ac

                @blk.vector
                def _(v):
                    v.wait_ge(self.sem["sx"].h, st_sx)
                    v.wait_ge(ac.h, tr_ac)
                    ve.inc(v.engine_nop(), 1)
                reuse_ve[i] = ve.n

            self.y1_sx = self.sem["sx"].n
            self.barrier()

    def vsync(self, v, ve):
        v.wait_ge(ve.h, ve.n)

    def routing(self):
        nc, sb, d = self.nc, self.sb, self.dram
        blk = self.block
        sx, sg, pe, ac, ve, sw, po = (self.sem[k] for k in
                                      ["sx", "sg", "pe", "ac", "ve", "sw", "po"])
        y1fm = sb["y1fm"]

        self.shared_ctx = ExitStack()
        self.sw_sb = {
            "swgb": self.sbuf(self.shared_ctx, "swgb", [128, 8 * D], BF16),
            "swub": self.sbuf(self.shared_ctx, "swub", [128, 8 * D], BF16),
            "swdb": self.sbuf(self.shared_ctx, "swdb", [128, 8 * D], BF16),
            "gsh": self.sbuf(self.shared_ctx, "gsh", [128, 4096], BF16),
            "hsh": self.sbuf(self.shared_ctx, "hsh", [128, 4096], BF16),
            "osh0": self.sbuf(self.shared_ctx, "osh", [128, D], BF16),
            "osh1": self.sbuf(self.shared_ctx, "osh", [128, D], BF16),
        }
        self.routing_ctx = ExitStack()
        ctx = self.routing_ctx
        ctok_all = self.sbuf(ctx, "ctok_all", [128, NTILE * 64], F32)

        wrt = self.sbuf(ctx, "wrt", [128, 512], BF16)
        ebc = self.sbuf(ctx, "ebc", [128, 64], F32)
        mT = self.sbuf(ctx, "mT", [64, NTOK], BF16)
        cX = self.sbuf(ctx, "cX", [64, NTOK], F32)
        msel_all = self.sbuf(ctx, "msel_all", [128, NTILE * 64], F32)
        coef_all = self.sbuf(ctx, "coef_all", [128, NTILE * 64], F32)
        sco = self.sbuf(ctx, "sco", [128, 64], F32)
        sfc = self.sbuf(ctx, "sfc", [128, 64], F32)
        g4 = self.sbuf(ctx, "g4", [128, 32], F32)
        g2 = self.sbuf(ctx, "g2", [128, 16], F32)
        m1 = self.sbuf(ctx, "m1", [128, 8], F32)
        m2 = self.sbuf(ctx, "m2", [128, 8], F32)
        w2 = self.sbuf(ctx, "w2", [128, 64], F32)
        gsum = self.sbuf(ctx, "gsum", [128, 8], F32)
        srt = self.sbuf(ctx, "srt", [128, 8], F32)
        rep8 = self.sbuf(ctx, "rep8", [128, 8], F32)
        g2r = self.sbuf(ctx, "g2r", [128, 8], F32)
        gsel = self.sbuf(ctx, "gsel", [128, 8], F32)
        msk = self.sbuf(ctx, "msk", [128, 64], F32)
        m8 = self.sbuf(ctx, "m8", [128, 8], F32)
        mrm = self.sbuf(ctx, "mrm", [128, 64], F32)
        wsel = self.sbuf(ctx, "wsel", [128, 64], F32)
        stat = self.sbuf(ctx, "rstat", [128, 8], F32)
        iota9 = self.sbuf(ctx, "iota9", [128, 1], F32)
        iota18 = self.sbuf(ctx, "iota18", [128, 8], F32)
        jbig = self.sbuf(ctx, "jbig", [128, 8], F32)
        js16 = self.sbuf(ctx, "js16", [128, 8], F32)
        iotat = self.sbuf(ctx, "iotat", [64, 128], F32)
        bk = self.sbuf(ctx, "bk", [128, 512], F32)
        bbc = self.sbuf(ctx, "bbc", [128, 64], F32)
        sfull = [self.sbuf(ctx, "sfull", [128, 64], F32) for _ in range(2)]
        incl = self.sbuf(ctx, "incl", [128, 64], F32)
        destf = self.sbuf(ctx, "destf", [128, 64], F32)
        jm = self.sbuf(ctx, "jm", [128, 64], F32)
        jm4 = self.sbuf(ctx, "jm4", [128, 64], F32)
        keyb = self.sbuf(ctx, "keyb", [128, 192], F32)
        k8 = self.sbuf(ctx, "k8", [128, 24], F32)
        stage = [self.sbuf(ctx, "stage", [128, 24], F32) for _ in range(2)]
        idxu = [self.sbuf(ctx, "idxu", [128, 8], U32) for _ in range(2)]

        dbg_coef = self.dbg("coef", [128, NTILE * 64], F32)
        dbg_msel = self.dbg("msel", [128, NTILE * 64], F32)
        dbg_cdest = self.dbg("cdest", [NSTREAM + 1, 3], F32)
        dbg_etab = self.dbg("etab", [1, 128], I32)
        dbg_mt = self.dbg("mT", [64, NTOK], F32)

        with ExitStack() as pctx:
            rws = self.sbuf(pctx, "rws", [64, D], F32)
            rwb = self.sbuf(pctx, "rwb", [64, D], BF16)
            ebr = self.sbuf(pctx, "ebr", [1, 64], F32)
            psR = [self.psum(pctx, "psR", [128, 64], F32) for _ in range(2)]
            pstr = [self.psum(pctx, "pstrR", [128, 128], F32) for _ in range(2)]
            psmix = self.psum(pctx, "psmix", [128, 128], F32)
            pstrb = [self.psum(pctx, "pstrb", [128, 64], BF16) for _ in range(2)]

            @blk.sync
            def _(sync):
                sx.inc(sync.dma_start(out=rws[:], in_=d["rw"][:, :]), 16)
                sx.inc(sync.dma_start(out=ebr[:], in_=d["ebias"][:, :]), 16)
            rw_sx = sx.n

            @blk.gpsimd
            def _(g):
                for wname, wt in [("swg", self.sw_sb["swgb"]), ("swu", self.sw_sb["swub"]),
                                  ("swd", self.sw_sb["swdb"])]:
                    for kc in range(8):
                        sw.inc(g.dma_start(
                            out=wt[:, kc * D:(kc + 1) * D],
                            in_=d[wname][kc * 128:(kc + 1) * 128, :]), 16)
                g.iota(iota9[:], pattern=[[0, 1]], base=0, channel_multiplier=9,
                       allow_small_or_imprecise_dtypes=True)
                g.iota(iota18[:], pattern=[[0, 8]], base=0, channel_multiplier=1,
                       allow_small_or_imprecise_dtypes=True)
                g.iota(jbig[:], pattern=[[-1, 8]], base=8,
                       channel_multiplier=0, allow_small_or_imprecise_dtypes=True)
                g.iota(js16[:], pattern=[[-16, 8]], base=128, channel_multiplier=0,
                       allow_small_or_imprecise_dtypes=True)
                po.inc(g.iota(iotat[:], pattern=[[1, 128]], base=0,
                              channel_multiplier=0,
                              allow_small_or_imprecise_dtypes=True), 1)
            iota_po = po.n
            self.sw_ready = sw.n

            @blk.vector
            def _(v):
                v.wait_ge(sx.h, rw_sx)
                v.wait_ge(po.h, iota_po)
                v.memset(rep8[:, 4:8], -1e30)
                ve.inc(v.tensor_scalar(jbig[:], jbig[:], 1048576.0, None, ALU.mult), 1)
                self.vsync(v, ve)
                ve.inc(v.tensor_copy(out=rwb[:], in_=rws[:]), 1)
            rwb_ve = ve.n

            rtr_ac0 = ac.n
            rtr_pe0 = pe.n

            @blk.tensor
            def _(t):
                t.wait_ge(ve.h, rwb_ve)
                for kc in range(8):
                    if kc >= 2:
                        t.wait_ge(ac.h, rtr_ac0 + kc - 1)
                    pe.inc(t.transpose(pstrb[kc % 2][:],
                                       rwb[:, kc * 128:(kc + 1) * 128],
                                       sb["identb"][0:64, 0:64]), 1)
                pe.inc(t.matmul(psmix[:, 0:64], sb["ones1p"][:], ebr[:],
                                start=True, stop=True), 1)

            @blk.scalar
            def _(a):
                for kc in range(8):
                    a.wait_ge(pe.h, rtr_pe0 + kc + 1)
                    ac.inc(a.copy(out=wrt[:, kc * 64:(kc + 1) * 64],
                                  in_=pstrb[kc % 2][:]), 1)
                a.wait_ge(pe.h, rtr_pe0 + 9)
                ac.inc(a.copy(out=ebc[:], in_=psmix[:, 0:64]), 1)
            setup_ac = ac.n

            sig_ac = {}
            scoused_ve = {}
            mT_ve = {}
            for i in range(NTILE):
                @blk.tensor
                def _(t):
                    t.wait_ge(ac.h, setup_ac)
                    if i >= 2:
                        t.wait_ge(ac.h, sig_ac[i - 2])
                    for kc in range(8):
                        ins = t.matmul(psR[i % 2][:],
                                       y1fm[:, kc * NTOK + i * 128: kc * NTOK + (i + 1) * 128],
                                       wrt[:, kc * 64:(kc + 1) * 64],
                                       start=(kc == 0), stop=(kc == 7))
                    pe.inc(ins, 1)
                r_pe = pe.n

                @blk.scalar
                def _(a):
                    a.wait_ge(pe.h, r_pe)
                    if i > 0:
                        a.wait_ge(ve.h, scoused_ve[i - 1])
                    ac.inc(a.activation(out=sco[:], in_=psR[i % 2][:], func=AF.Sigmoid), 1)
                sig_ac[i] = ac.n

                @blk.vector
                def _(v):
                    v.wait_ge(ac.h, sig_ac[i])
                    ve.inc(v.tensor_tensor(out=sfc[:], in0=sco[:], in1=ebc[:], op=ALU.add), 1)
                    self.vsync(v, ve)
                    v3 = sfc[:].rearrange("p (g e) -> p g e", g=8)
                    g43 = g4[:].rearrange("p (g e) -> p g e", g=8)
                    g23 = g2[:].rearrange("p (g e) -> p g e", g=8)
                    ve.inc(v.tensor_max(g43, v3[:, :, 0:4], v3[:, :, 4:8]), 1)
                    self.vsync(v, ve)
                    ve.inc(v.tensor_max(g23, g43[:, :, 0:2], g43[:, :, 2:4]), 1)
                    self.vsync(v, ve)
                    ve.inc(v.tensor_max(m1[:].rearrange("p (g e) -> p g e", g=8),
                                        g23[:, :, 0:1], g23[:, :, 1:2]), 1)
                    self.vsync(v, ve)
                    ve.inc(v.match_replace(out=w2[:], in_to_replace=m1[:],
                                           in_values=sfc[:], imm_value=-1e30), 1)
                    self.vsync(v, ve)
                    w23 = w2[:].rearrange("p (g e) -> p g e", g=8)
                    ve.inc(v.tensor_max(g43, w23[:, :, 0:4], w23[:, :, 4:8]), 1)
                    self.vsync(v, ve)
                    ve.inc(v.tensor_max(g23, g43[:, :, 0:2], g43[:, :, 2:4]), 1)
                    self.vsync(v, ve)
                    ve.inc(v.tensor_max(m2[:].rearrange("p (g e) -> p g e", g=8),
                                        g23[:, :, 0:1], g23[:, :, 1:2]), 1)
                    self.vsync(v, ve)
                    ve.inc(v.tensor_tensor(out=gsum[:], in0=m1[:], in1=m2[:], op=ALU.add), 1)
                    self.vsync(v, ve)
                    ve.inc(v.max(srt[:], gsum[:]), 1)
                    self.vsync(v, ve)
                    ve.inc(v.tensor_copy(out=rep8[:, 0:4], in_=srt[:, 0:4]), 1)
                    self.vsync(v, ve)
                    ve.inc(v.match_replace(out=g2r[:], in_to_replace=rep8[:],
                                           in_values=gsum[:], imm_value=-1e30), 1)
                    self.vsync(v, ve)
                    ve.inc(v.tensor_scalar(gsel[:], g2r[:], -1e29, None, ALU.is_lt), 1)
                    self.vsync(v, ve)
                    ve.inc(v.tensor_tensor(
                        out=msk[:].rearrange("p (g e) -> p g e", g=8),
                        in0=v3, in1=gsel[:, :, None].to_broadcast([128, 8, 8]),
                        op=ALU.mult), 1)
                    self.vsync(v, ve)
                    ve.inc(v.max(m8[:], msk[:]), 1)
                    self.vsync(v, ve)
                    ve.inc(v.match_replace(out=mrm[:], in_to_replace=m8[:],
                                           in_values=msk[:], imm_value=-1e30), 1)
                    self.vsync(v, ve)
                    ve.inc(v.tensor_scalar(msel_all[:, i * 64:(i + 1) * 64], mrm[:],
                                           -1e29, None, ALU.is_lt), 1)
                    self.vsync(v, ve)
                    ve.inc(v.tensor_tensor(out=wsel[:], in0=sco[:],
                                           in1=msel_all[:, i * 64:(i + 1) * 64],
                                           op=ALU.mult), 1)
                    scoused_ve[i] = ve.n
                    self.vsync(v, ve)
                    ve.inc(v.tensor_reduce(stat[:, 0:1], wsel[:], AX.X, ALU.add), 1)
                    self.vsync(v, ve)
                    ve.inc(v.tensor_scalar(stat[:, 1:2], stat[:, 0:1], 1e-20, None, ALU.add), 1)
                    self.vsync(v, ve)
                    ve.inc(v.reciprocal(stat[:, 2:3], stat[:, 1:2]), 1)
                    self.vsync(v, ve)
                    ve.inc(v.tensor_scalar(coef_all[:, i * 64:(i + 1) * 64], wsel[:],
                                           stat[:, 2:3], 2.5, ALU.mult, ALU.mult), 1)
                msel_ve = ve.n

                @blk.tensor
                def _(t):
                    t.wait_ge(ve.h, msel_ve)
                    if i >= 2:
                        t.wait_ge(ve.h, mT_ve[i - 2])
                    pe.inc(t.transpose(pstr[i % 2][0:64, :],
                                       msel_all[:, i * 64:(i + 1) * 64],
                                       sb["identf"][:]), 1)
                mseltp = pe.n

                @blk.vector
                def _(v):
                    v.wait_ge(pe.h, mseltp)
                    ve.inc(v.tensor_copy(out=mT[:, i * 128:(i + 1) * 128],
                                         in_=pstr[i % 2][0:64, :]), 1)
                mT_ve[i] = ve.n

            ones4k = self.sbuf(pctx, "ones4k", [64, NTOK], BF16)
            @blk.vector
            def _(v):
                ve.inc(v.memset(ones4k[:], 1.0), 1)
                self.vsync(v, ve)
                ve.inc(v.tensor_tensor_scan(cX[:], ones4k[:], mT[:], 0.0,
                                            ALU.mult, ALU.add), 1)
                self.vsync(v, ve)
                ve.inc(v.tensor_copy(out=bk[0:64, 0:1], in_=cX[0:64, NTOK - 1:NTOK]), 1)
                self.vsync(v, ve)
                ve.inc(v.tensor_tensor(out=cX[:], in0=cX[:], in1=mT[:], op=ALU.subtract), 1)
                self.vsync(v, ve)
                ve.inc(v.tensor_scalar(bk[0:64, 1:2], bk[0:64, 0:1], 511.0, None, ALU.add), 1)
                self.vsync(v, ve)
                ve.inc(v.tensor_scalar(bk[0:64, 2:3], bk[0:64, 1:2],
                                       1.0 / 512.0, -0.4990234375, ALU.mult, ALU.add), 1)
                self.vsync(v, ve)
                ve.inc(v.tensor_scalar(bk[0:64, 3:4], bk[0:64, 2:3], 8388608.0, None, ALU.add), 1)
                self.vsync(v, ve)
                ve.inc(v.tensor_scalar(bk[0:64, 4:5], bk[0:64, 3:4], -8388608.0, None, ALU.add), 1)
            tiles_ve = ve.n

            ctt_pe0 = pe.n
            ctt_ve0 = ve.n

            @blk.tensor
            def _(t):
                t.wait_ge(ve.h, tiles_ve)
                for i in range(NTILE):
                    if i >= 2:
                        t.wait_ge(ve.h, ctt_ve0 + i - 1)
                    pe.inc(t.transpose(pstr[i % 2][:, 0:64], cX[:, i * 128:(i + 1) * 128],
                                       sb["identf"][0:64, 0:64]), 1)

            @blk.vector
            def _(v):
                for i in range(NTILE):
                    v.wait_ge(pe.h, ctt_pe0 + i + 1)
                    ve.inc(v.tensor_copy(out=ctok_all[:, i * 64:(i + 1) * 64],
                                         in_=pstr[i % 2][:, 0:64]), 1)
            ctok_ve = ve.n

            @blk.tensor
            def _(t):
                t.wait_ge(ve.h, tiles_ve)
                pe.inc(t.matmul(psmix[0:1, 0:64], bk[0:64, 4:5],
                                sb["identf"][0:64, 0:64], start=True, stop=True), 1)
            trow_pe = pe.n

            @blk.vector
            def _(v):
                v.wait_ge(pe.h, trow_pe)
                ve.inc(v.tensor_copy(out=bk[0:1, 16:80], in_=psmix[0:1, 0:64]), 1)
                self.vsync(v, ve)
                ve.inc(v.tensor_tensor_scan(bk[0:1, 80:144], sb["ones64"][0:1, 0:64],
                                            bk[0:1, 16:80], 0.0, ALU.mult, ALU.add), 1)
                self.vsync(v, ve)
                ve.inc(v.tensor_tensor(out=bk[0:1, 144:208], in0=bk[0:1, 80:144],
                                       in1=bk[0:1, 16:80], op=ALU.subtract), 1)
                self.vsync(v, ve)
                ve.inc(v.tensor_scalar(bk[0:1, 208:272], bk[0:1, 144:208],
                                       512.0, None, ALU.mult), 1)
            brow_ve = ve.n

            @blk.tensor
            def _(t):
                t.wait_ge(ve.h, brow_ve)
                pe.inc(t.matmul(psmix[0:64, 64:65], bk[0:1, 144:208], sb["ones11"][:],
                                start=True, stop=True), 1)
            scol_pe = pe.n

            @blk.vector
            def _(v):
                v.wait_ge(pe.h, scol_pe)
                v.wait_ge(po.h, iota_po)
                ve.inc(v.tensor_copy(out=bk[0:64, 5:6], in_=psmix[0:64, 64:65]), 1)
                self.vsync(v, ve)
                ve.inc(v.tensor_scalar(bk[0:64, 384:512], iotat[:], bk[0:64, 5:6],
                                       None, ALU.is_ge), 1)
            A_ve = ve.n

            @blk.tensor
            def _(t):
                t.wait_ge(ve.h, A_ve)
                pe.inc(t.matmul(pstr[0][:, 0:1], bk[0:64, 384:512], sb["onesc"][:],
                                start=True, stop=True), 1)
            ecol_pe = pe.n

            @blk.vector
            def _(v):
                v.wait_ge(pe.h, ecol_pe)
                ve.inc(v.tensor_copy(out=bk[:, 6:7], in_=pstr[0][:, 0:1]), 1)
            ecol_ve = ve.n

            @blk.tensor
            def _(t):
                t.wait_ge(ve.h, ecol_ve)
                pe.inc(t.matmul(psmix[0:1, 0:128], bk[:, 6:7], sb["identf"][:],
                                start=True, stop=True), 1)
            erow_pe = pe.n

            @blk.vector
            def _(v):
                v.wait_ge(pe.h, erow_pe)
                ve.inc(v.tensor_scalar(bk[0:1, 272:400], psmix[0:1, 0:128],
                                       -1.0, None, ALU.add), 1)
                self.vsync(v, ve)
                ve.inc(v.tensor_copy(out=self.etab[:], in_=bk[0:1, 272:400]), 1)
                ve.inc(v.tensor_copy(out=self.etabf[:], in_=bk[0:1, 272:400]), 1)
                self.vsync(v, ve)
                ve.inc(v.tensor_scalar(bk[0:1, 384:512], iotat[0:1, 0:128],
                                       bk[0:1, 143:144], None, ALU.is_lt), 1)
                ve.inc(v.memset(self.zi32[:], 0.0), 1)
                self.vsync(v, ve)
                ve.inc(v.tensor_copy(out=self.alive[:], in_=bk[0:1, 384:512]), 1)
            etab_ve = ve.n
            self.alive_ve = etab_ve

            @blk.tensor
            def _(t):
                t.wait_ge(ve.h, etab_ve)
                pe.inc(t.matmul(psmix[:, 0:64], sb["ones1p"][:],
                                bk[0:1, 208:272], start=True, stop=True), 1)
            bbc_pe = pe.n

            @blk.vector
            def _(v):
                v.wait_ge(pe.h, bbc_pe)
                ve.inc(v.tensor_copy(out=bbc[:], in_=psmix[:, 0:64]), 1)
            bbc_ve = ve.n

            fcol = bk[0:64, 8:9]
            rmat = cX[0:64, 0:128]
            frow = keyb[0:1, 0:128]
            psfa = psmix[0:64, 0:128]
            psfb = pstr[0][0:1, 0:128]

            @blk.vector
            def _(v):
                v.wait_ge(ve.h, bbc_ve)
                ve.inc(v.tensor_scalar(fcol, bk[0:64, 4:5],
                                       -512.0, 512.0, ALU.mult, ALU.add), 1)
                self.vsync(v, ve)
                ve.inc(v.tensor_tensor(out=fcol, in0=bk[0:64, 0:1],
                                       in1=fcol, op=ALU.add), 1)
                self.vsync(v, ve)
                ve.inc(v.tensor_scalar(fcol, fcol, 128.5, None, ALU.is_le), 1)
            esm_ve = ve.n

            @blk.tensor
            def _(t):
                t.wait_ge(ve.h, esm_ve)
                pe.inc(t.matmul(psfa[:, :], sb["identf"][0:64, 1:65],
                                bk[0:64, 384:512], start=True, stop=True), 1)
            aup_pe = pe.n

            @blk.vector
            def _(v):
                v.wait_ge(pe.h, aup_pe)
                ve.inc(v.tensor_tensor(out=rmat, in0=bk[0:64, 384:512],
                                       in1=psfa[:, :], op=ALU.subtract), 1)
            rmat_ve = ve.n

            @blk.tensor
            def _(t):
                t.wait_ge(ve.h, rmat_ve)
                pe.inc(t.matmul(psfb[:, :], fcol, rmat,
                                start=True, stop=True), 1)
            gfast_pe = pe.n

            @blk.vector
            def _(v):
                v.wait_ge(pe.h, gfast_pe)
                ve.inc(v.tensor_tensor(out=frow[0:1, 0:127], in0=bk[0:1, 273:400],
                                       in1=bk[0:1, 272:399], op=ALU.not_equal), 1)
                ve.inc(v.memset(frow[0:1, 127:128], 1.0), 1)
                self.vsync(v, ve)
                ve.inc(v.tensor_tensor(out=frow, in0=psfb[0:1, :],
                                       in1=frow, op=ALU.mult), 1)
                self.vsync(v, ve)
                ve.inc(v.tensor_tensor(out=frow, in0=frow,
                                       in1=bk[0:1, 384:512], op=ALU.mult), 1)
                self.vsync(v, ve)
                ve.inc(v.tensor_copy(out=self.fastf[:], in_=frow), 1)
            self.alive_ve = ve.n

            if dbg_etab is not None:
                @blk.sync
                def _(sync):
                    sync.wait_ge(ve.h, etab_ve)
                    sx.inc(sync.dma_start(out=dbg_etab[:, :], in_=self.etab[:]), 16)
            if dbg_mt is not None:
                @blk.sync
                def _(sync):
                    sync.wait_ge(ve.h, bbc_ve)
                    sx.inc(sync.dma_start(out=dbg_mt[:, :], in_=cX[:]), 16)

            scat_sg = {-3: sg.n, -2: sg.n, -1: sg.n}
            gqB = self.sem["gq"]
            rgate = {-2: 0, -1: 0}
            sfB_ve = {}
            for i in range(NTILE):
                par = i % 2

                @blk.vector
                def _(v):
                    v.wait_ge(ve.h, bbc_ve)
                    if rgate[i - 2] > 0:
                        v.wait_ge(gqB.h, rgate[i - 2])
                    ve.inc(v.tensor_tensor(out=sfull[par][:],
                                           in0=ctok_all[:, i * 64:(i + 1) * 64],
                                           in1=bbc[:], op=ALU.add), 1)
                    self.vsync(v, ve)
                    msl = msel_all[:, i * 64:(i + 1) * 64]
                    ve.inc(v.tensor_tensor_scan(incl[:], sb["ones64"][:], msl, 0.0,
                                                ALU.mult, ALU.add), 1)
                    self.vsync(v, ve)
                    ve.inc(v.tensor_scalar(destf[:], incl[:], iota9[:, 0:1],
                                           float(i * 128 * 9 - 1), ALU.add, ALU.add), 1)
                    self.vsync(v, ve)
                    ve.inc(v.tensor_scalar(jm[:], incl[:], -1.0, 9.0, ALU.mult, ALU.add), 1)
                    self.vsync(v, ve)
                    ve.inc(v.tensor_tensor(out=jm[:], in0=jm[:], in1=msl, op=ALU.mult), 1)
                    self.vsync(v, ve)
                    ve.inc(v.scalar_tensor_tensor(keyb[:, 0:64], jm[:], 1048576.0,
                                                  sfull[par][:], ALU.mult, ALU.add), 1)
                    ve.inc(v.scalar_tensor_tensor(keyb[:, 64:128], jm[:], 1048576.0,
                                                  destf[:], ALU.mult, ALU.add), 1)
                    ve.inc(v.tensor_scalar(jm4[:], jm[:], 4.0, None, ALU.mult), 1)
                    self.vsync(v, ve)
                    ve.inc(v.scalar_tensor_tensor(keyb[:, 128:192],
                                                  coef_all[:, i * 64:(i + 1) * 64], 0.25,
                                                  jm4[:], ALU.mult, ALU.add), 1)
                    self.vsync(v, ve)
                    ve.inc(v.max(k8[:, 0:8], keyb[:, 0:64]), 1)
                    ve.inc(v.max(k8[:, 8:16], keyb[:, 64:128]), 1)
                    ve.inc(v.max(k8[:, 16:24], keyb[:, 128:192]), 1)
                    self.vsync(v, ve)
                    st3 = stage[par][:].rearrange("p (j t) -> p j t", t=3)
                    ve.inc(v.tensor_scalar(st3[:, :, 0:1].rearrange("p j t -> p (j t)"),
                                           iota18[:], float(i * 128), None, ALU.add), 1)
                    ve.inc(v.tensor_tensor(out=st3[:, :, 1:2].rearrange("p j t -> p (j t)"),
                                           in0=k8[:, 8:16], in1=jbig[:], op=ALU.subtract), 1)
                    ve.inc(v.scalar_tensor_tensor(st3[:, :, 2:3].rearrange("p j t -> p (j t)"),
                                                  k8[:, 16:24], 4.0, js16[:],
                                                  ALU.mult, ALU.subtract), 1)
                    self.vsync(v, ve)
                    ve.inc(v.tensor_tensor(out=k8[:, 0:8], in0=k8[:, 0:8],
                                           in1=jbig[:], op=ALU.subtract), 1)
                    self.vsync(v, ve)
                    ve.inc(v.tensor_scalar(k8[:, 0:8], k8[:, 0:8],
                                           0.0, float(NSTREAM), ALU.max, ALU.min), 1)
                    self.vsync(v, ve)
                    ve.inc(v.tensor_copy(out=idxu[par][:], in_=k8[:, 0:8]), 1)
                sfB_ve[i] = ve.n

                if not self.no_scatter:
                    @blk.gpsimd
                    def _(g):
                        g.wait_ge(ve.h, sfB_ve[i])
                        for j in range(8):
                            sg.inc(g.indirect_dma_start(
                                out=d["cdest"][:, :], out_offset=IOA(idxu[par][:, j:j + 1], 0),
                                in_=stage[par][:, j * 3:(j + 1) * 3], in_offset=None), 16)
                        g.wait_ge(sg.h, sg.n)
                        g.sem_inc(gqB.h, 1)
                    gqB.n += 1
                rgate[i] = gqB.n
                scat_sg[i] = sg.n

            if dbg_coef is not None:
                self.debug.add("msel")
                dbg_msel = self.dbg("msel", [128, NTILE * 64], F32)
                @blk.sync
                def _(sync):
                    sync.wait_ge(ve.h, sfB_ve[NTILE - 1])
                    sx.inc(sync.dma_start(out=dbg_coef[:, :], in_=coef_all[:]), 16)
                    sx.inc(sync.dma_start(out=dbg_msel[:, :], in_=msel_all[:]), 16)
            if dbg_cdest is not None:
                @blk.sync
                def _(sync):
                    sync.wait_ge(sg.h, scat_sg[NTILE - 1])
                    cdf = d["cdest"].rearrange("a b -> (a b)")
                    dbf = dbg_cdest.rearrange("a b -> (a b)")
                    sx.inc(sync.dma_start(
                        out=dbf[0:NSTREAM * 3].rearrange("(p f) -> p f", p=128),
                        in_=cdf[0:NSTREAM * 3].rearrange("(p f) -> p f", p=128)), 16)
                    sx.inc(sync.dma_start(out=dbf[NSTREAM * 3:NSTREAM * 3 + 3],
                                          in_=cdf[NSTREAM * 3:NSTREAM * 3 + 3]), 16)

        self.shared_gate = {"ve": max(bbc_ve, ctok_ve), "ac": sig_ac[NTILE - 1],
                            "sw": self.sw_ready}

    def shared_ffn(self):
        nc, sb, d = self.nc, self.sb, self.dram
        blk = self.block
        sx, sg, pe, ac, ve, sw = (self.sem[k] for k in ["sx", "sg", "pe", "ac", "ve", "sw"])
        y1fm = sb["y1fm"]
        swgb, swub, swdb = self.sw_sb["swgb"], self.sw_sb["swub"], self.sw_sb["swdb"]
        gsh, hsh = self.sw_sb["gsh"], self.sw_sb["hsh"]
        osh = [self.sw_sb["osh0"], self.sw_sb["osh1"]]

        with ExitStack() as ctx:
            psF = [self.psum(ctx, "psF", [128, 512], F32) for _ in range(4)]

            silu_ac = {}
            h_ve = {}
            o_ve = {}
            o_sx = {}
            cqS = self.sem["cq"]
            shgate = {}

            for tau in range(8):
                for ic in range(8):
                    gslice = gsh[:, ic * 512:(ic + 1) * 512]
                    hslice = hsh[:, ic * 512:(ic + 1) * 512]

                    @blk.tensor
                    def _(t):
                        if tau == 0 and ic == 0:
                            t.wait_ge(sw.h, self.shared_gate["sw"])
                            t.wait_ge(ve.h, self.shared_gate["ve"])
                            t.wait_ge(ac.h, self.shared_gate["ac"])
                        prev = (tau, ic - 2) if ic >= 2 else (tau - 1, ic + 6)
                        if prev in silu_ac:
                            t.wait_ge(ac.h, silu_ac[prev])
                        if prev in h_ve:
                            t.wait_ge(ve.h, h_ve[prev])
                        if ic < 2 and tau > 0:
                            t.wait_ge(ve.h, max(o_ve[(tau - 1, 2)], o_ve[(tau - 1, 3)]))
                        for kc in range(8):
                            ins = t.matmul(psF[2 * (ic % 2)][:],
                                           swgb[:, kc * D + ic * 128: kc * D + (ic + 1) * 128],
                                           y1fm[:, kc * NTOK + tau * 512: kc * NTOK + (tau + 1) * 512],
                                           start=(kc == 0), stop=(kc == 7))
                        pe.inc(ins, 1)
                        for kc in range(8):
                            ins = t.matmul(psF[2 * (ic % 2) + 1][:],
                                           swub[:, kc * D + ic * 128: kc * D + (ic + 1) * 128],
                                           y1fm[:, kc * NTOK + tau * 512: kc * NTOK + (tau + 1) * 512],
                                           start=(kc == 0), stop=(kc == 7))
                        pe.inc(ins, 1)
                    gu_pe = pe.n

                    @blk.scalar
                    def _(a):
                        a.wait_ge(pe.h, gu_pe)
                        ac.inc(a.activation(out=gslice, in_=psF[2 * (ic % 2)][:],
                                            func=AF.Silu), 1)
                    silu_ac[(tau, ic)] = ac.n

                    @blk.vector
                    def _(v):
                        v.wait_ge(ac.h, silu_ac[(tau, ic)])
                        ve.inc(v.tensor_tensor(out=hslice, in0=psF[2 * (ic % 2) + 1][:],
                                               in1=gslice, op=ALU.mult), 1)
                    h_ve[(tau, ic)] = ve.n

                for sub in range(4):
                    par = sub % 2

                    @blk.tensor
                    def _(t):
                        t.wait_ge(ve.h, h_ve[(tau, 7)])
                        prev = (tau, sub - 2) if sub >= 2 else (tau - 1, sub + 2)
                        if prev in o_ve:
                            t.wait_ge(ve.h, o_ve[prev])
                        for mh in range(2):
                            for ic in range(8):
                                ins = t.matmul(
                                    psF[2 * par + mh][:],
                                    hsh[:, ic * 512 + sub * 128: ic * 512 + (sub + 1) * 128],
                                    swdb[:, ic * D + mh * 512: ic * D + (mh + 1) * 512],
                                    start=(ic == 0), stop=(ic == 7))
                            pe.inc(ins, 1)
                    dd_pe = pe.n

                    @blk.vector
                    def _(v):
                        v.wait_ge(pe.h, dd_pe)
                        prev = (tau, sub - 2) if sub >= 2 else (tau - 1, sub + 2)
                        if prev in shgate:
                            v.wait_ge(cqS.h, shgate[prev])
                        for mh in range(2):
                            ins = v.tensor_copy(out=osh[par][:, mh * 512:(mh + 1) * 512],
                                                in_=psF[2 * par + mh][:])
                        ve.inc(ins, 1)
                    o_ve[(tau, sub)] = ve.n

                    @blk.sync
                    def _(sync):
                        sync.wait_ge(ve.h, o_ve[(tau, sub)])
                        r0 = (tau * 512 + sub * 128) * 9 + 8
                        sx.inc(sync.dma_start(
                            out=d["slots"][r0: r0 + 128 * 9: 9, :],
                            in_=osh[par][:]), 16)
                        sync.wait_ge(sx.h, sx.n)
                        sync.sem_inc(cqS.h, 1)
                    cqS.n += 1
                    shgate[(tau, sub)] = cqS.n
                    o_sx[(tau, sub)] = sx.n

        self.barrier()

    def moe(self):
        nc, sb, d = self.nc, self.sb, self.dram
        blk = self.block
        sx, sg, pe, ac, ve, sw = (self.sem[k] for k in ["sx", "sg", "pe", "ac", "ve", "sw"])
        gq = self.sem["gq"]
        cq = self.sem["cq"]
        gg = self.sem["gg"]
        gate = {}
        cdgate = {}
        self.routing_ctx.close()
        self.routing_ctx = None
        self.shared_ctx.close()
        self.y1fm_ctx.close()

        wgv = d["wgc"]
        wuv = d["wuc"]
        wdv = d["wdc"]

        with ExitStack() as ctx:
            wgb = [self.sbuf(ctx, "wgb", [128, 8 * EI], BF16) for _ in range(3)]
            wub = [self.sbuf(ctx, "wub", [128, 8 * EI], BF16) for _ in range(3)]
            wdb = [self.sbuf(ctx, "wdb", [128, 4 * D], BF16) for _ in range(3)]
            cdr4 = [self.sbuf(ctx, "cdr4", [4, 384], F32) for _ in range(4)]
            idxu = [self.sbuf(ctx, "idxuM", [128, 4], U32) for _ in range(4)]
            destu = [self.sbuf(ctx, "destu", [128, 4], U32) for _ in range(4)]
            coefc = [self.sbuf(ctx, "coefc", [128, 4], F32) for _ in range(4)]
            xg = [self.sbuf(ctx, "xg", [128, 4 * D], BF16) for _ in range(2)]
            xgf = [self.sbuf(ctx, "xgf", [128, 8 * 512], BF16) for _ in range(2)]
            gsl = [self.sbuf(ctx, "gsl", [128, 4 * 512], BF16) for _ in range(2)]
            hb = [self.sbuf(ctx, "hb", [128, 4 * 512], BF16) for _ in range(2)]
            ot = [self.sbuf(ctx, "ot", [128, 4 * D], BF16) for _ in range(2)]
            cd_sx = {}
            w_sg = {-3: sg.n, -2: sg.n, -1: sg.n}
            cd_pe = {}
            cd_ve = {}
            gath_sg = {-3: sg.n, -2: sg.n, -1: sg.n}
            tr_pe = {-3: 0, -2: 0, -1: 0}
            xgf_ac = {-3: 0, -2: 0, -1: 0}
            gu_pe = {-3: 0, -2: 0, -1: 0}
            h_ve = {-3: 0, -2: 0, -1: 0}
            d_pe = {-3: 0, -2: 0, -1: 0}
            ev_ve = {-3: 0, -2: 0, -1: 0}
            scat_sg = {-3: sg.n, -2: sg.n, -1: sg.n}

            base_sg = sg.n
            regs_ctx = ExitStack()
            erA = regs_ctx.enter_context(nc.gpsimd.register("erA"))
            erB = regs_ctx.enter_context(nc.gpsimd.register("erB"))
            erGz = regs_ctx.enter_context(nc.gpsimd.register("erGz"))
            erG2 = regs_ctx.enter_context(nc.gpsimd.register("erG2"))
            erT = regs_ctx.enter_context(nc.tensor.register("erT"))
            erTz = regs_ctx.enter_context(nc.tensor.register("erTz"))
            erT2 = regs_ctx.enter_context(nc.tensor.register("erT2"))
            erV = regs_ctx.enter_context(nc.vector.register("erV"))
            erVz = regs_ctx.enter_context(nc.vector.register("erVz"))
            erV2 = regs_ctx.enter_context(nc.vector.register("erV2"))
            erC = regs_ctx.enter_context(nc.scalar.register("erC"))
            erCz = regs_ctx.enter_context(nc.scalar.register("erCz"))
            erC2 = regs_ctx.enter_context(nc.scalar.register("erC2"))
            av = self.alive
            fst = self.fastf
            av_ve = self.alive_ve

            @blk.gpsimd
            def _(g):
                g.wait_ge(ve.h, av_ve)
                g.reg_load(erGz, self.zi32[0:1, 0:1])

            @blk.tensor
            def _(t):
                t.wait_ge(ve.h, av_ve)
                t.reg_load(erTz, self.zi32[0:1, 0:1])

            @blk.vector
            def _(v):
                v.wait_ge(ve.h, av_ve)
                v.reg_load(erVz, self.zi32[0:1, 0:1])

            @blk.scalar
            def _(a):
                a.wait_ge(ve.h, av_ve)
                a.reg_load(erCz, self.zi32[0:1, 0:1])
            iota1c = self.sbuf(ctx, "iota1c", [128, 1], F32)
            widx_all = self.sbuf(ctx, "widx_all", [128, NSLOT], U32)
            widx_f = self.sbuf(ctx, "widx_f", [128, NSLOT], F32)
            widf = self.sbuf(ctx, "widf", [128, 8], F32)
            po = self.sem["po"]
            io = self.sem["io"]

            @blk.gpsimd
            def _(g):
                io.inc(g.iota(iota1c[:], pattern=[[0, 1]], base=0, channel_multiplier=1,
                              allow_small_or_imprecise_dtypes=True), 1)
            iw_io = io.n

            with ExitStack() as wctx:
                pswx = self.psum(wctx, "pswx", [128, NSLOT], F32)

                @blk.tensor
                def _(t):
                    pe.inc(t.matmul(pswx[:], sb["ones1p"][:], self.etabf[:],
                                    start=True, stop=True), 1)
                wx_pe = pe.n

                @blk.vector
                def _(v):
                    v.wait_ge(pe.h, wx_pe)
                    v.wait_ge(io.h, iw_io)
                    ve.inc(v.tensor_scalar(widx_f[:], pswx[:], 128.0, None, ALU.mult), 1)
                    self.vsync(v, ve)
                    ve.inc(v.tensor_scalar(widx_f[:], widx_f[:], iota1c[:, 0:1],
                                           None, ALU.add), 1)
                    self.vsync(v, ve)
                    ve.inc(v.tensor_copy(out=widx_all[:], in_=widx_f[:]), 1)
                widx_ve = ve.n

            pscd = [self.psum(ctx, "pscd", [128, 16], F32) for _ in range(2)]
            pstr = [self.psum(ctx, "pstrM", [128, 128], BF16) for _ in range(2)]
            psF = [self.psum(ctx, "psFM", [128, 512], F32) for _ in range(4)]

            def stage_cdrow(s):
                par4 = s % 4

                @blk.sync
                def _(sync):
                    if s == 0:
                        sync.wait_ge(sg.h, base_sg)
                    if s >= 4:
                        sync.wait_ge(pe.h, cd_pe[s - 4])
                    sx.inc(sync.dma_start(
                        out=cdr4[par4][:, :],
                        in_=d["cdest"][s * 512:(s + 1) * 512, :].rearrange(
                            "(a b) t -> a (b t)", a=4)), 16)
                    sync.wait_ge(sx.h, sx.n)
                    sync.sem_inc(cq.h, 1)
                cq.n += 1
                cdgate[s] = cq.n
                cd_sx[s] = sx.n

            def stage_idx(s):
                par4 = s % 4

                @blk.tensor
                def _(t):
                    t.wait_ge(cq.h, cdgate[s])
                    if s < 2:
                        t.wait_ge(ve.h, widx_ve)
                    if s >= 2:
                        t.wait_ge(ve.h, cd_ve[s - 2])
                    for f in range(3):
                        ins = t.matmul(pscd[s % 2][:, f * 4:(f + 1) * 4],
                                       cdr4[par4][:, f: 384: 3],
                                       sb["identf"][0:4, 0:4],
                                       start=True, stop=True)
                    pe.inc(ins, 1)
                cd_pe[s] = pe.n

                @blk.vector
                def _(v):
                    v.wait_ge(pe.h, cd_pe[s])
                    prot = s - 1 if s % 2 == 0 else s - 2
                    if gate.get(prot, 0) > 0:
                        v.wait_ge(gq.h, gate[prot])
                    ve.inc(v.tensor_scalar(widf[:, 0:4], pscd[s % 2][:, 0:4],
                                           0.0, float(NTOK - 1), ALU.max, ALU.min), 1)
                    self.vsync(v, ve)
                    ve.inc(v.tensor_copy(out=idxu[par4][:], in_=widf[:, 0:4]), 1)
                    ve.inc(v.tensor_scalar(widf[:, 4:8], pscd[s % 2][:, 4:8],
                                           0.0, float(SLROWS - 1), ALU.max, ALU.min), 1)
                    self.vsync(v, ve)
                    ve.inc(v.tensor_copy(out=destu[par4][:], in_=widf[:, 4:8]), 1)
                    ve.inc(v.tensor_copy(out=coefc[par4][:], in_=pscd[s % 2][:, 8:12]), 1)
                cd_ve[s] = ve.n

            def stage_wload(s):
                par3 = s % 3

                @blk.gpsimd
                def _(g):
                    if s == 0:
                        g.wait_ge(sg.h, base_sg)
                        g.wait_ge(ve.h, widx_ve)
                        g.wait_ge(self.sem["po"].h, self.wcache_po)
                    g.wait_ge(pe.h, gu_pe[s - 3])
                    g.wait_ge(pe.h, d_pe[s - 3])

                    def emit_wload(g):
                        sg.inc(g.indirect_dma_start(
                            out=wgb[par3][:, :], out_offset=None,
                            in_=wgv[:, :], in_offset=IOA(widx_all[:, s:s + 1], 0)), 16)
                        sg.inc(g.indirect_dma_start(
                            out=wub[par3][:, :], out_offset=None,
                            in_=wuv[:, :], in_offset=IOA(widx_all[:, s:s + 1], 0)), 16)
                        sg.inc(g.indirect_dma_start(
                            out=wdb[par3][:, :], out_offset=None,
                            in_=wdv[:, :], in_offset=IOA(widx_all[:, s:s + 1], 0)), 16)

                    if s < 3:
                        emit_wload(g)
                    else:
                        g.reg_load(erA, self.etab[0:1, s:s + 1])
                        g.reg_load(erB, self.etab[0:1, s - 3:s - 2])
                        n0 = sg.n
                        with g.If_ne(erA, erB):
                            emit_wload(g)
                        delta = sg.n - n0
                        with g.Else():
                            g.sem_inc(sg.h, delta)
                w_sg[s] = sg.n

            def stage_gather(s):
                par = s % 2
                par4 = s % 4

                @blk.gpsimd
                def _(g):
                    if s == 0:
                        g.wait_ge(cq.h, cdgate[0])
                    g.reg_load(erA, av[0:1, s:s + 1])
                    g.reg_load(erG2, fst[0:1, s:s + 1])
                    n0 = gg.n
                    with g.If_ne(erA, erGz):
                        g.wait_ge(ve.h, cd_ve[s])
                        g.wait_ge(pe.h, tr_pe[s - 2])
                        with g.If_ne(erG2, erGz):
                            g.indirect_dma_start(
                                out=xg[par][:, 0:D], out_offset=None,
                                in_=d["y1b"][:, :],
                                in_offset=IOA(idxu[par4][:, 0:1], 0)
                            ).then_inc(gg.h, 16)
                            g.sem_inc(gg.h, 48)
                        with g.Else():
                            for sub in range(4):
                                gg.inc(g.indirect_dma_start(
                                    out=xg[par][:, sub * D:(sub + 1) * D], out_offset=None,
                                    in_=d["y1b"][:, :],
                                    in_offset=IOA(idxu[par4][:, sub:sub + 1], 0)), 16)
                    delta = gg.n - n0
                    with g.Else():
                        g.sem_inc(gg.h, delta)
                    if s % 2 == 1:
                        g.wait_ge(gg.h, gg.n)
                        g.sem_inc(gq.h, 1)
                if s % 2 == 1:
                    gq.n += 1
                    gate[s] = gq.n
                gath_sg[s] = sg.n

            def stage_compute(s):
                par = s % 2
                par3 = s % 3
                par4 = s % 4

                if s == 0:
                    self.barrier()
                pe0 = pe.n
                ve0 = ve.n
                ac0 = ac.n
                silu_s = {ic: ac0 + ic + 1 for ic in range(4)}
                hh_s = {ic: ve0 + 33 + ic for ic in range(4)}
                evv = {sub: ve0 + 37 + sub for sub in range(4)}
                dd_s = {sub: pe0 + 40 + 2 * (sub + 1) for sub in range(4)}

                @blk.tensor
                def _(t):
                    t.reg_load(erT, av[0:1, s:s + 1])
                    t.reg_load(erT2, fst[0:1, s:s + 1])
                    with t.If_ne(erT, erTz):
                        t.wait_ge(gq.h, gate[s if s % 2 == 1 else s + 1])
                        t.wait_ge(ve.h, xgf_ac[s - 1])
                        with t.If_ne(erT2, erTz):
                            for i in range(8):
                                if i >= 2:
                                    t.wait_ge(ve.h, ve0 + i - 1)
                                t.transpose(
                                    pstr[i % 2][:],
                                    xg[par][:, i * 128:(i + 1) * 128],
                                    sb["identb"][:]).then_inc(pe.h, 1)
                            for ic in range(4):
                                if ic == 0:
                                    t.wait_ge(ve.h, ve0 + 32)
                                    t.wait_ge(ve.h, ev_ve[s - 1])
                                if ic >= 2:
                                    t.wait_ge(ac.h, silu_s[ic - 2])
                                    t.wait_ge(ve.h, hh_s[ic - 2])
                                for kc in range(8):
                                    ins = t.matmul(
                                        psF[2 * (ic % 2)][:, 0:128],
                                        wgb[par3][:, kc * EI + ic * 128: kc * EI + (ic + 1) * 128],
                                        xgf[par][:, kc * 512: kc * 512 + 128],
                                        start=(kc == 0), stop=(kc == 7))
                                ins.then_inc(pe.h, 1)
                                for kc in range(8):
                                    ins = t.matmul(
                                        psF[2 * (ic % 2) + 1][:, 0:128],
                                        wub[par3][:, kc * EI + ic * 128: kc * EI + (ic + 1) * 128],
                                        xgf[par][:, kc * 512: kc * 512 + 128],
                                        start=(kc == 0), stop=(kc == 7))
                                ins.then_inc(pe.h, 1 if ic < 3 else 17)
                            t.wait_ge(ve.h, hh_s[3])
                            for mh in range(2):
                                for ic in range(4):
                                    ins = t.matmul(
                                        psF[mh][:],
                                        hb[par][:, ic * 512: ic * 512 + 128],
                                        wdb[par3][:, ic * D + mh * 512: ic * D + (mh + 1) * 512],
                                        start=(ic == 0), stop=(ic == 3))
                                ins.then_inc(pe.h, 1 if mh == 0 else 15)
                        tEl = t.Else()
                        tEl.__enter__()
                        for i in range(32):
                            sub, kc = divmod(i, 8)
                            if i >= 2:
                                t.wait_ge(ve.h, ve0 + i - 1)
                            pe.inc(t.transpose(
                                pstr[i % 2][:],
                                xg[par][:, sub * D + kc * 128: sub * D + (kc + 1) * 128],
                                sb["identb"][:]), 1)
                        for ic in range(4):
                            if ic == 0:
                                t.wait_ge(ve.h, ve0 + 32)
                                t.wait_ge(ve.h, ev_ve[s - 1])
                            if ic >= 2:
                                t.wait_ge(ac.h, silu_s[ic - 2])
                                t.wait_ge(ve.h, hh_s[ic - 2])
                            for kc in range(8):
                                ins = t.matmul(psF[2 * (ic % 2)][:],
                                               wgb[par3][:, kc * EI + ic * 128: kc * EI + (ic + 1) * 128],
                                               xgf[par][:, kc * 512:(kc + 1) * 512],
                                               start=(kc == 0), stop=(kc == 7))
                            pe.inc(ins, 1)
                            for kc in range(8):
                                ins = t.matmul(psF[2 * (ic % 2) + 1][:],
                                               wub[par3][:, kc * EI + ic * 128: kc * EI + (ic + 1) * 128],
                                               xgf[par][:, kc * 512:(kc + 1) * 512],
                                               start=(kc == 0), stop=(kc == 7))
                            pe.inc(ins, 1)
                        for sub in range(4):
                            if sub == 0:
                                t.wait_ge(ve.h, hh_s[3])
                            if sub >= 2:
                                t.wait_ge(ve.h, evv[sub - 2])
                            for mh in range(2):
                                for ic in range(4):
                                    ins = t.matmul(
                                        psF[2 * (sub % 2) + mh][:],
                                        hb[par][:, ic * 512 + sub * 128: ic * 512 + (sub + 1) * 128],
                                        wdb[par3][:, ic * D + mh * 512: ic * D + (mh + 1) * 512],
                                        start=(ic == 0), stop=(ic == 3))
                                pe.inc(ins, 1)
                        tEl.__exit__(None, None, None)
                    dpe = pe.n - pe0
                    with t.Else():
                        t.sem_inc(pe.h, dpe)
                assert pe.n == pe0 + 48

                @blk.vector
                def _(v):
                    v.reg_load(erV, av[0:1, s:s + 1])
                    v.reg_load(erV2, fst[0:1, s:s + 1])
                    with v.If_ne(erV, erVz):
                        v.wait_ge(pe.h, gu_pe[s - 2])
                        with v.If_ne(erV2, erVz):
                            for i in range(8):
                                v.wait_ge(pe.h, pe0 + i + 1)
                                v.tensor_copy(
                                    out=xgf[par][:, i * 512: i * 512 + 128],
                                    in_=pstr[i % 2][:]
                                ).then_inc(ve.h, 1 if i < 7 else 25)
                            for ic in range(4):
                                v.wait_ge(ac.h, silu_s[ic])
                                v.tensor_tensor(
                                    out=hb[par][:, ic * 512: ic * 512 + 128],
                                    in0=psF[2 * (ic % 2) + 1][:, 0:128],
                                    in1=gsl[par][:, ic * 512: ic * 512 + 128],
                                    op=ALU.mult).then_inc(ve.h, 1)
                            v.wait_ge(pe.h, pe0 + 48)
                            for mh in range(2):
                                v.tensor_scalar(
                                    ot[par][:, mh * 512: (mh + 1) * 512],
                                    psF[mh][:],
                                    coefc[par4][:, 0:1], None, ALU.mult
                                ).then_inc(ve.h, 1 if mh == 0 else 3)
                        vEl = v.Else()
                        vEl.__enter__()
                        for i in range(32):
                            sub, kc = divmod(i, 8)
                            v.wait_ge(pe.h, pe0 + i + 1)
                            ve.inc(v.tensor_copy(
                                out=xgf[par][:, kc * 512 + sub * 128: kc * 512 + (sub + 1) * 128],
                                in_=pstr[i % 2][:]), 1)
                        for ic in range(4):
                            v.wait_ge(ac.h, silu_s[ic])
                            ve.inc(v.tensor_tensor(
                                out=hb[par][:, ic * 512:(ic + 1) * 512],
                                in0=psF[2 * (ic % 2) + 1][:],
                                in1=gsl[par][:, ic * 512:(ic + 1) * 512],
                                op=ALU.mult), 1)
                        for sub in range(4):
                            v.wait_ge(pe.h, dd_s[sub])
                            for mh in range(2):
                                ins = v.tensor_scalar(
                                    ot[par][:, sub * D + mh * 512: sub * D + (mh + 1) * 512],
                                    psF[2 * (sub % 2) + mh][:],
                                    coefc[par4][:, sub:sub + 1], None, ALU.mult)
                            ve.inc(ins, 1)
                        vEl.__exit__(None, None, None)
                    dve = ve.n - ve0
                    with v.Else():
                        v.sem_inc(ve.h, dve)
                assert ve.n == ve0 + 40

                @blk.scalar
                def _(a):
                    a.reg_load(erC, av[0:1, s:s + 1])
                    a.reg_load(erC2, fst[0:1, s:s + 1])
                    with a.If_ne(erC, erCz):
                        with a.If_ne(erC2, erCz):
                            for ic in range(4):
                                a.wait_ge(pe.h, min(pe0 + 10 + 2 * ic, pe0 + 32))
                                a.activation(out=gsl[par][:, ic * 512: ic * 512 + 128],
                                             in_=psF[2 * (ic % 2)][:, 0:128],
                                             func=AF.Silu).then_inc(ac.h, 1)
                        with a.Else():
                            for ic in range(4):
                                a.wait_ge(pe.h, pe0 + 32 + 2 * (ic + 1))
                                ac.inc(a.activation(out=gsl[par][:, ic * 512:(ic + 1) * 512],
                                                    in_=psF[2 * (ic % 2)][:], func=AF.Silu), 1)
                    dac = ac.n - ac0
                    with a.Else():
                        a.sem_inc(ac.h, dac)
                assert ac.n == ac0 + 4

                if s == 0:
                    self.barrier()
                tr_pe[s] = pe0 + 32
                xgf_ac[s] = ve0 + 32
                gu_pe[s] = pe0 + 40
                h_ve[s] = hh_s[3]
                d_pe[s] = pe0 + 48
                ev_ve[s] = ve0 + 40

                @blk.gpsimd
                def _(g):
                    g.reg_load(erA, av[0:1, s:s + 1])
                    g.reg_load(erG2, fst[0:1, s:s + 1])
                    n0 = sg.n
                    with g.If_ne(erA, erGz):
                        g.wait_ge(ve.h, ev_ve[s])
                        with g.If_ne(erG2, erGz):
                            g.indirect_dma_start(
                                out=d["slots"][:, :],
                                out_offset=IOA(destu[par4][:, 0:1], 0),
                                in_=ot[par][:, 0:D],
                                in_offset=None).then_inc(sg.h, 16)
                            g.sem_inc(sg.h, 48)
                        with g.Else():
                            for sub in range(4):
                                sg.inc(g.indirect_dma_start(
                                    out=d["slots"][:, :],
                                    out_offset=IOA(destu[par4][:, sub:sub + 1], 0),
                                    in_=ot[par][:, sub * D:(sub + 1) * D],
                                    in_offset=None), 16)
                    dsg = sg.n - n0
                    with g.Else():
                        g.sem_inc(sg.h, dsg)
                scat_sg[s] = sg.n

            for it in range(NSLOT + 3):
                if it < NSLOT:
                    stage_cdrow(it)
                if 0 <= it - 1 < NSLOT:
                    stage_idx(it - 1)
                if 0 <= it - 3 < NSLOT:
                    stage_compute(it - 3)
                if 0 <= it - 1 < NSLOT:
                    stage_gather(it - 1)
                if 0 <= it - 1 < NSLOT:
                    stage_wload(it - 1)

            regs_ctx.close()
        self.barrier()

    def final(self):
        nc, sb, d = self.nc, self.sb, self.dram
        blk = self.block
        sx, sg, pe, ac, ve = (self.sem[k] for k in ["sx", "sg", "pe", "ac", "ve"])
        dbg_slots = self.dbg("slots", [SLROWS, D], BF16)
        if dbg_slots is not None:
            aa, bb = self.sem["sg"].n, self.sem["sx"].n
            @blk.sync
            def _(sync):
                sync.wait_ge(sg.h, aa)
                sync.wait_ge(sx.h, bb)
                for r0 in range(0, SLROWS - 128, 4096):
                    sx.inc(sync.dma_start(out=dbg_slots[r0:r0 + 4096, :],
                                          in_=d["slots"][r0:r0 + 4096, :]), 16)

        with ExitStack() as ctx:
            slb = [self.sbuf(ctx, "slb", [128, 9 * D], BF16) for _ in range(2)]
            y1r = [self.sbuf(ctx, "y1r", [128, D], F32) for _ in range(2)]
            acc = [self.sbuf(ctx, "acc", [128, 2 * D], F32) for _ in range(2)]
            ytile = [self.sbuf(ctx, "ytF", [128, D], F32) for _ in range(2)]
            sqs = [self.sbuf(ctx, "sqsF", [128, D], F32) for _ in range(2)]
            stat = self.sbuf(ctx, "statF", [128, 8], F32)

            ld_sx = {}
            sum_ve = {-2: 0, -1: 0}
            all_sg = sg.n
            all_sx = sx.n

            cq = self.sem["cq"]
            fgate = {}

            @blk.sync
            def _(sync):
                sync.wait_ge(sg.h, all_sg)
                sync.wait_ge(sx.h, all_sx)
                for i in range(2):
                    for j in range(9):
                        r0 = i * 1152 + j
                        sx.inc(sync.dma_start(
                            out=slb[i][:, j * D:(j + 1) * D],
                            in_=d["slots"][r0: r0 + 128 * 9: 9, :]), 16)
                    sx.inc(sync.dma_start(out=y1r[i][:],
                                          in_=d["y1f"][i * 128:(i + 1) * 128, :]), 16)
                    ld_sx[i] = sx.n
                    sync.wait_ge(sx.h, sx.n)
                    sync.sem_inc(cq.h, 1)
            for i in range(2):
                cq.n += 1
                fgate[i] = cq.n

            for i in range(NTILE):
                par = i % 2

                @blk.vector
                def _(v):
                    v.wait_ge(cq.h, fgate[i])
                    v.wait_ge(ve.h, sum_ve[i - 2])
                    ve.inc(v.tensor_tensor(out=acc[par][:, 0:D],
                                           in0=slb[par][:, 0:D], in1=slb[par][:, D:2 * D],
                                           op=ALU.add), 1)
                    ve.inc(v.tensor_tensor(out=acc[par][:, D:2 * D],
                                           in0=slb[par][:, 2 * D:3 * D],
                                           in1=slb[par][:, 3 * D:4 * D], op=ALU.add), 1)
                    self.vsync(v, ve)
                    ve.inc(v.tensor_tensor(out=acc[par][:, 0:D], in0=acc[par][:, 0:D],
                                           in1=acc[par][:, D:2 * D], op=ALU.add), 1)
                    ve.inc(v.tensor_tensor(out=acc[par][:, D:2 * D],
                                           in0=slb[par][:, 4 * D:5 * D],
                                           in1=slb[par][:, 5 * D:6 * D], op=ALU.add), 1)
                    self.vsync(v, ve)
                    ve.inc(v.tensor_tensor(out=acc[par][:, 0:D], in0=acc[par][:, 0:D],
                                           in1=acc[par][:, D:2 * D], op=ALU.add), 1)
                    ve.inc(v.tensor_tensor(out=acc[par][:, D:2 * D],
                                           in0=slb[par][:, 6 * D:7 * D],
                                           in1=slb[par][:, 7 * D:8 * D], op=ALU.add), 1)
                    self.vsync(v, ve)
                    ve.inc(v.tensor_tensor(out=acc[par][:, 0:D], in0=acc[par][:, 0:D],
                                           in1=acc[par][:, D:2 * D], op=ALU.add), 1)
                    self.vsync(v, ve)
                    ve.inc(v.tensor_tensor(out=acc[par][:, 0:D], in0=acc[par][:, 0:D],
                                           in1=slb[par][:, 8 * D:9 * D], op=ALU.add), 1)
                    self.vsync(v, ve)
                    ve.inc(v.tensor_tensor(out=ytile[par][:], in0=acc[par][:, 0:D],
                                           in1=y1r[par][:], op=ALU.add), 1)
                    self.vsync(v, ve)
                    ve.inc(v.tensor_reduce(stat[:, 0:1], ytile[par][:], AX.X, ALU.add), 1)
                    self.vsync(v, ve)
                    ve.inc(v.tensor_scalar(stat[:, 1:2], stat[:, 0:1], -1.0 / D, None, ALU.mult), 1)
                    self.vsync(v, ve)
                    ve.inc(v.tensor_scalar(ytile[par][:], ytile[par][:], stat[:, 1:2], None, ALU.add), 1)
                    self.vsync(v, ve)
                    ve.inc(v.tensor_tensor(out=sqs[par][:], in0=ytile[par][:],
                                           in1=ytile[par][:], op=ALU.mult), 1)
                    self.vsync(v, ve)
                    ve.inc(v.tensor_reduce(stat[:, 3:4], sqs[par][:], AX.X, ALU.add), 1)
                    self.vsync(v, ve)
                    ve.inc(v.tensor_scalar(stat[:, 6:7], stat[:, 3:4], 1.0 / D, 1e-5,
                                           ALU.mult, ALU.add), 1)
                sq_ve = ve.n

                @blk.scalar
                def _(a):
                    a.wait_ge(ve.h, sq_ve)
                    ac.inc(a.sqrt(out=stat[:, 4:5], in_=stat[:, 6:7]), 1)
                rstd_ac = ac.n

                @blk.vector
                def _(v):
                    v.wait_ge(ac.h, rstd_ac)
                    ve.inc(v.reciprocal(stat[:, 5:6], stat[:, 4:5]), 1)
                    self.vsync(v, ve)
                    ve.inc(v.tensor_scalar(ytile[par][:], ytile[par][:], stat[:, 5:6],
                                           None, ALU.mult), 1)
                sum_ve[i] = ve.n

                @blk.sync
                def _(sync):
                    sync.wait_ge(ve.h, sum_ve[i])
                    sx.inc(sync.dma_start(out=d["out"][i * 128:(i + 1) * 128, :],
                                          in_=ytile[par][:]), 16)
                    if i + 2 < NTILE:
                        for j in range(9):
                            r0 = (i + 2) * 1152 + j
                            sx.inc(sync.dma_start(
                                out=slb[par][:, j * D:(j + 1) * D],
                                in_=d["slots"][r0: r0 + 128 * 9: 9, :]), 16)
                        sx.inc(sync.dma_start(out=y1r[par][:],
                                              in_=d["y1f"][(i + 2) * 128:(i + 3) * 128, :]), 16)
                        ld_sx[i + 2] = sx.n
                        sync.wait_ge(sx.h, sx.n)
                        sync.sem_inc(cq.h, 1)
                if i + 2 < NTILE:
                    cq.n += 1
                    fgate[i + 2] = cq.n

            tgt = sx.n

            @blk.sync
            def _(sync):
                sync.wait_ge(sx.h, tgt)

    def finalize_stub(self):
        nc, d = self.nc, self.dram
        blk = self.block
        sx = self.sem["sx"]

        @blk.sync
        def _(sync):
            sync.wait_ge(sx.h, sx.n)
            for i in range(NTILE):
                sx.inc(sync.dma_start(out=d["out"][i * 128:(i + 1) * 128, :],
                                      in_=d["y1f"][i * 128:(i + 1) * 128, :]), 16)
        tgt = sx.n

        @blk.sync
        def _(sync):
            sync.wait_ge(sx.h, tgt)

    def finish(self):
        if getattr(self, "routing_ctx", None) is not None:
            try:
                self.routing_ctx.close()
            except Exception:
                pass
            self.routing_ctx = None
        self.ctx.close()
        return self.nc


def make_consts():
    identb = np.eye(128).astype(ml_dtypes.bfloat16)
    identf = np.eye(128).astype(np.float32)
    cc = np.arange(384)[None, :]
    pp = np.arange(128)[:, None]
    triu = (cc >= pp + 128).astype(ml_dtypes.bfloat16)
    cdpat = np.zeros((128, 1536), np.float32)
    cdpat[:, 1::3] = (DUMP0 + np.arange(128))[:, None]
    return {"identb": identb, "identf": identf, "triu": triu, "cdpat": cdpat}


def build_nc(debug=(), stages=("attn", "wo", "routing", "shared", "moe", "final")):
    b = B(debug=debug, no_scatter="noscat" in stages)
    b.load_consts()
    b.attention()
    b.wo_ln1()
    if "routing" in stages:
        b.routing()
    if "shared" in stages:
        b.shared_ffn()
    if "moe" in stages:
        b.moe()
    if "final" in stages:
        b.final()
    else:
        b.finalize_stub()
    b.finish()
    return b


def kernel(L0_attn_x, L0_attn_W_q, L0_attn_W_k, L0_attn_W_v, L0_attn_W_o,
           L0_ffn_router_w, L0_ffn_e_bias, L0_ffn_Wg, L0_ffn_Wu, L0_ffn_Wd,
           L0_ffn_SWg, L0_ffn_SWu, L0_ffn_SWd, _debug=(), _cores=None,
           _stages=("attn", "wo", "routing", "shared", "moe", "final")):
    x = np.ascontiguousarray(np.asarray(L0_attn_x, np.float32))
    consts = make_consts()
    common = {
        "wq": np.ascontiguousarray(L0_attn_W_q), "wk": np.ascontiguousarray(L0_attn_W_k),
        "wv": np.ascontiguousarray(L0_attn_W_v), "wo": np.ascontiguousarray(L0_attn_W_o),
        "rw": np.ascontiguousarray(L0_ffn_router_w),
        "ebias": np.ascontiguousarray(np.asarray(L0_ffn_e_bias, np.float32).reshape(1, E)),
        "wg": np.ascontiguousarray(np.asarray(L0_ffn_Wg).reshape(E, 8, 128, EI)
                                   .transpose(0, 2, 1, 3).reshape(E * 128, 8 * EI)),
        "wu": np.ascontiguousarray(np.asarray(L0_ffn_Wu).reshape(E, 8, 128, EI)
                                   .transpose(0, 2, 1, 3).reshape(E * 128, 8 * EI)),
        "wd": np.ascontiguousarray(np.asarray(L0_ffn_Wd).reshape(E, 4, 128, D)
                                   .transpose(0, 2, 1, 3).reshape(E * 128, 4 * D)),
        "swg": np.ascontiguousarray(L0_ffn_SWg), "swu": np.ascontiguousarray(L0_ffn_SWu),
        "swd": np.ascontiguousarray(L0_ffn_SWd),
        **consts,
    }
    cores = list(range(N_CORES)) if _cores is None else _cores
    n = len(cores)
    in_maps = []
    for i in range(n):
        m = dict(common)
        m["x"] = np.ascontiguousarray(x[2 * i:2 * i + 2].reshape(NTOK, D))
        in_maps.append(m)
    b = build_nc(debug=_debug, stages=_stages)
    res = run_bass_kernel_spmd(b.nc, in_maps, core_ids=cores)
    outs = [res.results[i]["out"].reshape(2, SEQ, D) for i in range(n)]
    full = np.concatenate(outs, axis=0).astype(np.float32)
    if _debug:
        return full, [res.results[i] for i in range(n)]
    return full
